# version 04
# baseline (speedup 1.0000x reference)
import numpy as np
import ml_dtypes
from contextlib import ExitStack

import concourse.bass as bass
import concourse.mybir as mybir
from concourse.bass_utils import run_bass_kernel_spmd

F32 = mybir.dt.float32
BF16 = mybir.dt.bfloat16
AF = mybir.ActivationFunctionType
ALU = mybir.AluOpType
AX = mybir.AxisListType
BF = ml_dtypes.bfloat16

NCORE = 8
TOK = 2048
NT = TOK // 128
D = 1024
NCH = D // 128
SEQ = 8192
NKT = SEQ // 128
H = 16
DH = 64


def _f(x):
    return np.float32(x)


class Prog:

    def __init__(self):
        self.nc = bass.Bass("TRN2", target_bir_lowering=False)
        self.es = ExitStack()
        self.sems = {}
        self.cnt = {}
        for s in ["ld", "wd", "pe", "ve", "ac", "gp", "st", "ev", "e2"]:
            self.sem(s)

    def dram_in(self, name, shape, dt=F32):
        return self.nc.dram_tensor(name, list(shape), dt, kind="ExternalInput").ap()

    def dram_out(self, name, shape, dt=F32):
        return self.nc.dram_tensor(name, list(shape), dt, kind="ExternalOutput").ap()

    def sb(self, name, shape, dt=F32):
        return self.es.enter_context(self.nc.sbuf_tensor(name, list(shape), dt))

    def sb_scoped(self, name, shape, dt=F32):
        cm = self.nc.sbuf_tensor(name, list(shape), dt)
        return cm.__enter__(), cm

    def ps(self, name, shape, dt=F32):
        return self.es.enter_context(self.nc.psum_tensor(name, list(shape), dt))

    def sem(self, name):
        s = self.es.enter_context(self.nc.semaphore(name))
        self.sems[name] = s
        self.cnt[name] = 0
        return s

    def inc(self, ins, name, k=1):
        ins.then_inc(self.sems[name], k)
        self.cnt[name] += k
        return self.cnt[name]

    def val(self, name):
        return self.cnt[name]



def _chunked(w):
    F = w.shape[1]
    return np.ascontiguousarray(w.reshape(NCH, 128, F).transpose(1, 0, 2))


def _colvec(b):
    return np.ascontiguousarray(np.asarray(b, np.float32).reshape(NCH, 128).T)


def _tokmajor(x):
    return np.ascontiguousarray(np.asarray(x, np.float32).reshape(NT, 128, D).transpose(1, 0, 2))


def _untokmajor(x):
    return np.ascontiguousarray(np.asarray(x).transpose(1, 0, 2).reshape(TOK, D))


def _ident_bf():
    return np.eye(128, dtype=np.float32).astype(BF)



def emit_ln(p, blk, x_sb, xn_sb, wait_sem, wait_val, done_sem, stats, scr,
            xf_sb=None, nt_list=None):
    if nt_list is None:
        nt_list = list(range(NT))

    @blk.vector
    def _(v):
        v.wait_ge(p.sems[wait_sem], wait_val)
        for nt in range(NT):
            p.inc(v.tensor_reduce(out=stats[:, nt:nt + 1], in_=x_sb[:, nt, :],
                                  axis=AX.X, op=ALU.add), "ve")
            ins = v.tensor_tensor(out=scr[:, :], in0=x_sb[:, nt, :],
                                  in1=x_sb[:, nt, :], op=ALU.mult)
            v.wait_ge(p.sems["ve"], p.inc(ins, "ve"))
            p.inc(v.tensor_reduce(out=stats[:, 16 + nt:17 + nt], in_=scr[:, :],
                                  axis=AX.X, op=ALU.add), "ve")
        ins = v.tensor_scalar(out=stats[:, 32:48], in0=stats[:, 0:16],
                              scalar1=1.0 / 1024.0, scalar2=0.0,
                              op0=ALU.mult, op1=ALU.add)
        v.wait_ge(p.sems["ve"], p.inc(ins, "ve"))
        ins = v.tensor_tensor(out=stats[:, 48:64], in0=stats[:, 32:48],
                              in1=stats[:, 32:48], op=ALU.mult)
        v.wait_ge(p.sems["ve"], p.inc(ins, "ve"))
        ins = v.tensor_scalar(out=stats[:, 64:80], in0=stats[:, 16:32],
                              scalar1=1.0 / 1024.0, scalar2=0.0,
                              op0=ALU.mult, op1=ALU.add)
        v.wait_ge(p.sems["ve"], p.inc(ins, "ve"))
        ins = v.tensor_tensor(out=stats[:, 80:96], in0=stats[:, 64:80],
                              in1=stats[:, 48:64], op=ALU.subtract)
        p.inc(ins, "ve")

    @blk.scalar
    def _(s):
        s.wait_ge(p.sems["ve"], p.val("ve"))
        ins = s.activation(out=stats[:, 96:112], in_=stats[:, 80:96],
                           func=AF.Sqrt, bias=_f(1e-5), scale=_f(1.0))
        p.inc(ins, "ac")

    @blk.vector
    def _(v):
        v.wait_ge(p.sems["ac"], p.val("ac"))
        ins = v.reciprocal(out=stats[:, 48:64], in_=stats[:, 96:112])
        v.wait_ge(p.sems["ve"], p.inc(ins, "ve"))
        for nt in nt_list:
            if xn_sb is not None:
                p.inc(v.tensor_scalar(out=xn_sb[:, nt, :], in0=x_sb[:, nt, :],
                                      scalar1=stats[:, 32 + nt:33 + nt],
                                      scalar2=stats[:, 48 + nt:49 + nt],
                                      op0=ALU.subtract, op1=ALU.mult), done_sem)
            if xf_sb is not None:
                p.inc(v.tensor_scalar(out=xf_sb[:, nt, :], in0=x_sb[:, nt, :],
                                      scalar1=stats[:, 32 + nt:33 + nt],
                                      scalar2=stats[:, 48 + nt:49 + nt],
                                      op0=ALU.subtract, op1=ALU.mult), done_sem)


def emit_tr(p, blk, groups, src_of, dst_of, pbank, id_sb, wait_sem, wait_val,
            done_sem):
    done_base = p.val(done_sem)

    @blk.tensor
    def _(t):
        t.wait_ge(p.sems[wait_sem], wait_val)
        for gi, key in enumerate(groups):
            bank = pbank[gi % 2]
            if gi >= 2:
                t.wait_ge(p.sems[done_sem], done_base + gi - 1)
            last = None
            for j, src in enumerate(src_of(key)):
                last = t.transpose(bank[:, j * 128:(j + 1) * 128], src, id_sb[:, :])
            p.inc(last, "pe")

    pe_base = p.val("pe") - len(groups)

    @blk.vector
    def _(v):
        for gi, key in enumerate(groups):
            bank = pbank[gi % 2]
            v.wait_ge(p.sems["pe"], pe_base + gi + 1)
            out_ap, in_ap = dst_of(key, bank)
            ins = v.tensor_copy(out=out_ap, in_=in_ap)
            p.inc(ins, done_sem)


def emit_tm2fm(p, blk, src_tm, dst_fm, pbank, id_sb, wait_sem, wait_val, done_sem):
    groups = [(nt, cj) for nt in range(NT) for cj in (0, 4)]

    def src_of(key):
        nt, cj = key
        return [src_tm[:, nt, (cj + j) * 128:(cj + j + 1) * 128] for j in range(4)]

    def dst_of(key, bank):
        nt, cj = key
        return (dst_fm[:, cj:cj + 4, nt * 128:(nt + 1) * 128], bank[:, :])

    emit_tr(p, blk, groups, src_of, dst_of, pbank, id_sb, wait_sem, wait_val, done_sem)


def emit_fm2tm(p, blk, src_fm, dst_tm, pbank, id_sb, wait_sem, wait_val, done_sem):
    groups = [(nt, cj) for nt in range(NT) for cj in (0, 4)]

    def src_of(key):
        nt, cj = key
        return [src_fm[:, cj + j, nt * 128:(nt + 1) * 128] for j in range(4)]

    def dst_of(key, bank):
        nt, cj = key
        return (dst_tm[:, nt, cj * 128:(cj + 4) * 128], bank[:, :])

    emit_tr(p, blk, groups, src_of, dst_of, pbank, id_sb, wait_sem, wait_val, done_sem)



def build_L1():
    p = Prog()
    nc = p.nc
    xs = p.dram_in("xs", [128, NT, D])
    wqT = p.dram_in("wqT", [128, NCH, D])
    wkT = p.dram_in("wkT", [2, 128, NCH, D])
    wvT = p.dram_in("wvT", [2, 128, NCH, D])
    bq_i = p.dram_in("bq", [128, NCH])
    bkv_i = p.dram_in("bkv", [1, 4, D])
    sel_i = p.dram_in("sel", [128, 2])
    sel2_i = p.dram_in("sel2", [2, 128])
    idm = p.dram_in("idm", [128, 128], BF16)
    qn_o = p.dram_out("qn", [128, NCH, TOK], BF16)
    st_o = p.dram_out("pstates", [128, 2, 512])
    xn_o = p.dram_out("xn_dbg", [128, NT, D], BF16)
    xnT_o = p.dram_out("xnT_dbg", [128, NCH, TOK], BF16)

    xn_sb = p.sb("xn", [128, NT, D], BF16)
    xnT = p.sb("xnT", [128, NCH, TOK], BF16)
    qst = p.sb("qst", [128, 2, 512], BF16)
    stats = p.sb("stats", [128, 128])
    scr = p.sb("scr", [128, D])
    id_sb = p.sb("id", [128, 128], BF16)
    wq_sb = p.sb("wq", [128, NCH, D], BF16)
    bq_sb = p.sb("bqs", [128, NCH])
    bkv_sb = p.sb("bkvs", [1, 4, D], BF16)
    t1 = p.sb("t1", [128, 512])
    rn2 = p.sb("rn2", [2, 512])
    kv0 = p.sb("kv0", [128, D])
    kv1 = p.sb("kv1", [128, D])
    kvb = p.sb("kvb", [128, D], BF16)
    kvv = p.sb("kvv", [128, D], BF16)
    nrm = p.sb("nrm", [128, 16])
    sts = p.sb("sts", [128, 2, 512])
    ones_b = p.sb("ones_b", [128, 8], BF16)
    ones_r = p.sb("ones_r", [1, 128], BF16)
    sel_f = p.sb("sel_f", [128, 2])
    sel2_f = p.sb("sel2_f", [2, 128])
    t0 = scr[:, 0:512]
    t2 = scr[:, 512:1024]
    x_sb, x_cm = p.sb_scoped("x", [128, NT, D])

    P0 = p.ps("P0", [128, D])
    P1 = p.ps("P1", [128, D])
    P2 = p.ps("P2", [128, 512])
    P3 = p.ps("P3", [128, 512])
    pt_ctx = [nc.psum_tensor("PTa", [128, 512], BF16), nc.psum_tensor("PTb", [128, 512], BF16)]
    PT = [c.__enter__() for c in pt_ctx]

    with nc.Block() as blk:
        @blk.sync
        def _(sync):
            sync.dma_start(out=x_sb[:, :, :], in_=xs[:, :, :]).then_inc(p.sems["ld"], 16)
            p.cnt["ld"] += 16
            ins = sync.dma_start(out=id_sb[:, :], in_=idm[:, :])
            p.inc(ins, "ld", 16)

        @blk.gpsimd
        def _(g):
            g.dma_start(out=wq_sb[:, :, :], in_=wqT[:, :, :]).then_inc(p.sems["wd"], 16)
            g.dma_start(out=bq_sb[:, :], in_=bq_i[:, :]).then_inc(p.sems["wd"], 16)
            g.dma_start(out=bkv_sb[:, :, :], in_=bkv_i[:, :, :]).then_inc(p.sems["wd"], 16)
            p.cnt["wd"] += 48
            g.dma_start(out=sel_f[:, :], in_=sel_i[:, :]).then_inc(p.sems["wd"], 16)
            g.dma_start(out=sel2_f[:, :], in_=sel2_i[:, :]).then_inc(p.sems["wd"], 16)
            p.cnt["wd"] += 32
            g.memset(ones_b[:, :], 1.0)
            ins = g.memset(ones_r[:, :], 1.0)
            p.inc(ins, "wd", 1)

        emit_ln(p, blk, x_sb, xn_sb, "ld", 32, "ev", stats, scr)
        emit_tm2fm(p, blk, xn_sb, xnT, PT, id_sb, "ev", p.val("ev"), "e2")
        tr_done = p.val("e2")
        for c in reversed(pt_ctx):
            c.__exit__(None, None, None)
        x_cm.__exit__(None, None, None)
        w2_sb = p.sb("w2", [128, 2, NCH, D], BF16)
        P4 = p.ps("P4", [128, 512])
        P5 = p.ps("P5", [128, 512])

        @blk.sync
        def _(sync):
            sync.wait_ge(p.sems["e2"], tr_done)
            sync.dma_start(out=xn_o[:, :, :], in_=xn_sb[:, :, :]).then_inc(p.sems["st"], 16)
            ins = sync.dma_start(out=xnT_o[:, :, :], in_=xnT[:, :, :])
            p.inc(ins, "st", 16)
            p.cnt["st"] += 16

        ev_base = p.val("ev")
        its = [(fj, qc) for fj in range(NCH) for qc in range(4)]
        for it, (fj, qc) in enumerate(its):
            PA = [P2, P3][it % 2]

            @blk.tensor
            def _(t, fj=fj, qc=qc, PA=PA, it=it):
                t.wait_ge(p.sems["wd"], 81)
                t.wait_ge(p.sems["e2"], tr_done)
                if it >= 1:
                    t.wait_ge(p.sems["ev"], ev_base + it)
                last = None
                for ci in range(NCH):
                    last = t.matmul(PA[:, :], wq_sb[:, ci, fj * 128:(fj + 1) * 128],
                                    xnT[:, ci, qc * 512:(qc + 1) * 512],
                                    start=(ci == 0), stop=(ci == NCH - 1))
                p.inc(last, "pe")

            pe1 = p.val("pe")

            @blk.vector
            def _(v, fj=fj, qc=qc, PA=PA, pe1=pe1):
                v.wait_ge(p.sems["pe"], pe1)
                ins = v.scalar_tensor_tensor(out=t0[:, :], in0=PA[:, :],
                                             scalar=bq_sb[:, fj:fj + 1],
                                             in1=xnT[:, fj, qc * 512:(qc + 1) * 512],
                                             op0=ALU.add, op1=ALU.add)
                p.inc(ins, "ve")

            ve1 = p.val("ve")

            @blk.scalar
            def _(s, ve1=ve1):
                s.wait_ge(p.sems["ve"], ve1)
                p.inc(s.activation(out=t1[:, :], in_=t0[:, :], func=AF.Silu), "ac")
                ins = s.activation(out=t2[:, :], in_=t1[:, :], func=AF.Square,
                                   bias=_f(0.5), scale=_f(1.0))
                p.inc(ins, "ac")

            ac1 = p.val("ac")

            @blk.tensor
            def _(t, ac1=ac1):
                t.wait_ge(p.sems["ac"], ac1)
                ins = t.matmul(P4[0:2, :], sel_f[:, :], t2[:, :], start=True, stop=True)
                p.inc(ins, "pe")

            pe2 = p.val("pe")

            @blk.scalar
            def _(s, pe2=pe2):
                s.wait_ge(p.sems["pe"], pe2)
                ins = s.activation(out=rn2[:, :], in_=P4[0:2, :], func=AF.Sqrt)
                p.inc(ins, "ac")

            ac2 = p.val("ac")

            @blk.vector
            def _(v, ac2=ac2):
                v.wait_ge(p.sems["ac"], ac2)
                ins = v.tensor_scalar(out=rn2[:, :], in0=rn2[:, :], scalar1=1e-6,
                                      scalar2=0.0, op0=ALU.add, op1=ALU.add)
                v.wait_ge(p.sems["ve"], p.inc(ins, "ve"))
                ins = v.reciprocal(out=rn2[:, :], in_=rn2[:, :])
                p.inc(ins, "ve")

            ve2 = p.val("ve")

            @blk.tensor
            def _(t, ve2=ve2):
                t.wait_ge(p.sems["ve"], ve2)
                ins = t.matmul(P5[:, :], sel2_f[:, :], rn2[:, :], start=True, stop=True)
                p.inc(ins, "pe")

            pe3 = p.val("pe")

            @blk.vector
            def _(v, fj=fj, qc=qc, pe3=pe3, it=it):
                v.wait_ge(p.sems["pe"], pe3)
                if it >= 2:
                    v.wait_ge(p.sems["st"], 16 * (it - 1))
                ins = v.scalar_tensor_tensor(out=qst[:, it % 2, :],
                                             in0=t1[:, :], scalar=0.5, in1=P5[:, :],
                                             op0=ALU.add, op1=ALU.mult)
                p.inc(ins, "ev")

            ev_it = p.val("ev")

            @blk.sync
            def _(sync, fj=fj, qc=qc, it=it, ev_it=ev_it):
                sync.wait_ge(p.sems["ev"], ev_it)
                ins = sync.dma_start(out=qn_o[:, fj, qc * 512:(qc + 1) * 512],
                                     in_=qst[:, it % 2, :])
                p.inc(ins, "st", 16)

        q_done = p.val("ev")

        @blk.gpsimd
        def _(g):
            g.wait_ge(p.sems["ev"], q_done)
            g.dma_start(out=w2_sb[:, 0, :, :], in_=wkT[0, :, :, :]).then_inc(p.sems["wd"], 16)
            ins = g.dma_start(out=w2_sb[:, 1, :, :], in_=wvT[0, :, :, :])
            p.inc(ins, "wd", 16)
            p.cnt["wd"] += 16

        PS_A = [P2, P3]
        PS_B = [P4, P5]
        PP = [P0, P1]
        wd2 = p.val("wd")
        ve_p2_base = p.val("ve")
        for k in range(2):
            if k == 1:
                pe_after_k0 = p.val("pe")

                @blk.gpsimd
                def _(g, pe_after_k0=pe_after_k0):
                    g.wait_ge(p.sems["pe"], pe_after_k0)
                    g.dma_start(out=w2_sb[:, 0, :, :], in_=wkT[1, :, :, :]).then_inc(p.sems["wd"], 16)
                    ins = g.dma_start(out=w2_sb[:, 1, :, :], in_=wvT[1, :, :, :])
                    p.inc(ins, "wd", 16)
                    p.cnt["wd"] += 16
            wd_need = wd2 + 32 * k
            for nt in range(NT):
                PA = PP[nt % 2]
                idx = k * NT + nt

                @blk.tensor
                def _(t, k=k, nt=nt, PA=PA, wd_need=wd_need, idx=idx):
                    t.wait_ge(p.sems["wd"], wd_need)
                    if idx >= 1:
                        t.wait_ge(p.sems["ve"], ve_p2_base + 4 * idx)
                    last = None
                    for ci in range(NCH):
                        last = t.matmul(PA[:, 0:512], xnT[:, ci, nt * 128:(nt + 1) * 128],
                                        w2_sb[:, 0, ci, 0:512], start=(ci == 0), stop=False)
                        last = t.matmul(PA[:, 512:1024], xnT[:, ci, nt * 128:(nt + 1) * 128],
                                        w2_sb[:, 0, ci, 512:1024], start=(ci == 0), stop=False)
                    t.matmul(PA[:, 0:512], ones_r[:, :], bkv_sb[:, k, 0:512],
                             start=False, stop=True)
                    last = t.matmul(PA[:, 512:1024], ones_r[:, :], bkv_sb[:, k, 512:1024],
                                    start=False, stop=True)
                    p.inc(last, "pe")

                pe_ks = p.val("pe")

                @blk.vector
                def _(v, nt=nt, PA=PA, pe_ks=pe_ks):
                    v.wait_ge(p.sems["pe"], pe_ks)
                    ins = v.tensor_tensor(out=kv0[:, :], in0=PA[:, :],
                                          in1=xn_sb[:, nt, :], op=ALU.add)
                    p.inc(ins, "ve")

                ve_a = p.val("ve")

                @blk.scalar
                def _(s, ve_a=ve_a):
                    s.wait_ge(p.sems["ve"], ve_a)
                    p.inc(s.activation(out=kv1[:, :], in_=kv0[:, :], func=AF.Silu), "ac")
                    ins = s.activation(out=kv0[:, :], in_=kv1[:, :], func=AF.Square,
                                       bias=_f(0.5), scale=_f(1.0))
                    p.inc(ins, "ac")

                ac_a = p.val("ac")

                @blk.vector
                def _(v, ac_a=ac_a):
                    v.wait_ge(p.sems["ac"], ac_a)
                    ins = v.tensor_reduce(out=nrm[:, 0:16],
                                          in_=kv0[:, :].rearrange("p (h d) -> p h d", h=16),
                                          axis=AX.X, op=ALU.add)
                    p.inc(ins, "ve")

                ve_b = p.val("ve")

                @blk.scalar
                def _(s, ve_b=ve_b):
                    s.wait_ge(p.sems["ve"], ve_b)
                    ins = s.activation(out=nrm[:, 0:16], in_=nrm[:, 0:16], func=AF.Sqrt)
                    p.inc(ins, "ac")

                ac_b = p.val("ac")

                @blk.vector
                def _(v, ac_b=ac_b):
                    v.wait_ge(p.sems["ac"], ac_b)
                    ins = v.tensor_scalar(out=nrm[:, 0:16], in0=nrm[:, 0:16], scalar1=1e-6,
                                          scalar2=0.0, op0=ALU.add, op1=ALU.add)
                    v.wait_ge(p.sems["ve"], p.inc(ins, "ve"))
                    ins = v.reciprocal(out=nrm[:, 0:16], in_=nrm[:, 0:16])
                    v.wait_ge(p.sems["ve"], p.inc(ins, "ve"))
                    for h in range(16):
                        p.inc(v.tensor_scalar(out=kvb[:, h * 64:(h + 1) * 64],
                                              in0=kv1[:, h * 64:(h + 1) * 64],
                                              scalar1=0.5, scalar2=nrm[:, h:h + 1],
                                              op0=ALU.add, op1=ALU.mult), "ve")

                ve_c = p.val("ve")

                @blk.tensor
                def _(t, k=k, nt=nt, PA=PA, ve_c=ve_c):
                    t.wait_ge(p.sems["ve"], ve_c)
                    last = None
                    for ci in range(NCH):
                        last = t.matmul(PA[:, 0:512], xnT[:, ci, nt * 128:(nt + 1) * 128],
                                        w2_sb[:, 1, ci, 0:512], start=(ci == 0), stop=False)
                        last = t.matmul(PA[:, 512:1024], xnT[:, ci, nt * 128:(nt + 1) * 128],
                                        w2_sb[:, 1, ci, 512:1024], start=(ci == 0), stop=False)
                    t.matmul(PA[:, 0:512], ones_r[:, :], bkv_sb[:, 2 + k, 0:512],
                             start=False, stop=True)
                    last = t.matmul(PA[:, 512:1024], ones_r[:, :], bkv_sb[:, 2 + k, 512:1024],
                                    start=False, stop=True)
                    p.inc(last, "pe")

                pe_vs = p.val("pe")

                @blk.vector
                def _(v, nt=nt, PA=PA, pe_vs=pe_vs):
                    v.wait_ge(p.sems["pe"], pe_vs)
                    ins = v.tensor_tensor(out=kvv[:, :], in0=PA[:, :],
                                          in1=xn_sb[:, nt, :], op=ALU.add)
                    p.inc(ins, "ve")

                ve_d = p.val("ve")

                @blk.tensor
                def _(t, k=k, nt=nt, ve_d=ve_d):
                    t.wait_ge(p.sems["ve"], ve_d)
                    last = None
                    for hp in range(8):
                        t.matmul(PS_A[k][0:64, hp * 64:(hp + 1) * 64],
                                 kvb[:, (2 * hp) * 64:(2 * hp + 1) * 64],
                                 kvv[:, (2 * hp) * 64:(2 * hp + 1) * 64],
                                 start=(nt == 0 and hp == 0), stop=(nt == NT - 1),
                                 tile_position=(0, 0))
                        last = t.matmul(PS_B[k][64:128, hp * 64:(hp + 1) * 64],
                                        kvb[:, (2 * hp + 1) * 64:(2 * hp + 2) * 64],
                                        kvv[:, (2 * hp + 1) * 64:(2 * hp + 2) * 64],
                                        start=(nt == 0 and hp == 0), stop=(nt == NT - 1),
                                        tile_position=(0, 64))
                    p.inc(last, "pe")

        pe_states = p.val("pe")

        @blk.vector
        def _(v):
            v.wait_ge(p.sems["pe"], pe_states)
            p.inc(v.tensor_copy(out=sts[0:64, 0, :], in_=P2[0:64, :]), "ev")
            p.inc(v.tensor_copy(out=sts[64:128, 0, :], in_=P4[64:128, :]), "ev")
            p.inc(v.tensor_copy(out=sts[0:64, 1, :], in_=P3[0:64, :]), "ev")
            p.inc(v.tensor_copy(out=sts[64:128, 1, :], in_=P5[64:128, :]), "ev")

        @blk.sync
        def _(sync):
            sync.wait_ge(p.sems["ev"], p.val("ev"))
            ins = sync.dma_start(out=st_o[:, :, :], in_=sts[:, :, :])
            p.inc(ins, "st", 16)
            sync.wait_ge(p.sems["st"], p.val("st"))
    return p


def prep_L1(I, core):
    b, t4 = core // 4, core % 4
    xs = np.asarray(I["L0_attn_a_x"][b, t4 * TOK:(t4 + 1) * TOK, :], np.float32)
    return dict(
        xs=_tokmajor(xs),
        wqT=_chunked(np.ascontiguousarray(np.asarray(I["L0_attn_a_Wq"], np.float32).T)),
        wkT=np.stack([_chunked(np.ascontiguousarray(np.asarray(I["L0_attn_a_Wk"][k], np.float32).T))
                      for k in range(2)]),
        wvT=np.stack([_chunked(np.ascontiguousarray(np.asarray(I["L0_attn_a_Wv"][k], np.float32).T))
                      for k in range(2)]),
        bq=_colvec(I["L0_attn_a_bq"]),
        bkv=np.stack([np.asarray(I["L0_attn_a_bk"][0], np.float32),
                      np.asarray(I["L0_attn_a_bk"][1], np.float32),
                      np.asarray(I["L0_attn_a_bv"][0], np.float32),
                      np.asarray(I["L0_attn_a_bv"][1], np.float32)])[None, :, :],
        idm=_ident_bf(),
        sel=np.kron(np.eye(2, dtype=np.float32), np.ones((64, 1), np.float32)),
        sel2=np.kron(np.eye(2, dtype=np.float32), np.ones((1, 64), np.float32)),
    )



def build_L2():
    p = Prog()
    nc = p.nc
    xs = p.dram_in("xs", [128, NT, D])
    qn_i = p.dram_in("qn", [128, NCH, TOK], BF16)
    pst = p.dram_in("pst", [128, 4, 2, 512])
    woT = p.dram_in("woT", [128, NCH, D])
    lng = p.dram_in("lng", [128, NCH])
    lnb = p.dram_in("lnb", [128, NCH])
    bo_row = p.dram_in("bo_row", [1, D])
    idm = p.dram_in("idm", [128, 128], BF16)
    x1_o = p.dram_out("x1", [128, NT, D])
    psum_o = p.dram_out("psums", [1, D])

    x_sb = p.sb("x", [128, NT, D])
    stf = p.sb("stf", [128, 2, 512])
    stb = p.sb("stb", [128, 512], BF16)
    wo_sb = p.sb("wo", [128, NCH, D], BF16)
    lng_sb = p.sb("lngs", [128, NCH])
    lnb_sb = p.sb("lnbs", [128, NCH], BF16)
    bo_sb = p.sb("bos", [1, D])
    id_sb = p.sb("id", [128, 128], BF16)
    stats = p.sb("stats", [128, 128])
    scr = p.sb("scr", [128, D])
    brow = p.sb("brow", [1, D])
    brow_b = p.sb("browb", [1, D], BF16)
    ones_b = p.sb("ones_b", [128, 8], BF16)
    ones_r = p.sb("ones_r", [1, 128], BF16)
    psums_sb = p.sb("psums_s", [1, D])
    pst_sb = p.sb("pst_s", [128, 1, 2, 512])
    bufA = p.sb("bufA", [128, NCH, TOK], BF16)
    bufB = p.sb("bufB", [128, NCH, TOK], BF16)
    bufC = p.sb("bufC", [128, NCH, TOK], BF16)
    qstage = p.sb("qstage", [128, 2, 512], BF16)
    out_fm = bufA

    def _tiles(buf):
        return buf[:, :, :].rearrange("p c t -> p (c t)").rearrange("p (n d) -> p n d", n=NT)

    P0 = p.ps("P0", [128, D])
    P1 = p.ps("P1", [128, D])
    P2 = p.ps("P2", [128, 512])
    PT = [p.ps("PTa", [128, 512], BF16), p.ps("PTb", [128, 512], BF16)]

    with nc.Block() as blk:
        @blk.sync
        def _(sync):
            sync.dma_start(out=x_sb[:, :, :], in_=xs[:, :, :]).then_inc(p.sems["ld"], 16)
            sync.dma_start(out=pst_sb[:, 0, :, :], in_=pst[:, 0, :, :]).then_inc(p.sems["ld"], 16)
            p.cnt["ld"] += 32
            ins = sync.dma_start(out=id_sb[:, :], in_=idm[:, :])
            p.inc(ins, "ld", 16)

        @blk.gpsimd
        def _(g):
            g.dma_start(out=wo_sb[:, :, :], in_=woT[:, :, :]).then_inc(p.sems["wd"], 16)
            g.dma_start(out=lng_sb[:, :], in_=lng[:, :]).then_inc(p.sems["wd"], 16)
            g.dma_start(out=lnb_sb[:, :], in_=lnb[:, :]).then_inc(p.sems["wd"], 16)
            g.dma_start(out=bo_sb[:, :], in_=bo_row[:, :]).then_inc(p.sems["wd"], 16)
            p.cnt["wd"] += 64
            g.memset(ones_b[:, :], 1.0)
            ins = g.memset(ones_r[:, :], 1.0)
            p.inc(ins, "wd", 1)

        @blk.tensor
        def _(t):
            t.wait_ge(p.sems["wd"], 65)
            last = None
            for half in range(2):
                for ci in range(NCH):
                    last = t.matmul(P0[0:1, half * 512:(half + 1) * 512],
                                    lnb_sb[:, ci:ci + 1],
                                    wo_sb[:, ci, half * 512:(half + 1) * 512],
                                    start=(ci == 0), stop=(ci == NCH - 1))
            p.inc(last, "pe")

        pe_brow = p.val("pe")

        @blk.vector
        def _(v):
            v.wait_ge(p.sems["ld"], 48)
            v.wait_ge(p.sems["pe"], pe_brow)
            ins = v.tensor_copy(out=brow[:, :], in_=P0[0:1, 0:1024])
            p.inc(ins, "ve")

        @blk.vector
        def _(v):
            p.inc(v.tensor_copy(out=stf[:, 0, :], in_=pst_sb[:, 0, 0, :]), "ve")
            p.inc(v.tensor_copy(out=stf[:, 1, :], in_=pst_sb[:, 0, 1, :]), "ve")

        for c4 in range(1, 4):
            @blk.sync
            def _(sync, c4=c4):
                sync.wait_ge(p.sems["ve"], p.val("ve"))
                ins = sync.dma_start(out=pst_sb[:, 0, :, :], in_=pst[:, c4, :, :])
                p.inc(ins, "ld", 16)

            ld_c = p.val("ld")

            @blk.vector
            def _(v, ld_c=ld_c):
                v.wait_ge(p.sems["ld"], ld_c)
                p.inc(v.tensor_tensor(out=stf[:, 0, :], in0=stf[:, 0, :],
                                      in1=pst_sb[:, 0, 0, :], op=ALU.add), "ve")
                p.inc(v.tensor_tensor(out=stf[:, 1, :], in0=stf[:, 1, :],
                                      in1=pst_sb[:, 0, 1, :], op=ALU.add), "ve")

        @blk.vector
        def _(v):
            v.wait_ge(p.sems["ve"], p.val("ve"))
            ins = v.scalar_tensor_tensor(out=stb[:, :], in0=stf[:, 0, :],
                                         scalar=float(SEQ) ** -2.0, in1=stf[:, 1, :],
                                         op0=ALU.mult, op1=ALU.mult)
            v.wait_ge(p.sems["ve"], p.inc(ins, "ve"))
            ins = v.tensor_tensor(out=brow[:, :], in0=brow[:, :], in1=bo_sb[:, :], op=ALU.add)
            v.wait_ge(p.sems["ve"], p.inc(ins, "ve"))
            p.inc(v.tensor_copy(out=brow_b[:, :], in_=brow[:, :]), "ve")
            for ci in range(NCH):
                p.inc(v.tensor_scalar(out=wo_sb[:, ci, :], in0=wo_sb[:, ci, :],
                                      scalar1=lng_sb[:, ci:ci + 1], scalar2=0.0,
                                      op0=ALU.mult, op1=ALU.add), "ve")

        ve_prep = p.val("ve")

        ev_base = p.val("ev")
        pe_qs_base = p.val("pe")
        its = [(hp, qc) for hp in range(NCH) for qc in range(4)]
        for it, (hp, qc) in enumerate(its):
            PA = [P0, P1][it % 2]

            @blk.sync
            def _(sync, hp=hp, qc=qc, it=it):
                if it >= 2:
                    sync.wait_ge(p.sems["pe"], pe_qs_base + it - 1)
                ins = sync.dma_start(out=qstage[:, it % 2, :],
                                     in_=qn_i[:, hp, qc * 512:(qc + 1) * 512])
                p.inc(ins, "ld", 16)

            ld_it = p.val("ld")

            @blk.tensor
            def _(t, hp=hp, qc=qc, PA=PA, it=it, ld_it=ld_it):
                t.wait_ge(p.sems["ve"], ve_prep)
                t.wait_ge(p.sems["ld"], ld_it)
                if it >= 1:
                    t.wait_ge(p.sems["ev"], ev_base + it)
                t.matmul(PA[0:64, 0:512], stb[0:64, hp * 64:(hp + 1) * 64],
                         qstage[0:64, it % 2, :],
                         start=True, stop=True, tile_position=(0, 0))
                ins = t.matmul(PA[64:128, 0:512], stb[64:128, hp * 64:(hp + 1) * 64],
                               qstage[64:128, it % 2, :],
                               start=True, stop=True, tile_position=(64, 64))
                p.inc(ins, "pe")

            pe_o = p.val("pe")

            @blk.vector
            def _(v, hp=hp, qc=qc, PA=PA, pe_o=pe_o):
                v.wait_ge(p.sems["pe"], pe_o)
                ins = v.tensor_copy(out=out_fm[:, hp, qc * 512:(qc + 1) * 512],
                                    in_=PA[0:128, 0:512])
                p.inc(ins, "ev")

        out_tm = _tiles(bufB)
        emit_fm2tm(p, blk, out_fm, out_tm, PT, id_sb, "ev", p.val("ev"), "e2")
        oln = _tiles(bufC)
        emit_ln(p, blk, out_tm, oln, "e2", p.val("e2"), "ev", stats, scr)
        olnT = bufA
        emit_tm2fm(p, blk, oln, olnT, PT, id_sb, "ev", p.val("ev"), "e2")
        tr2 = p.val("e2")

        ev2_base = p.val("ev")
        for nt in range(NT):
            PA = [P0, P1][nt % 2]

            @blk.tensor
            def _(t, nt=nt, PA=PA):
                t.wait_ge(p.sems["e2"], tr2)
                if nt >= 1:
                    t.wait_ge(p.sems["ev"], ev2_base + nt)
                last = None
                for ci in range(NCH):
                    last = t.matmul(PA[:, 0:512], olnT[:, ci, nt * 128:(nt + 1) * 128],
                                    wo_sb[:, ci, 0:512], start=(ci == 0), stop=False)
                    last = t.matmul(PA[:, 512:1024], olnT[:, ci, nt * 128:(nt + 1) * 128],
                                    wo_sb[:, ci, 512:1024], start=(ci == 0), stop=False)
                t.matmul(PA[:, 0:512], ones_r[:, :], brow_b[:, 0:512],
                         start=False, stop=True)
                last = t.matmul(PA[:, 512:1024], ones_r[:, :], brow_b[:, 512:1024],
                                start=False, stop=True)
                p.inc(last, "pe")

            pe_p = p.val("pe")

            @blk.vector
            def _(v, nt=nt, PA=PA, pe_p=pe_p):
                v.wait_ge(p.sems["pe"], pe_p)
                ins = v.tensor_tensor(out=x_sb[:, nt, :], in0=PA[:, :],
                                      in1=x_sb[:, nt, :], op=ALU.add)
                p.inc(ins, "ev")

        x1_done = p.val("ev")
        xn1 = _tiles(bufB)

        @blk.sync
        def _(sync):
            sync.wait_ge(p.sems["ev"], x1_done)
            ins = sync.dma_start(out=x1_o[:, :, :], in_=x_sb[:, :, :])
            p.inc(ins, "st", 16)

        emit_ln(p, blk, x_sb, xn1, "ev", x1_done, "ev", stats, scr)
        ln2 = p.val("ev")

        @blk.tensor
        def _(t):
            t.wait_ge(p.sems["ev"], ln2)
            last = None
            for half in range(2):
                for nt in range(NT):
                    last = t.matmul(P0[0:1, half * 512:(half + 1) * 512], ones_b[:, 0:1],
                                    xn1[:, nt, half * 512:(half + 1) * 512],
                                    start=(nt == 0), stop=(nt == NT - 1))
            p.inc(last, "pe")

        pe_s = p.val("pe")

        @blk.vector
        def _(v):
            v.wait_ge(p.sems["pe"], pe_s)
            ins = v.tensor_copy(out=psums_sb[:, :], in_=P0[0:1, 0:1024])
            p.inc(ins, "ev")

        @blk.sync
        def _(sync):
            sync.wait_ge(p.sems["ev"], p.val("ev"))
            ins = sync.dma_start(out=psum_o[:, :], in_=psums_sb[:, :])
            p.inc(ins, "st", 16)
            sync.wait_ge(p.sems["st"], p.val("st"))
    return p


def prep_L2(I, core, qn, pstates):
    b, t4 = core // 4, core % 4
    xs = np.asarray(I["L0_attn_a_x"][b, t4 * TOK:(t4 + 1) * TOK, :], np.float32)
    return dict(
        xs=_tokmajor(xs),
        qn=qn,
        pst=pstates,
        woT=_chunked(np.ascontiguousarray(np.asarray(I["L0_attn_a_Wo"], np.float32).T)),
        lng=_colvec(I["L0_attn_a_ln_g"]),
        lnb=_colvec(I["L0_attn_a_ln_b"]),
        bo_row=np.asarray(I["L0_attn_a_bo"], np.float32)[None, :],
        idm=_ident_bf(),
    )




def emit_ln_T_dram(p, blk, x_sb, xnT_dram, stage, PTb, id_sb, wait_sem,
                   wait_val, stats, scr):
    emit_ln(p, blk, x_sb, None, wait_sem, wait_val, "ve", stats, scr, nt_list=[])
    ve_stats = p.val("ve")
    stg2 = p.sb("stg2lnT", [128, 2, 512], BF16)
    st_base = p.val("st")
    pe_after_nt = {}

    for nt in range(NT):
        @blk.vector
        def _(v, nt=nt):
            v.wait_ge(p.sems["ve"], ve_stats)
            if nt >= 2:
                v.wait_ge(p.sems["pe"], pe_after_nt[nt - 2])
            ins = v.tensor_scalar(out=stage[:, nt % 2, :], in0=x_sb[:, nt, :],
                                  scalar1=stats[:, 32 + nt:33 + nt],
                                  scalar2=stats[:, 48 + nt:49 + nt],
                                  op0=ALU.subtract, op1=ALU.mult)
            p.inc(ins, "ve")

        ve_ap = p.val("ve")
        for ch in range(2):
            cj = ch * 4
            gi = 2 * nt + ch

            @blk.tensor
            def _(t, nt=nt, cj=cj, gi=gi, ve_ap=ve_ap):
                t.wait_ge(p.sems["ve"], ve_ap)
                if gi >= 2:
                    t.wait_ge(p.sems["e2"], e2_of[gi - 2])
                bank = PTb[gi % 2]
                last = None
                for j in range(4):
                    last = t.transpose(bank[:, j * 128:(j + 1) * 128],
                                       stage[:, nt % 2, (cj + j) * 128:(cj + j + 1) * 128],
                                       id_sb[:, :])
                p.inc(last, "pe")

            pe_tr = p.val("pe")

            @blk.vector
            def _(v, gi=gi, pe_tr=pe_tr):
                v.wait_ge(p.sems["pe"], pe_tr)
                if gi >= 2:
                    v.wait_ge(p.sems["st"], st_base + 16 * (gi - 1))
                bank = PTb[gi % 2]
                ins = v.tensor_copy(out=stg2[:, gi % 2, :], in_=bank[:, :])
                p.inc(ins, "e2")

            if gi == 0:
                e2_of = {}
            e2_of[gi] = p.val("e2")
            e2_now = p.val("e2")

            @blk.sync
            def _(sync, nt=nt, cj=cj, gi=gi, e2_now=e2_now):
                sync.wait_ge(p.sems["e2"], e2_now)
                ins = sync.dma_start(out=xnT_dram[:, cj:cj + 4, nt * 128:(nt + 1) * 128],
                                     in_=stg2[:, gi % 2, :])
                p.inc(ins, "st", 16)

        pe_after_nt[nt] = p.val("pe")



def build_FFN0():
    p = Prog()
    nc = p.nc
    xs = p.dram_in("xs", [128, NT, D])
    parts = p.dram_in("parts", [128, 32])
    ar_i = p.dram_in("ar", [1, 1])
    cnw1 = p.dram_in("cnw1", [128, NCH, D])
    cnw2 = p.dram_in("cnw2", [128, NCH, 512])
    cnw3 = p.dram_in("cnw3", [128, 4, 256])
    cnw4 = p.dram_in("cnw4", [128, 2, 1])
    cnb1 = p.dram_in("cnb1", [128, NCH])
    cnb2 = p.dram_in("cnb2", [128, 4])
    cnb3 = p.dram_in("cnb3", [128, 2])
    cnb4 = p.dram_in("cnb4", [1, 1])
    gcw1 = p.dram_in("gcw1", [128, NCH, 512])
    gcw2 = p.dram_in("gcw2", [128, 4, 256])
    gcw3 = p.dram_in("gcw3", [128, 2, 1])
    gcb1 = p.dram_in("gcb1", [128, 4])
    gcb2 = p.dram_in("gcb2", [128, 2])
    gcb3 = p.dram_in("gcb3", [1, 1])
    idm = p.dram_in("idm", [128, 128], BF16)
    idf_i = p.dram_in("idf", [128, 128])
    x_o = p.dram_out("x_out", [128, NT, D])
    xnT_o = p.dram_out("xnT_out", [128, NCH, TOK], BF16)

    x_sb = p.sb("x", [128, NT, D])
    OA = p.sb("OA", [128, NT, D])
    tT = p.sb("tT", [128, NCH, 512], BF16)
    h1c = p.sb("h1c", [128, NCH, 512], BF16)
    h2c = p.sb("h2c", [128, 4, 512], BF16)
    h3c = p.sb("h3c", [128, 2, 512], BF16)
    stage = p.sb("stage", [128, 2, D], BF16)
    stats = p.sb("stats", [128, 128])
    scr = p.sb("scr", [128, D])
    id_sb = p.sb("id", [128, 128], BF16)
    idf_sb = p.sb("idf_s", [128, 128])
    w1 = p.sb("w1", [128, NCH, D], BF16)
    w2 = p.sb("w2c", [128, NCH, 512], BF16)
    w3 = p.sb("w3c", [128, 4, 256], BF16)
    w4 = p.sb("w4c", [128, 2, 1], BF16)
    b1 = p.sb("b1c", [128, NCH])
    b2 = p.sb("b2c", [128, 4])
    b3 = p.sb("b3c", [128, 2])
    gw1 = p.sb("gw1", [128, NCH, 512], BF16)
    gw2 = p.sb("gw2", [128, 4, 256], BF16)
    gw3 = p.sb("gw3", [128, 2, 1], BF16)
    gb1 = p.sb("gb1", [128, 4])
    gb2 = p.sb("gb2", [128, 2])
    rows = p.sb("rows", [128, 32])
    gcf = p.sb("gcf", [128, 16])
    onecol = p.sb("onecol", [1, 8])
    gcv = p.sb("gcv", [128, NCH], BF16)
    g1v = p.sb("g1v", [128, 4], BF16)
    g2v = p.sb("g2v", [128, 2], BF16)
    tfv = p.sb("tfv", [128, 16])
    gpos = p.sb("gpos", [128, 16])
    gneg = p.sb("gneg", [128, 16])
    acol = p.sb("acol", [128, 4])
    scrq = p.sb("scrq", [128, 512])
    ones_fr = p.sb("ones_fr", [1, 128])

    P0 = p.ps("P0", [128, 512])
    P1 = p.ps("P1", [128, 512])
    PTF = p.ps("PTF", [128, 16])
    PT = [p.ps("PTa", [128, 512], BF16), p.ps("PTb", [128, 512], BF16)]
    PF = [p.ps("PFa", [128, 512]), p.ps("PFb", [128, 512])]

    with nc.Block() as blk:
        @blk.sync
        def _(sync):
            sync.dma_start(out=x_sb[:, :, :], in_=xs[:, :, :]).then_inc(p.sems["ld"], 16)
            sync.dma_start(out=id_sb[:, :], in_=idm[:, :]).then_inc(p.sems["ld"], 16)
            sync.dma_start(out=idf_sb[:, :], in_=idf_i[:, :]).then_inc(p.sems["ld"], 16)
            p.cnt["ld"] += 16
            ins = sync.dma_start(out=rows[:, :], in_=parts[:, :])
            p.inc(ins, "ld", 16)
            p.cnt["ld"] += 32

        @blk.gpsimd
        def _(g):
            for dst, srci in [(w1, cnw1), (w2, cnw2), (w3, cnw3), (w4, cnw4),
                              (gw1, gcw1), (gw2, gcw2), (gw3, gcw3)]:
                g.dma_start(out=dst[:, :, :], in_=srci[:, :, :]).then_inc(p.sems["wd"], 16)
                p.cnt["wd"] += 16
            for dst, srci in [(b1, cnb1), (b2, cnb2), (b3, cnb3),
                              (gb1, gcb1), (gb2, gcb2)]:
                g.dma_start(out=dst[:, :], in_=srci[:, :]).then_inc(p.sems["wd"], 16)
                p.cnt["wd"] += 16
            g.dma_start(out=onecol[0:1, 0:1], in_=gcb3[:, :]).then_inc(p.sems["wd"], 16)
            g.dma_start(out=onecol[0:1, 1:2], in_=cnb4[:, :]).then_inc(p.sems["wd"], 16)
            g.dma_start(out=onecol[0:1, 2:3], in_=ar_i[:, :]).then_inc(p.sems["wd"], 16)
            p.cnt["wd"] += 48
            ins = g.memset(ones_fr[:, :], 1.0)
            p.inc(ins, "wd", 1)

        wd_all = p.val("wd")

        emit_ln(p, blk, x_sb, None, "ld", 48, "ev", stats, scr, xf_sb=OA)
        ln0 = p.val("ev")

        @blk.vector
        def _(v):
            v.wait_ge(p.sems["ld"], 48)
            ins = v.tensor_tensor(out=gcf[:, 0:8], in0=rows[:, 0:8],
                                  in1=rows[:, 8:16], op=ALU.add)
            v.wait_ge(p.sems["ve"], p.inc(ins, "ve"))
            ins = v.tensor_tensor(out=gcf[:, 8:16], in0=rows[:, 16:24],
                                  in1=rows[:, 24:32], op=ALU.add)
            v.wait_ge(p.sems["ve"], p.inc(ins, "ve"))
            ins = v.tensor_tensor(out=gcf[:, 0:8], in0=gcf[:, 0:8],
                                  in1=gcf[:, 8:16], op=ALU.add)
            v.wait_ge(p.sems["ve"], p.inc(ins, "ve"))
            ins = v.tensor_scalar(out=gcv[:, :], in0=gcf[:, 0:8],
                                  scalar1=1.0 / 8192.0, scalar2=0.0,
                                  op0=ALU.mult, op1=ALU.add)
            v.wait_ge(p.sems["ve"], p.inc(ins, "ve"))

        gcv_done = p.val("ve")

        @blk.tensor
        def _(t):
            t.wait_ge(p.sems["ve"], gcv_done)
            t.wait_ge(p.sems["wd"], wd_all)
            for fj in range(4):
                last = None
                for ci in range(NCH):
                    last = t.matmul(P0[:, fj:fj + 1], gw1[:, ci, fj * 128:(fj + 1) * 128],
                                    gcv[:, ci:ci + 1],
                                    start=(ci == 0 and fj == 0), stop=False)
                p.inc(last, "pe")

        pe_g1 = p.val("pe")

        @blk.scalar
        def _(s):
            s.wait_ge(p.sems["pe"], pe_g1)
            for fj in range(4):
                p.inc(s.activation(out=g1v[:, fj:fj + 1], in_=P0[:, fj:fj + 1],
                                   func=AF.Gelu, bias=gb1[:, fj:fj + 1],
                                   scale=_f(1.0)), "ac")

        ac_g1 = p.val("ac")

        @blk.tensor
        def _(t):
            t.wait_ge(p.sems["ac"], ac_g1)
            for fj in range(2):
                last = None
                for ci in range(4):
                    last = t.matmul(P1[:, fj:fj + 1], gw2[:, ci, fj * 128:(fj + 1) * 128],
                                    g1v[:, ci:ci + 1],
                                    start=(ci == 0 and fj == 0), stop=False)
                p.inc(last, "pe")

        pe_g2 = p.val("pe")

        @blk.scalar
        def _(s):
            s.wait_ge(p.sems["pe"], pe_g2)
            for fj in range(2):
                p.inc(s.activation(out=g2v[:, fj:fj + 1], in_=P1[:, fj:fj + 1],
                                   func=AF.Gelu, bias=gb2[:, fj:fj + 1],
                                   scale=_f(1.0)), "ac")

        ac_g2 = p.val("ac")

        @blk.tensor
        def _(t):
            t.wait_ge(p.sems["ac"], ac_g2)
            last = None
            for ci in range(2):
                last = t.matmul(P0[0:1, 8:9], gw3[:, ci, 0:1], g2v[:, ci:ci + 1],
                                start=(ci == 0), stop=(ci == 1))
            p.inc(last, "pe")

        pe_gf = p.val("pe")

        @blk.scalar
        def _(s):
            s.wait_ge(p.sems["pe"], pe_gf)
            p.inc(s.activation(out=onecol[0:1, 3:4], in_=P0[0:1, 8:9],
                               func=AF.Sigmoid, bias=onecol[0:1, 0:1],
                               scale=_f(1.0)), "ac")

        ac_gf = p.val("ac")

        @blk.vector
        def _(v):
            v.wait_ge(p.sems["ac"], ac_gf)
            ins = v.tensor_tensor(out=onecol[0:1, 4:5], in0=onecol[0:1, 2:3],
                                  in1=onecol[0:1, 3:4], op=ALU.mult)
            v.wait_ge(p.sems["ve"], p.inc(ins, "ve"))
            ins = v.tensor_scalar(out=onecol[0:1, 4:5], in0=onecol[0:1, 4:5],
                                  scalar1=1.4, scalar2=0.0, op0=ALU.mult, op1=ALU.add)
            v.wait_ge(p.sems["ve"], p.inc(ins, "ve"))
            ins = v.tensor_tensor(out=onecol[0:1, 4:5], in0=onecol[0:1, 4:5],
                                  in1=onecol[0:1, 2:3], op=ALU.subtract)
            v.wait_ge(p.sems["ve"], p.inc(ins, "ve"))
            ins = v.tensor_scalar(out=onecol[0:1, 4:5], in0=onecol[0:1, 4:5],
                                  scalar1=0.1, scalar2=0.0, op0=ALU.add, op1=ALU.add)
            v.wait_ge(p.sems["ve"], p.inc(ins, "ve"))
            ins = v.tensor_scalar(out=onecol[0:1, 5:6], in0=onecol[0:1, 2:3],
                                  scalar1=0.6, scalar2=0.0, op0=ALU.mult, op1=ALU.add)
            v.wait_ge(p.sems["ve"], p.inc(ins, "ve"))
            ins = v.tensor_copy(out=onecol[0:1, 6:7], in_=onecol[0:1, 1:2])
            v.wait_ge(p.sems["ve"], p.inc(ins, "ve"))

        ve_sc = p.val("ve")

        @blk.tensor
        def _(t):
            t.wait_ge(p.sems["ve"], ve_sc)
            ins = t.matmul(P1[:, 8:11], ones_fr[:, :], onecol[0:1, 4:7],
                           start=True, stop=True)
            p.inc(ins, "pe")

        pe_rep = p.val("pe")

        @blk.vector
        def _(v):
            v.wait_ge(p.sems["pe"], pe_rep)
            p.inc(v.tensor_copy(out=acol[:, 0:3], in_=P1[:, 8:11]), "ve")

        prep_done = p.val("ve")

        for cyc in range(3):
            for qc in range(4):
                e2_qbase = p.val("e2")

                @blk.tensor
                def _(t, qc=qc, cyc=cyc, e2_qbase=e2_qbase):
                    t.wait_ge(p.sems["ve"], prep_done)
                    for gi in range(8):
                        nt = qc * 4 + gi // 2
                        cj = (gi % 2) * 4
                        bank = PF[gi % 2]
                        if gi >= 2:
                            t.wait_ge(p.sems["e2"], e2_qbase + gi - 1)
                        last = None
                        for j in range(4):
                            last = t.transpose(bank[:, j * 128:(j + 1) * 128],
                                               OA[:, nt, (cj + j) * 128:(cj + j + 1) * 128],
                                               idf_sb[:, :])
                        p.inc(last, "pe")

                pe_trb = p.val("pe") - 8

                @blk.vector
                def _(v, qc=qc, pe_trb=pe_trb):
                    for gi in range(8):
                        nt_in_qc = gi // 2
                        cj = (gi % 2) * 4
                        bank = PF[gi % 2]
                        v.wait_ge(p.sems["pe"], pe_trb + gi + 1)
                        ins = v.tensor_copy(
                            out=tT[:, cj:cj + 4, nt_in_qc * 128:(nt_in_qc + 1) * 128],
                            in_=bank[:, :])
                        p.inc(ins, "e2")

                tr_q = p.val("e2")

                ac_h1base = p.val("ac")
                for fj in range(NCH):
                    PA = [P0, P1][fj % 2]

                    @blk.tensor
                    def _(t, fj=fj, PA=PA, tr_q=tr_q, ac_h1base=ac_h1base):
                        if fj == 0:
                            t.wait_ge(p.sems["e2"], tr_q)
                        if fj >= 2:
                            t.wait_ge(p.sems["ac"], ac_h1base + fj - 1)
                        last = None
                        for ci in range(NCH):
                            last = t.matmul(PA[:, :], w1[:, ci, fj * 128:(fj + 1) * 128],
                                            tT[:, ci, :], start=(ci == 0),
                                            stop=(ci == NCH - 1))
                        p.inc(last, "pe")

                    pe_h1 = p.val("pe")

                    @blk.scalar
                    def _(s, fj=fj, PA=PA, pe_h1=pe_h1):
                        s.wait_ge(p.sems["pe"], pe_h1)
                        p.inc(s.activation(out=h1c[:, fj, :], in_=PA[:, :],
                                           func=AF.Gelu, bias=b1[:, fj:fj + 1],
                                           scale=_f(1.0)), "ac")

                h1_done = p.val("ac")

                ac_h2base = p.val("ac")
                for fj in range(4):
                    PA = [P0, P1][fj % 2]

                    @blk.tensor
                    def _(t, fj=fj, PA=PA, h1_done=h1_done, ac_h2base=ac_h2base):
                        if fj == 0:
                            t.wait_ge(p.sems["ac"], h1_done)
                        if fj >= 2:
                            t.wait_ge(p.sems["ac"], ac_h2base + fj - 1)
                        last = None
                        for ci in range(NCH):
                            last = t.matmul(PA[:, :], w2[:, ci, fj * 128:(fj + 1) * 128],
                                            h1c[:, ci, :], start=(ci == 0),
                                            stop=(ci == NCH - 1))
                        p.inc(last, "pe")

                    pe_h2 = p.val("pe")

                    @blk.scalar
                    def _(s, fj=fj, PA=PA, pe_h2=pe_h2):
                        s.wait_ge(p.sems["pe"], pe_h2)
                        p.inc(s.activation(out=h2c[:, fj, :], in_=PA[:, :],
                                           func=AF.Gelu, bias=b2[:, fj:fj + 1],
                                           scale=_f(1.0)), "ac")

                h2_done = p.val("ac")

                for fj in range(2):
                    PA = [P0, P1][fj % 2]

                    @blk.tensor
                    def _(t, fj=fj, PA=PA, h2_done=h2_done):
                        if fj == 0:
                            t.wait_ge(p.sems["ac"], h2_done)
                        last = None
                        for ci in range(4):
                            last = t.matmul(PA[:, :], w3[:, ci, fj * 128:(fj + 1) * 128],
                                            h2c[:, ci, :], start=(ci == 0),
                                            stop=(ci == 3))
                        p.inc(last, "pe")

                    pe_h3 = p.val("pe")

                    @blk.scalar
                    def _(s, fj=fj, PA=PA, pe_h3=pe_h3):
                        s.wait_ge(p.sems["pe"], pe_h3)
                        p.inc(s.activation(out=h3c[:, fj, :], in_=PA[:, :],
                                           func=AF.Gelu, bias=b3[:, fj:fj + 1],
                                           scale=_f(1.0)), "ac")

                h3_done = p.val("ac")

                @blk.tensor
                def _(t, qc=qc, h3_done=h3_done):
                    t.wait_ge(p.sems["ac"], h3_done)
                    last = None
                    for ntq in range(4):
                        col = qc * 4 + ntq
                        for fj in range(2):
                            last = t.matmul(PTF[:, col:col + 1],
                                            h3c[:, fj, ntq * 128:(ntq + 1) * 128],
                                            w4[:, fj, 0:1],
                                            start=(qc == 0 and ntq == 0 and fj == 0),
                                            stop=False)
                    p.inc(last, "pe")

            pe_tf = p.val("pe")

            @blk.scalar
            def _(s, pe_tf=pe_tf):
                s.wait_ge(p.sems["pe"], pe_tf)
                p.inc(s.activation(out=tfv[:, :], in_=PTF[:, :], func=AF.Sigmoid,
                                   bias=acol[:, 2:3], scale=_f(1.0)), "ac")

            ac_tf = p.val("ac")

            @blk.vector
            def _(v, ac_tf=ac_tf):
                v.wait_ge(p.sems["ac"], ac_tf)
                ins = v.tensor_scalar(out=gpos[:, :], in0=tfv[:, :],
                                      scalar1=acol[:, 1:2], scalar2=acol[:, 0:1],
                                      op0=ALU.mult, op1=ALU.add)
                v.wait_ge(p.sems["ve"], p.inc(ins, "ve"))
                ins = v.tensor_scalar(out=gneg[:, :], in0=gpos[:, :],
                                      scalar1=-1.0, scalar2=0.0,
                                      op0=ALU.mult, op1=ALU.add)
                v.wait_ge(p.sems["ve"], p.inc(ins, "ve"))

            g_done = p.val("ve")

            @blk.vector
            def _(v, g_done=g_done):
                v.wait_ge(p.sems["ve"], g_done)
                Q = 256
                for nt in range(NT):
                    gp = gpos[:, nt:nt + 1]
                    gn = gneg[:, nt:nt + 1]
                    ins = v.tensor_copy(out=scrq[:, 0:Q], in_=OA[:, nt, 0:Q])
                    v.wait_ge(p.sems["ve"], p.inc(ins, "ve"))
                    ins = v.tensor_copy(out=scrq[:, Q:2 * Q], in_=OA[:, nt, Q:2 * Q])
                    v.wait_ge(p.sems["ve"], p.inc(ins, "ve"))
                    ins = v.scalar_tensor_tensor(out=OA[:, nt, 0:Q],
                                                 in0=OA[:, nt, 2 * Q:3 * Q], scalar=gp,
                                                 in1=OA[:, nt, 0:Q],
                                                 op0=ALU.mult, op1=ALU.add)
                    v.wait_ge(p.sems["ve"], p.inc(ins, "ve"))
                    ins = v.scalar_tensor_tensor(out=OA[:, nt, Q:2 * Q],
                                                 in0=OA[:, nt, 3 * Q:4 * Q], scalar=gn,
                                                 in1=OA[:, nt, Q:2 * Q],
                                                 op0=ALU.mult, op1=ALU.add)
                    v.wait_ge(p.sems["ve"], p.inc(ins, "ve"))
                    ins = v.scalar_tensor_tensor(out=OA[:, nt, 2 * Q:3 * Q],
                                                 in0=scrq[:, 0:Q], scalar=gn,
                                                 in1=OA[:, nt, 2 * Q:3 * Q],
                                                 op0=ALU.mult, op1=ALU.add)
                    v.wait_ge(p.sems["ve"], p.inc(ins, "ve"))
                    ins = v.scalar_tensor_tensor(out=OA[:, nt, 3 * Q:4 * Q],
                                                 in0=scrq[:, Q:2 * Q], scalar=gp,
                                                 in1=OA[:, nt, 3 * Q:4 * Q],
                                                 op0=ALU.mult, op1=ALU.add)
                    v.wait_ge(p.sems["ve"], p.inc(ins, "ve"))

        cycles_done = p.val("ve")

        @blk.vector
        def _(v):
            v.wait_ge(p.sems["ve"], cycles_done)
            for nt in range(NT):
                p.inc(v.tensor_tensor(out=x_sb[:, nt, :], in0=x_sb[:, nt, :],
                                      in1=OA[:, nt, :], op=ALU.add), "ev")

        x_done = p.val("ev")

        @blk.sync
        def _(sync):
            sync.wait_ge(p.sems["ev"], x_done)
            ins = sync.dma_start(out=x_o[:, :, :], in_=x_sb[:, :, :])
            p.inc(ins, "st", 16)

        emit_ln_T_dram(p, blk, x_sb, xnT_o, stage, PT, id_sb, "ev", x_done,
                       stats, scr)

        dbg_o = p.dram_out("dbg", [128, 256])
        dbg2_o = p.dram_out("dbg2", [128, 2, D], BF16)

        @blk.sync
        def _(sync):
            sync.wait_ge(p.sems["st"], p.val("st"))
            sync.dma_start(out=dbg_o[:, 0:128], in_=stats[:, :]).then_inc(p.sems["st"], 16)
            sync.dma_start(out=dbg2_o[:, :, :], in_=stage[:, :, :]).then_inc(p.sems["st"], 16)
            p.cnt["st"] += 32
            sync.wait_ge(p.sems["st"], p.val("st"))
    return p


def prep_FFN0(I, xs, parts):
    return dict(
        xs=xs, parts=parts,
        ar=np.asarray(I["L0_ffn_adaptive_range"], np.float32).reshape(1, 1),
        cnw1=_chunked(np.asarray(I["L0_ffn_cn_w1"], np.float32)),
        cnw2=_chunked(np.asarray(I["L0_ffn_cn_w2"], np.float32)),
        cnw3=np.ascontiguousarray(np.asarray(I["L0_ffn_cn_w3"], np.float32)
                                  .reshape(4, 128, 256).transpose(1, 0, 2)),
        cnw4=np.ascontiguousarray(np.asarray(I["L0_ffn_cn_w4"], np.float32)
                                  .reshape(2, 128, 1).transpose(1, 0, 2)),
        cnb1=_colvec(I["L0_ffn_cn_b1"]),
        cnb2=np.ascontiguousarray(np.asarray(I["L0_ffn_cn_b2"], np.float32).reshape(4, 128).T),
        cnb3=np.ascontiguousarray(np.asarray(I["L0_ffn_cn_b3"], np.float32).reshape(2, 128).T),
        cnb4=np.asarray(I["L0_ffn_cn_b4"], np.float32).reshape(1, 1),
        gcw1=_chunked(np.asarray(I["L0_ffn_gc_w1"], np.float32)),
        gcw2=np.ascontiguousarray(np.asarray(I["L0_ffn_gc_w2"], np.float32)
                                  .reshape(4, 128, 256).transpose(1, 0, 2)),
        gcw3=np.ascontiguousarray(np.asarray(I["L0_ffn_gc_w3"], np.float32)
                                  .reshape(2, 128, 1).transpose(1, 0, 2)),
        gcb1=np.ascontiguousarray(np.asarray(I["L0_ffn_gc_b1"], np.float32).reshape(4, 128).T),
        gcb2=np.ascontiguousarray(np.asarray(I["L0_ffn_gc_b2"], np.float32).reshape(2, 128).T),
        gcb3=np.asarray(I["L0_ffn_gc_b3"], np.float32).reshape(1, 1),
        idm=_ident_bf(),
        idf=np.eye(128, dtype=np.float32),
    )


class _ColView:

    def __init__(self, t, off, width=512):
        self.t = t
        self.off = off
        self.width = width

    def __getitem__(self, idx):
        ps, fs = idx
        start = self.off + (fs.start or 0)
        stop = self.off + (fs.stop if fs.stop is not None else self.width)
        return self.t[ps, start:stop]



def emit_phaseB(p, blk, causal, M, qsb, ksb, vsb, msk_sb, attT, avf, rcp, ysb,
                ones_c, ones64, SCall, PAV, PAVB, PDEN, PDENB,
                done_sem, done_val, ld_val):
    for pr in range(2):
        for qc in range(16):
            ktmax = 4 * (qc + 1) if causal else NKT
            nsk = ktmax // 2
            e2_prev = p.val("e2")
            ac_of = {}
            pe_sc_of = {}
            pe_av_of = {}
            ve_msk_of = {}

            def emit_sc_pair(skt):
                @blk.tensor
                def _(t, pr=pr, qc=qc, skt=skt):
                    if pr == 0 and qc == 0 and skt == 0:
                        t.wait_ge(p.sems[done_sem], done_val)
                        t.wait_ge(p.sems["ld"], ld_val)
                    if skt >= 1:
                        t.wait_ge(p.sems["ac"], ac_of[skt - 1])
                    elif not (pr == 0 and qc == 0):
                        t.wait_ge(p.sems["e2"], e2_prev)
                    last = None
                    for kt in (2 * skt, 2 * skt + 1):
                        half = (kt % 2) * 1024
                        t.matmul(SCall[:, half:half + 512],
                                 ksb[0:64, pr, kt * 128:(kt + 1) * 128],
                                 qsb[0:64, pr, qc * 512:(qc + 1) * 512],
                                 start=True, stop=True, tile_position=(0, 0))
                        last = t.matmul(SCall[:, half + 512:half + 1024],
                                        ksb[64:128, pr, kt * 128:(kt + 1) * 128],
                                        qsb[64:128, pr, qc * 512:(qc + 1) * 512],
                                        start=True, stop=True,
                                        tile_position=(64, 0))
                    pe_sc_of[skt] = p.inc(last, "pe")

            def emit_exp(skt):
                @blk.scalar
                def _(s, pr=pr, qc=qc, skt=skt):
                    s.wait_ge(p.sems["pe"], pe_sc_of[skt])
                    if skt >= 2:
                        s.wait_ge(p.sems["pe"], pe_av_of[skt - 2])
                    elif not (pr == 0 and qc == 0):
                        s.wait_ge(p.sems["e2"], e2_prev)
                    ins = s.activation(out=attT[:, skt % 2, :],
                                       in_=SCall[:, :], func=AF.Exp,
                                       bias=_f(-M), scale=_f(0.125))
                    ac_of[skt] = p.inc(ins, "ac")

            def emit_masks(skt):
                @blk.vector
                def _(v, qc=qc, skt=skt):
                    v.wait_ge(p.sems["ac"], ac_of[skt])
                    for kt in (2 * skt, 2 * skt + 1):
                        d = kt - 4 * qc
                        ins = v.tensor_tensor(
                            out=attT[:, skt % 2, (kt % 2) * 1024:(kt % 2) * 1024 + 1024],
                            in0=attT[:, skt % 2, (kt % 2) * 1024:(kt % 2) * 1024 + 1024],
                            in1=msk_sb[:, d, :], op=ALU.mult)
                    ve_msk_of[skt] = p.inc(ins, "ve")

            def emit_av_pair(skt):
                diag = causal and (2 * skt >= 4 * qc)

                @blk.tensor
                def _(t, pr=pr, qc=qc, skt=skt, diag=diag):
                    if diag:
                        t.wait_ge(p.sems["ve"], ve_msk_of[skt])
                    else:
                        t.wait_ge(p.sems["ac"], ac_of[skt])
                    if skt == 0 and not (pr == 0 and qc == 0):
                        t.wait_ge(p.sems["e2"], e2_prev)
                    last = None
                    for kt in (2 * skt, 2 * skt + 1):
                        st = (kt == 0)
                        sp = (kt == ktmax - 1)
                        a0 = attT[:, skt % 2, (kt % 2) * 1024:(kt % 2) * 1024 + 512]
                        a1 = attT[:, skt % 2,
                                  (kt % 2) * 1024 + 512:(kt % 2) * 1024 + 1024]
                        t.matmul(PAV[0:64, :], vsb[:, kt, pr * 128:pr * 128 + 64],
                                 a0, start=st, stop=sp, tile_position=(0, 0))
                        t.matmul(PAVB[64:128, :],
                                 vsb[:, kt, pr * 128 + 64:pr * 128 + 128],
                                 a1, start=st, stop=sp, tile_position=(0, 64))
                        t.matmul(PDEN[0:1, :], ones_c[:, :], a0,
                                 start=st, stop=sp, tile_position=(0, 0))
                        last = t.matmul(PDENB[0:1, :], ones_c[:, :], a1,
                                        start=st, stop=sp, tile_position=(0, 0))
                    pe_av_of[skt] = p.inc(last, "pe")

            for skt in range(nsk):
                emit_sc_pair(skt)
                emit_exp(skt)
                if causal and 2 * skt >= 4 * qc:
                    emit_masks(skt)
                if skt >= 1:
                    emit_av_pair(skt - 1)
            emit_av_pair(nsk - 1)

            pe_av = p.val("pe")

            @blk.vector
            def _(v, pe_av=pe_av):
                v.wait_ge(p.sems["pe"], pe_av)
                ins = v.reciprocal(out=rcp[0:1, 0, :], in_=PDEN[0:1, :])
                v.wait_ge(p.sems["ve"], p.inc(ins, "ve"))
                ins = v.reciprocal(out=rcp[0:1, 1, :], in_=PDENB[0:1, :])
                v.wait_ge(p.sems["ve"], p.inc(ins, "ve"))

            ve_rc = p.val("ve")

            @blk.tensor
            def _(t, ve_rc=ve_rc):
                t.wait_ge(p.sems["ve"], ve_rc)
                t.matmul(SCall[0:64, 0:512], ones64[0:1, :], rcp[0:1, 0, :],
                         start=True, stop=True, tile_position=(0, 0))
                ins = t.matmul(SCall[64:128, 1024:1536], ones64[0:1, :], rcp[0:1, 1, :],
                               start=True, stop=True, tile_position=(0, 64))
                p.inc(ins, "pe")

            pe_bc = p.val("pe")

            @blk.vector
            def _(v, pr=pr, qc=qc, pe_bc=pe_bc):
                v.wait_ge(p.sems["pe"], pe_bc)
                ins = v.tensor_copy(out=avf[0:64, :], in_=PAV[0:64, :])
                v.wait_ge(p.sems["ve"], p.inc(ins, "ve"))
                ins = v.tensor_copy(out=avf[64:128, :], in_=PAVB[64:128, :])
                v.wait_ge(p.sems["ve"], p.inc(ins, "ve"))
                ins = v.tensor_tensor(out=ysb[0:64, pr, qc * 512:(qc + 1) * 512],
                                      in0=avf[0:64, :], in1=SCall[0:64, 0:512],
                                      op=ALU.mult)
                p.inc(ins, "e2")
                ins = v.tensor_tensor(out=ysb[64:128, pr, qc * 512:(qc + 1) * 512],
                                      in0=avf[64:128, :], in1=SCall[64:128, 1024:1536],
                                      op=ALU.mult)
                p.inc(ins, "e2")



def build_ATT(causal, M):
    p = Prog()
    nc = p.nc
    xnT_i = p.dram_in("xnT", [128, NCH, SEQ], BF16)
    wq_i = p.dram_in("wq", [128, NCH, 256])
    wk_i = p.dram_in("wk", [128, NCH, 256])
    wv_i = p.dram_in("wv", [128, NCH, 256])
    msk_i = p.dram_in("msk", [128, 4, 1024], BF16)
    y_o = p.dram_out("y", [128, 2, SEQ], BF16)

    qsb = p.sb("qsb", [128, 2, SEQ], BF16)
    ksb = p.sb("ksb", [128, 2, SEQ], BF16)
    vsb = p.sb("vsb", [128, NKT, 256], BF16)
    ysb = p.sb("ysb", [128, 2, SEQ], BF16)
    xnc = p.sb("xnc", [128, 2, NCH, 512], BF16)
    wq_sb = p.sb("wqs", [128, NCH, 256], BF16)
    wk_sb = p.sb("wks", [128, NCH, 256], BF16)
    wv_sb = p.sb("wvs", [128, NCH, 256], BF16)
    msk_sb = p.sb("msks", [128, 4, 1024], BF16)
    attT = p.sb("attT", [128, 2, 2048], BF16)
    avf = p.sb("avf", [128, 512])
    rcp = p.sb("rcp", [1, 2, 512])
    ones_c = p.sb("ones_c", [128, 1], BF16)
    ones64 = p.sb("ones64", [64, 64])

    SCall = p.ps("SCall", [128, 2048])
    P0 = _ColView(SCall, 0)
    P1 = _ColView(SCall, 512)
    P2 = _ColView(SCall, 1024)
    P3 = _ColView(SCall, 1536)
    PAV = p.ps("PAV", [128, 512])
    PAVB = p.ps("PAVB", [128, 512])
    PDEN = p.ps("PDEN", [128, 512])
    PDENB = p.ps("PDENB", [128, 512])

    with nc.Block() as blk:
        @blk.gpsimd
        def _(g):
            g.dma_start(out=wq_sb[:, :, :], in_=wq_i[:, :, :]).then_inc(p.sems["wd"], 16)
            g.dma_start(out=wk_sb[:, :, :], in_=wk_i[:, :, :]).then_inc(p.sems["wd"], 16)
            g.dma_start(out=wv_sb[:, :, :], in_=wv_i[:, :, :]).then_inc(p.sems["wd"], 16)
            p.cnt["wd"] += 48
            ins = g.memset(ones_c[:, :], 1.0)
            p.inc(ins, "wd", 1)
            ins = g.memset(ones64[:, :], 1.0)
            p.inc(ins, "wd", 1)

        @blk.sync
        def _(sync):
            ins = sync.dma_start(out=msk_sb[:, :, :], in_=msk_i[:, :, :])
            p.inc(ins, "ld", 16)

        evA_base = p.val("ev")
        for qc in range(16):
            @blk.sync
            def _(sync, qc=qc):
                if qc >= 2:
                    sync.wait_ge(p.sems["ev"], evA_base + 8 * (qc - 1))
                ins = sync.dma_start(out=xnc[:, qc % 2, :, :],
                                     in_=xnT_i[:, :, qc * 512:(qc + 1) * 512])
                p.inc(ins, "ld", 16)

            ld_q = p.val("ld")

            for which, wsb, PA_pair in [("q", wq_sb, (P0, P1)), ("k", wk_sb, (P2, P3))]:
                for fj in range(2):
                    PA = PA_pair[fj]

                    @blk.tensor
                    def _(t, qc=qc, fj=fj, wsb=wsb, PA=PA, ld_q=ld_q):
                        t.wait_ge(p.sems["ld"], ld_q)
                        t.wait_ge(p.sems["wd"], 50)
                        if qc >= 1:
                            t.wait_ge(p.sems["ev"], evA_base + 8 * qc)
                        last = None
                        for ci in range(NCH):
                            last = t.matmul(PA[:, :], wsb[:, ci, fj * 128:(fj + 1) * 128],
                                            xnc[:, qc % 2, ci, :],
                                            start=(ci == 0), stop=(ci == NCH - 1))
                        p.inc(last, "pe")

                    pe_q = p.val("pe")

                    @blk.vector
                    def _(v, qc=qc, fj=fj, which=which, PA=PA, pe_q=pe_q):
                        v.wait_ge(p.sems["pe"], pe_q)
                        dst = qsb if which == "q" else ksb
                        ins = v.tensor_copy(out=dst[:, fj, qc * 512:(qc + 1) * 512],
                                            in_=PA[:, :])
                        p.inc(ins, "ev")

            for nt in range(4):
                PA = [P0, P1, P2, P3][nt]

                @blk.tensor
                def _(t, qc=qc, nt=nt, PA=PA):
                    t.wait_ge(p.sems["ev"], evA_base + 8 * qc + 4)
                    last = None
                    for ci in range(NCH):
                        last = t.matmul(PA[:, 0:256],
                                        xnc[:, qc % 2, ci, nt * 128:(nt + 1) * 128],
                                        wv_sb[:, ci, :],
                                        start=(ci == 0), stop=(ci == NCH - 1))
                    p.inc(last, "pe")

                pe_v = p.val("pe")

                @blk.vector
                def _(v, qc=qc, nt=nt, PA=PA, pe_v=pe_v):
                    v.wait_ge(p.sems["pe"], pe_v)
                    ins = v.tensor_copy(out=vsb[:, qc * 4 + nt, :], in_=PA[:, 0:256])
                    p.inc(ins, "ev")

        phaseA_done = p.val("ev")

        emit_phaseB(p, blk, causal, M, qsb, ksb, vsb, msk_sb, attT, avf, rcp,
                    ysb, ones_c, ones64, SCall, PAV, PAVB, PDEN, PDENB,
                    "ev", phaseA_done, 16)

        yB_done = p.val("e2")

        @blk.sync
        def _(sync):
            sync.wait_ge(p.sems["e2"], yB_done)
            ins = sync.dma_start(out=y_o[:, :, :], in_=ysb[:, :, :])
            p.inc(ins, "st", 16)
            sync.wait_ge(p.sems["st"], p.val("st"))
    return p


def prep_ATT_L1aa(I, core, xnT_full):
    m = core % 4
    w = np.asarray(I["L1_attn_a_w_qkv"], np.float32)
    cols = slice(m * 256, (m + 1) * 256)
    return dict(
        xnT=xnT_full,
        wq=_chunked(np.ascontiguousarray(w[:, 0:1024][:, cols])),
        wk=_chunked(np.ascontiguousarray(w[:, 1024:2048][:, cols])),
        wv=_chunked(np.ascontiguousarray(w[:, 2048:3072][:, cols])),
        msk=_masks_bf(),
    )


def prep_ATT_L1ab(I, core, xnT_full):
    m = core % 4
    wt = np.ascontiguousarray(np.asarray(I["L1_attn_b_w_qkv"], np.float32).T)
    cols = slice(m * 256, (m + 1) * 256)
    return dict(
        xnT=xnT_full,
        wq=_chunked(np.ascontiguousarray(wt[:, 0:1024][:, cols])),
        wk=_chunked(np.ascontiguousarray(wt[:, 1024:2048][:, cols])),
        wv=_chunked(np.ascontiguousarray(wt[:, 2048:3072][:, cols])),
        msk=_masks_bf(),
    )


def _masks_bf():
    kk = np.arange(128)[:, None]
    qq = np.arange(512)[None, :]
    m = np.stack([(kk + d <= qq) for d in (0, 128, 256, 384)]).astype(BF)
    m = np.ascontiguousarray(m.transpose(1, 0, 2))
    return np.ascontiguousarray(np.concatenate([m, m], axis=2))



def build_PROJ():
    p = Prog()
    nc = p.nc
    xs = p.dram_in("xs", [128, NT, D])
    ysl_i = p.dram_in("ysl", [128, NCH, TOK], BF16)
    wp_i = p.dram_in("wp", [128, NCH, D])
    x_o = p.dram_out("x_out", [128, NT, D])
    ps_o = p.dram_out("psums", [1, D])

    x_sb = p.sb("x", [128, NT, D])
    ysl = p.sb("ysl_s", [128, NCH, TOK], BF16)
    wp = p.sb("wp_s", [128, NCH, D], BF16)
    xn_sb = p.sb("xn", [128, NT, D], BF16)
    stats = p.sb("stats", [128, 128])
    scr = p.sb("scr", [128, D])
    ones_c = p.sb("ones_c", [128, 1], BF16)
    pss = p.sb("pss", [1, D])
    P0 = p.ps("P0", [128, D])
    P1 = p.ps("P1", [128, D])

    with nc.Block() as blk:
        @blk.sync
        def _(sync):
            sync.dma_start(out=x_sb[:, :, :], in_=xs[:, :, :]).then_inc(p.sems["ld"], 16)
            ins = sync.dma_start(out=ysl[:, :, :], in_=ysl_i[:, :, :])
            p.inc(ins, "ld", 16)
            p.cnt["ld"] += 16

        @blk.gpsimd
        def _(g):
            g.dma_start(out=wp[:, :, :], in_=wp_i[:, :, :]).then_inc(p.sems["wd"], 16)
            p.cnt["wd"] += 16
            ins = g.memset(ones_c[:, :], 1.0)
            p.inc(ins, "wd", 1)

        ev_base = p.val("ev")
        for nt in range(NT):
            PA = [P0, P1][nt % 2]

            @blk.tensor
            def _(t, nt=nt, PA=PA):
                t.wait_ge(p.sems["ld"], 32)
                t.wait_ge(p.sems["wd"], 17)
                if nt >= 2:
                    t.wait_ge(p.sems["ev"], ev_base + nt - 1)
                last = None
                for ci in range(NCH):
                    last = t.matmul(PA[:, 0:512], ysl[:, ci, nt * 128:(nt + 1) * 128],
                                    wp[:, ci, 0:512], start=(ci == 0), stop=(ci == NCH - 1))
                for ci in range(NCH):
                    last = t.matmul(PA[:, 512:1024], ysl[:, ci, nt * 128:(nt + 1) * 128],
                                    wp[:, ci, 512:1024], start=(ci == 0), stop=(ci == NCH - 1))
                p.inc(last, "pe")

            pe_p = p.val("pe")

            @blk.vector
            def _(v, nt=nt, PA=PA, pe_p=pe_p):
                v.wait_ge(p.sems["pe"], pe_p)
                ins = v.tensor_tensor(out=x_sb[:, nt, :], in0=PA[:, :],
                                      in1=x_sb[:, nt, :], op=ALU.add)
                p.inc(ins, "ev")

        x_done = p.val("ev")

        @blk.sync
        def _(sync):
            sync.wait_ge(p.sems["ev"], x_done)
            ins = sync.dma_start(out=x_o[:, :, :], in_=x_sb[:, :, :])
            p.inc(ins, "st", 16)

        emit_ln(p, blk, x_sb, xn_sb, "ev", x_done, "ev", stats, scr)
        ln_done = p.val("ev")

        @blk.tensor
        def _(t):
            t.wait_ge(p.sems["ev"], ln_done)
            last = None
            for half in range(2):
                for nt in range(NT):
                    last = t.matmul(P0[0:1, half * 512:(half + 1) * 512], ones_c[:, :],
                                    xn_sb[:, nt, half * 512:(half + 1) * 512],
                                    start=(nt == 0), stop=(nt == NT - 1))
            p.inc(last, "pe")

        pe_s = p.val("pe")

        @blk.vector
        def _(v):
            v.wait_ge(p.sems["pe"], pe_s)
            ins = v.tensor_copy(out=pss[:, :], in_=P0[0:1, 0:1024])
            p.inc(ins, "e2")

        @blk.sync
        def _(sync):
            sync.wait_ge(p.sems["e2"], p.val("e2"))
            ins = sync.dma_start(out=ps_o[:, :], in_=pss[:, :])
            p.inc(ins, "st", 16)
            sync.wait_ge(p.sems["st"], p.val("st"))
    return p




def prep_PROJ(x_slice, ysl, wmat):
    return dict(xs=_tokmajor(x_slice), ysl=ysl, wp=_chunked(np.asarray(wmat, np.float32)))


def _yfeat(y_cores_of_batch):
    y_feat = np.empty((1024, SEQ), BF)
    for m, yd in enumerate(y_cores_of_batch):
        yd = np.asarray(yd).view(np.uint16).view(BF)
        for hl in range(4):
            h = 4 * m + hl
            y_feat[h * 64:(h + 1) * 64] = yd[(hl % 2) * 64:(hl % 2) * 64 + 64, hl // 2, :]
    return y_feat


def _ysl_from_yfeat(y_feat, m):
    sl = y_feat[:, m * TOK:(m + 1) * TOK]
    return np.ascontiguousarray(sl.reshape(NCH, 128, TOK).transpose(1, 0, 2))



def build_PF(want_ln2=True):
    p = Prog()
    nc = p.nc
    xs = p.dram_in("xs", [128, NT, D])
    ysl_i = p.dram_in("ysl", [128, NCH, TOK], BF16)
    wp_i = p.dram_in("wp", [128, NCH, D])
    rw_i = p.dram_in("rwT", [128, NCH, 512])
    rb_i = p.dram_in("rb", [1, 512])
    lng_i = p.dram_in("lng2", [128, NCH])
    lnb_i = p.dram_in("lnb2", [128, NCH])
    ut_i = p.dram_in("UT", [128, NCH, 512])
    vt_i = p.dram_in("VT", [128, NCH, 512])
    vr_i = p.dram_in("Vr", [128, 4, D])
    gam_i = p.dram_in("gam", [1, D])
    w1_i = p.dram_in("w1T", [128, NCH, 4096])
    w2_i = p.dram_in("w2T", [128, 32, D])
    idm = p.dram_in("idm", [128, 128], BF16)
    x_o = p.dram_out("x_out", [128, NT, D])
    xnT_o = p.dram_out("xnT_out", [128, NCH, TOK], BF16)
    xnT2_o = p.dram_out("xnT2_out", [128, NCH, TOK], BF16) if want_ln2 else None

    xio = p.sb("xio", [128, 2, D])
    yst = p.sb("yst", [128, 2, NCH, 128], BF16)
    stage = p.sb("stage", [128, 2, D], BF16)
    stats = p.sb("stats", [128, 128])
    scr = p.sb("scr", [128, D])
    xnt2 = p.sb("xnt2", [128, 2, NCH, 128], BF16)
    hzT = p.sb("hzT", [128, 4, TOK], BF16)
    ut_sb = p.sb("uts", [128, NCH, 512], BF16)
    rw_sb = p.sb("rws", [128, NCH, 512], BF16)
    vg_sb = p.sb("vgs", [128, 4, D], BF16)
    rsb = p.sb("rsb", [128, 512])
    alp = p.sb("alp", [128, 512])
    brow_bc = p.sb("brow_bc", [128, 512])
    rnuv_bc = p.sb("rnuv_bc", [128, 512])
    rowp = p.sb("rowp", [1, 3, 512])
    gam_sb = p.sb("gam_s", [1, D])
    lng_sb = p.sb("lng2s", [128, NCH])
    lnb_sb = p.sb("lnb2s", [128, NCH], BF16)
    r2c = p.sb("r2c", [128, 16])
    phi = p.sb("phi", [128, 8])
    svec = p.sb("svec", [128, 4])
    id_sb = p.sb("idsb", [128, 128], BF16)
    ones_cf = p.sb("ones_cf", [128, 1])
    ones_fr = p.sb("ones_fr", [1, 128])
    tg1 = scr[:, 0:512]
    tg2 = scr[:, 512:1024]

    wp_sb, wp_cm = p.sb_scoped("wps", [128, NCH, D], BF16)
    vt_sb, vt_cm = p.sb_scoped("vts", [128, NCH, 512], BF16)
    vgf, vgf_cm = p.sb_scoped("vgf", [128, 4, D])

    P01 = p.ps("P01", [128, D])
    P02 = p.ps("P02", [128, D])
    PR = p.ps("PR", [128, 512])
    PH = p.ps("PH", [128, 512])
    PT = [p.ps("PTa", [128, 512], BF16), p.ps("PTb", [128, 512], BF16)]

    with nc.Block() as blk:
        @blk.gpsimd
        def _(g):
            for dst, srci in [(wp_sb, wp_i), (ut_sb, ut_i), (vt_sb, vt_i),
                              (rw_sb, rw_i)]:
                g.dma_start(out=dst[:, :, :], in_=srci[:, :, :]).then_inc(p.sems["wd"], 16)
                p.cnt["wd"] += 16
            g.dma_start(out=lnb_sb[:, :], in_=lnb_i[:, :]).then_inc(p.sems["wd"], 16)
            p.cnt["wd"] += 16
            g.memset(ones_cf[:, :], 1.0)
            ins = g.memset(ones_fr[:, :], 1.0)
            p.inc(ins, "wd", 1)

        wd_ld = p.val("wd")

        @blk.sync
        def _(sync):
            sync.dma_start(out=id_sb[:, :], in_=idm[:, :]).then_inc(p.sems["ld"], 16)
            sync.dma_start(out=rowp[0:1, 0, :], in_=rb_i[:, :]).then_inc(p.sems["ld"], 16)
            sync.dma_start(out=lng_sb[:, :], in_=lng_i[:, :]).then_inc(p.sems["ld"], 16)
            sync.dma_start(out=gam_sb[:, :], in_=gam_i[:, :]).then_inc(p.sems["ld"], 16)
            ins = sync.dma_start(out=vgf[:, :, :], in_=vr_i[:, :, :])
            p.inc(ins, "ld", 16)
            p.cnt["ld"] += 64

        ld0 = p.val("ld")

        ve_add = {}
        ve_sq = {}
        pe_proj = {}
        st_store = {}
        ld_x = {}
        ld_y = {}
        for nt in range(NT):
            @blk.sync
            def _(sync, nt=nt):
                if nt >= 2:
                    sync.wait_ge(p.sems["st"], st_store[nt - 2])
                    sync.wait_ge(p.sems["ve"], ve_sq[nt - 2])
                ins = sync.dma_start(out=xio[:, nt % 2, :], in_=xs[:, nt, :])
                ld_x[nt] = p.inc(ins, "ld", 16)
                if nt >= 2:
                    sync.wait_ge(p.sems["pe"], pe_proj[nt - 2])
                ins = sync.dma_start(out=yst[:, nt % 2, :, :],
                                     in_=ysl_i[:, :, nt * 128:(nt + 1) * 128])
                ld_y[nt] = p.inc(ins, "ld", 16)

            @blk.tensor
            def _(t, nt=nt):
                t.wait_ge(p.sems["ld"], ld_y[nt])
                t.wait_ge(p.sems["wd"], wd_ld)
                if nt >= 2:
                    t.wait_ge(p.sems["ve"], ve_add[nt - 2])
                PA = [P01, P02][nt % 2]
                last = None
                for half in range(2):
                    for ci in range(NCH):
                        last = t.matmul(PA[:, half * 512:(half + 1) * 512],
                                        yst[:, nt % 2, ci, :],
                                        wp_sb[:, ci, half * 512:(half + 1) * 512],
                                        start=(ci == 0), stop=(ci == NCH - 1))
                pe_proj[nt] = p.inc(last, "pe")

            @blk.vector
            def _(v, nt=nt):
                v.wait_ge(p.sems["pe"], pe_proj[nt])
                v.wait_ge(p.sems["ld"], ld_x[nt])
                PA = [P01, P02][nt % 2]
                ins = v.tensor_tensor(out=xio[:, nt % 2, :], in0=PA[:, :],
                                      in1=xio[:, nt % 2, :], op=ALU.add)
                ve_add[nt] = p.inc(ins, "ve")
                v.wait_ge(p.sems["ve"], ve_add[nt])
                p.inc(v.tensor_reduce(out=stats[:, nt:nt + 1], in_=xio[:, nt % 2, :],
                                      axis=AX.X, op=ALU.add), "ve")
                ins = v.tensor_tensor(out=scr[:, :], in0=xio[:, nt % 2, :],
                                      in1=xio[:, nt % 2, :], op=ALU.mult)
                v.wait_ge(p.sems["ve"], p.inc(ins, "ve"))
                ins = v.tensor_reduce(out=stats[:, 16 + nt:17 + nt], in_=scr[:, :],
                                      axis=AX.X, op=ALU.add)
                ve_sq[nt] = p.inc(ins, "ve")

            @blk.sync
            def _(sync, nt=nt):
                sync.wait_ge(p.sems["ve"], ve_add[nt])
                ins = sync.dma_start(out=x_o[:, nt, :], in_=xio[:, nt % 2, :])
                st_store[nt] = p.inc(ins, "st", 16)

        st_x5 = p.val("st")

        @blk.vector
        def _(v):
            v.wait_ge(p.sems["ve"], ve_sq[NT - 1])
            ins = v.tensor_scalar(out=stats[:, 32:48], in0=stats[:, 0:16],
                                  scalar1=1.0 / 1024.0, scalar2=0.0,
                                  op0=ALU.mult, op1=ALU.add)
            v.wait_ge(p.sems["ve"], p.inc(ins, "ve"))
            ins = v.tensor_tensor(out=stats[:, 48:64], in0=stats[:, 32:48],
                                  in1=stats[:, 32:48], op=ALU.mult)
            v.wait_ge(p.sems["ve"], p.inc(ins, "ve"))
            ins = v.tensor_scalar(out=stats[:, 64:80], in0=stats[:, 16:32],
                                  scalar1=1.0 / 1024.0, scalar2=0.0,
                                  op0=ALU.mult, op1=ALU.add)
            v.wait_ge(p.sems["ve"], p.inc(ins, "ve"))
            ins = v.tensor_tensor(out=stats[:, 80:96], in0=stats[:, 64:80],
                                  in1=stats[:, 48:64], op=ALU.subtract)
            p.inc(ins, "ve")

        @blk.scalar
        def _(s):
            s.wait_ge(p.sems["ve"], p.val("ve"))
            ins = s.activation(out=stats[:, 96:112], in_=stats[:, 80:96],
                               func=AF.Sqrt, bias=_f(1e-5), scale=_f(1.0))
            p.inc(ins, "ac")

        @blk.vector
        def _(v):
            v.wait_ge(p.sems["ac"], p.val("ac"))
            ins = v.reciprocal(out=stats[:, 48:64], in_=stats[:, 96:112])
            v.wait_ge(p.sems["ve"], p.inc(ins, "ve"))
            ins = v.tensor_scalar(out=stats[:, 112:128], in0=stats[:, 80:96],
                                  scalar1=1e-5, scalar2=0.0, op0=ALU.add, op1=ALU.add)
            v.wait_ge(p.sems["ve"], p.inc(ins, "ve"))
            ins = v.reciprocal(out=stats[:, 112:128], in_=stats[:, 112:128])
            v.wait_ge(p.sems["ve"], p.inc(ins, "ve"))
            ins = v.tensor_tensor(out=stats[:, 112:128], in0=stats[:, 80:96],
                                  in1=stats[:, 112:128], op=ALU.mult)
            p.inc(ins, "ve")

        @blk.scalar
        def _(s):
            s.wait_ge(p.sems["ve"], p.val("ve"))
            ins = s.activation(out=r2c[:, :], in_=stats[:, 112:128], func=AF.Sqrt,
                               bias=_f(1e-5), scale=_f(1.0))
            p.inc(ins, "ac")

        @blk.vector
        def _(v):
            v.wait_ge(p.sems["ac"], p.val("ac"))
            ins = v.reciprocal(out=r2c[:, :], in_=r2c[:, :])
            p.inc(ins, "ve")

        ve_r2 = p.val("ve")

        @blk.tensor
        def _(t):
            t.wait_ge(p.sems["wd"], wd_ld)
            last = None
            for ci in range(NCH):
                last = t.matmul(PR[0:1, :], lnb_sb[:, ci:ci + 1], rw_sb[:, ci, :],
                                start=(ci == 0), stop=(ci == NCH - 1))
            p.inc(last, "pe")

        pe_brow = p.val("pe")

        @blk.vector
        def _(v):
            v.wait_ge(p.sems["pe"], pe_brow)
            v.wait_ge(p.sems["ld"], ld0)
            ins = v.tensor_tensor(out=rowp[0:1, 1, :], in0=PR[0:1, :],
                                  in1=rowp[0:1, 0, :], op=ALU.add)
            v.wait_ge(p.sems["ve"], p.inc(ins, "ve"))
            for ci in range(NCH):
                p.inc(v.tensor_scalar(out=rw_sb[:, ci, :], in0=rw_sb[:, ci, :],
                                      scalar1=lng_sb[:, ci:ci + 1], scalar2=0.0,
                                      op0=ALU.mult, op1=ALU.add), "ve")

        ve_rw = p.val("ve")

        ac_norm = {}
        pe_nsq_prev = None
        for wi, (wsb, dst_row) in enumerate([(ut_sb, 2), (vt_sb, 0)]):
            pe_nsq = {}
            ve_sqr = {}
            for ci in range(NCH):
                @blk.vector
                def _(v, ci=ci, wsb=wsb, wi=wi):
                    if ci >= 2:
                        v.wait_ge(p.sems["pe"], pe_nsq[ci - 2])
                    elif wi == 1 and pe_nsq_prev is not None:
                        v.wait_ge(p.sems["pe"], pe_nsq_prev[NCH - 1])
                    ins = v.tensor_tensor(out=scr[:, (ci % 2) * 512:(ci % 2 + 1) * 512],
                                          in0=wsb[:, ci, :], in1=wsb[:, ci, :],
                                          op=ALU.mult)
                    ve_sqr[ci] = p.inc(ins, "ve")

                @blk.tensor
                def _(t, ci=ci, wi=wi):
                    t.wait_ge(p.sems["ve"], ve_sqr[ci])
                    if wi == 1 and ci == 0:
                        t.wait_ge(p.sems["ac"], ac_norm[0])
                    ins = t.matmul(PH[0:1, :], ones_cf[:, :],
                                   scr[:, (ci % 2) * 512:(ci % 2 + 1) * 512],
                                   start=(ci == 0), stop=(ci == NCH - 1))
                    pe_nsq[ci] = p.inc(ins, "pe")

            pe_nsq_prev = pe_nsq

            @blk.scalar
            def _(s, wi=wi, dst_row=dst_row, pe_nsq=pe_nsq):
                s.wait_ge(p.sems["pe"], pe_nsq[NCH - 1])
                ins = s.activation(out=rowp[0:1, dst_row, :], in_=PH[0:1, :],
                                   func=AF.Sqrt)
                ac_norm[wi] = p.inc(ins, "ac")

            @blk.vector
            def _(v, wi=wi, dst_row=dst_row):
                v.wait_ge(p.sems["ac"], ac_norm[wi])
                ins = v.tensor_scalar(out=rowp[0:1, dst_row, :],
                                      in0=rowp[0:1, dst_row, :],
                                      scalar1=1e-6, scalar2=0.0,
                                      op0=ALU.max, op1=ALU.add)
                v.wait_ge(p.sems["ve"], p.inc(ins, "ve"))
                ins = v.reciprocal(out=rowp[0:1, dst_row, :], in_=rowp[0:1, dst_row, :])
                v.wait_ge(p.sems["ve"], p.inc(ins, "ve"))

        @blk.vector
        def _(v):
            ins = v.tensor_tensor(out=rowp[0:1, 2, :], in0=rowp[0:1, 2, :],
                                  in1=rowp[0:1, 0, :], op=ALU.mult)
            v.wait_ge(p.sems["ve"], p.inc(ins, "ve"))

        ve_rows = p.val("ve")

        @blk.tensor
        def _(t):
            t.wait_ge(p.sems["ve"], ve_rows)
            t.matmul(PR[:, :], ones_fr[:, :], rowp[0:1, 1, :], start=True, stop=True)
            ins = t.matmul(PH[:, :], ones_fr[:, :], rowp[0:1, 2, :], start=True,
                           stop=True)
            p.inc(ins, "pe")

        pe_bc = p.val("pe")

        @blk.vector
        def _(v):
            v.wait_ge(p.sems["pe"], pe_bc)
            p.inc(v.tensor_copy(out=brow_bc[:, :], in_=PR[:, :]), "ve")
            p.inc(v.tensor_copy(out=rnuv_bc[:, :], in_=PH[:, :]), "ve")

        ve_bc = p.val("ve")

        @blk.tensor
        def _(t):
            t.wait_ge(p.sems["ld"], ld0)
            t.wait_ge(p.sems["ve"], ve_add[NT - 1])
            t.matmul(P01[:, 0:512], ones_fr[:, :], gam_sb[0:1, 0:512],
                     start=True, stop=True)
            ins = t.matmul(P01[:, 512:1024], ones_fr[:, :], gam_sb[0:1, 512:1024],
                           start=True, stop=True)
            p.inc(ins, "pe")

        pe_gam = p.val("pe")

        @blk.vector
        def _(v):
            v.wait_ge(p.sems["pe"], pe_gam)
            for jc in range(4):
                p.inc(v.tensor_tensor(out=vg_sb[:, jc, :], in0=vgf[:, jc, :],
                                      in1=P01[:, :], op=ALU.mult), "ve")

        ve_vg = p.val("ve")
        pe_ph2 = p.val("pe")

        vgf_cm.__exit__(None, None, None)
        vt_cm.__exit__(None, None, None)
        wp_cm.__exit__(None, None, None)

        hid = p.sb("hid", [128, 8, 1024], BF16)
        w2q = p.sb("w2q", [128, 2, 8, 1024], BF16)
        xnq = p.sb("xnq", [128, NCH, 1024], BF16)
        acc = p.sb("acc", [128, 8, 1024])
        w1st = p.sb("w1st", [128, 2, NCH, 128], BF16)

        e2_base = p.val("e2")
        ve_apply = {}
        ve_rstt = {}
        ve_hz = {}
        pe_rh = {}
        st_xnt = {}
        e2_u = {}

        def pt_use(u):
            return PT[u % 2], e2_base + u - 1

        for nt in range(NT):
            @blk.sync
            def _(sync, nt=nt):
                sync.wait_ge(p.sems["st"], st_x5)
                if nt >= 2:
                    sync.wait_ge(p.sems["ve"], ve_apply[nt - 2])
                else:
                    sync.wait_ge(p.sems["ve"], ve_r2)
                ins = sync.dma_start(out=xio[:, nt % 2, :], in_=x_o[:, nt, :])
                ld_x[("p3", nt)] = p.inc(ins, "ld", 16)

            @blk.vector
            def _(v, nt=nt):
                v.wait_ge(p.sems["ld"], ld_x[("p3", nt)])
                v.wait_ge(p.sems["ve"], ve_r2)
                if nt >= 2:
                    v.wait_ge(p.sems["e2"], e2_u[(nt - 2, 2)])
                ins = v.tensor_scalar(out=stage[:, nt % 2, :], in0=xio[:, nt % 2, :],
                                      scalar1=stats[:, 32 + nt:33 + nt],
                                      scalar2=stats[:, 48 + nt:49 + nt],
                                      op0=ALU.subtract, op1=ALU.mult)
                ve_apply[nt] = p.inc(ins, "ve")

            for g in range(2):
                u = 3 * nt + g
                bank, free_e2 = pt_use(u)

                @blk.tensor
                def _(t, nt=nt, g=g, u=u, bank=bank, free_e2=free_e2):
                    t.wait_ge(p.sems["ve"], ve_apply[nt])
                    if u >= 2:
                        t.wait_ge(p.sems["e2"], free_e2)
                    last = None
                    for j in range(4):
                        last = t.transpose(bank[:, j * 128:(j + 1) * 128],
                                           stage[:, nt % 2, (4 * g + j) * 128:
                                                 (4 * g + j + 1) * 128],
                                           id_sb[:, :])
                    p.inc(last, "pe")

                pe_tr = p.val("pe")

                @blk.vector
                def _(v, nt=nt, g=g, bank=bank, pe_tr=pe_tr):
                    v.wait_ge(p.sems["pe"], pe_tr)
                    if nt >= 2:
                        v.wait_ge(p.sems["pe"], pe_rh[nt - 2])
                        v.wait_ge(p.sems["st"], st_xnt[(nt - 2, g)])
                    ins = v.tensor_copy(out=xnt2[:, nt % 2, 4 * g:4 * g + 4, :],
                                        in_=bank[:, :])
                    e2_u[(nt, g)] = p.inc(ins, "e2")

                @blk.sync
                def _(sync, nt=nt, g=g):
                    sync.wait_ge(p.sems["e2"], e2_u[(nt, g)])
                    ins = sync.dma_start(
                        out=xnT_o[:, 4 * g:4 * g + 4, nt * 128:(nt + 1) * 128],
                        in_=xnt2[:, nt % 2, 4 * g:4 * g + 4, :])
                    st_xnt[(nt, g)] = p.inc(ins, "st", 16)

            @blk.tensor
            def _(t, nt=nt):
                t.wait_ge(p.sems["e2"], e2_u[(nt, 1)])
                t.wait_ge(p.sems["ve"], ve_rw)
                if nt == 0:
                    t.wait_ge(p.sems["ve"], ve_bc)
                if nt >= 1:
                    t.wait_ge(p.sems["ve"], ve_rstt[nt - 1])
                last = None
                for ci in range(NCH):
                    last = t.matmul(PR[:, :], xnt2[:, nt % 2, ci, :], rw_sb[:, ci, :],
                                    start=(ci == 0), stop=(ci == NCH - 1))
                p.inc(last, "pe")
                if nt >= 1:
                    t.wait_ge(p.sems["ve"], ve_hz[nt - 1])
                last = None
                for ci in range(NCH):
                    last = t.matmul(PH[:, :], xnt2[:, nt % 2, ci, :], ut_sb[:, ci, :],
                                    start=(ci == 0), stop=(ci == NCH - 1))
                pe_rh[nt] = p.inc(last, "pe")

            @blk.vector
            def _(v, nt=nt):
                v.wait_ge(p.sems["pe"], pe_rh[nt] - 1)
                v.wait_ge(p.sems["ve"], ve_bc)
                ins = v.scalar_tensor_tensor(out=rsb[:, :], in0=PR[:, :],
                                             scalar=r2c[:, nt:nt + 1],
                                             in1=brow_bc[:, :],
                                             op0=ALU.mult, op1=ALU.add)
                ve_rstt[nt] = p.inc(ins, "ve")
                v.wait_ge(p.sems["ve"], ve_rstt[nt])
                ins = v.tensor_scalar(out=rsb[:, :], in0=rsb[:, :], scalar1=10.0,
                                      scalar2=-10.0, op0=ALU.min, op1=ALU.max)
                v.wait_ge(p.sems["ve"], p.inc(ins, "ve"))

            ve_clip = p.val("ve")

            @blk.scalar
            def _(s, ve_clip=ve_clip):
                s.wait_ge(p.sems["ve"], ve_clip)
                ins = s.activation(out=alp[:, :], in_=rsb[:, :], func=AF.Exp)
                p.inc(ins, "ac")
                s.wait_ge(p.sems["ac"], p.val("ac"))
                ins = s.activation(out=rsb[:, :], in_=alp[:, :], func=AF.Ln,
                                   bias=_f(1.0), scale=_f(1.0))
                p.inc(ins, "ac")

            ac_alpha = p.val("ac")

            @blk.vector
            def _(v, ac_alpha=ac_alpha):
                v.wait_ge(p.sems["ac"], ac_alpha)
                ins = v.max(out=phi[:, :], in_=rsb[:, :])
                v.wait_ge(p.sems["ve"], p.inc(ins, "ve"))
                ins = v.tensor_reduce(out=svec[:, 0:1], in_=phi[:, :], axis=AX.X,
                                      op=ALU.add)
                v.wait_ge(p.sems["ve"], p.inc(ins, "ve"))

            ve_svec = p.val("ve")

            @blk.scalar
            def _(s, ve_svec=ve_svec):
                s.wait_ge(p.sems["ve"], ve_svec)
                ins = s.activation(out=svec[:, 1:2], in_=svec[:, 0:1], func=AF.Tanh)
                p.inc(ins, "ac")

            ac_tanh = p.val("ac")

            @blk.vector
            def _(v, nt=nt, ac_tanh=ac_tanh):
                v.wait_ge(p.sems["ac"], ac_tanh)
                ins = v.tensor_scalar(out=svec[:, 2:3], in0=svec[:, 0:1],
                                      scalar1=1e-6, scalar2=0.0,
                                      op0=ALU.add, op1=ALU.add)
                v.wait_ge(p.sems["ve"], p.inc(ins, "ve"))
                ins = v.reciprocal(out=svec[:, 2:3], in_=svec[:, 2:3])
                v.wait_ge(p.sems["ve"], p.inc(ins, "ve"))
                ins = v.tensor_tensor(out=svec[:, 1:2], in0=svec[:, 1:2],
                                      in1=svec[:, 2:3], op=ALU.mult)
                v.wait_ge(p.sems["ve"], p.inc(ins, "ve"))
                ins = v.tensor_scalar(out=tg1, in0=rsb[:, :],
                                      scalar1=phi[:, 7:8], scalar2=svec[:, 1:2],
                                      op0=ALU.is_ge, op1=ALU.mult)
                v.wait_ge(p.sems["ve"], p.inc(ins, "ve"))
                ins = v.tensor_tensor(out=tg2, in0=tg1, in1=rsb[:, :], op=ALU.mult)
                v.wait_ge(p.sems["ve"], p.inc(ins, "ve"))
                ins = v.tensor_tensor(out=tg1, in0=tg2, in1=rnuv_bc[:, :],
                                      op=ALU.mult)
                v.wait_ge(p.sems["ve"], p.inc(ins, "ve"))
                ins = v.tensor_tensor(out=stage[:, nt % 2, 0:512], in0=tg1,
                                      in1=PH[:, :], op=ALU.mult)
                ve_hz[nt] = p.inc(ins, "ve")

            u = 3 * nt + 2
            bank, free_e2 = pt_use(u)

            @blk.tensor
            def _(t, nt=nt, u=u, bank=bank, free_e2=free_e2):
                t.wait_ge(p.sems["ve"], ve_hz[nt])
                if u >= 2:
                    t.wait_ge(p.sems["e2"], free_e2)
                last = None
                for j in range(4):
                    last = t.transpose(bank[:, j * 128:(j + 1) * 128],
                                       stage[:, nt % 2, j * 128:(j + 1) * 128],
                                       id_sb[:, :])
                p.inc(last, "pe")

            pe_hztr = p.val("pe")

            @blk.vector
            def _(v, nt=nt, bank=bank, pe_hztr=pe_hztr):
                v.wait_ge(p.sems["pe"], pe_hztr)
                ins = v.tensor_copy(out=hzT[:, 0:4, nt * 128:(nt + 1) * 128],
                                    in_=bank[:, :])
                e2_u[(nt, 2)] = p.inc(ins, "e2")

        st_p3 = p.val("st")
        e2_p3 = p.val("e2")

        pe_hid = {}
        ac_hid = {}
        ve_sq2 = {}
        pe_ntq = {}
        ve_ev = {}
        st_x6 = {}
        ld_xnq = {}
        wd_w1 = {}
        wd_w2 = {}

        for qcb in range(2):
            @blk.sync
            def _(sync, qcb=qcb):
                sync.wait_ge(p.sems["st"], st_p3)
                if qcb == 1:
                    sync.wait_ge(p.sems["pe"], pe_hid[(0, 3, 7)])
                ins = sync.dma_start(out=xnq[:, :, :],
                                     in_=xnT_o[:, :, qcb * 1024:(qcb + 1) * 1024])
                ld_xnq[qcb] = p.inc(ins, "ld", 16)

            for q in range(4):
                Q = 4 * qcb + q

                @blk.gpsimd
                def _(g, qcb=qcb, q=q, Q=Q):
                    if Q == 0:
                        g.wait_ge(p.sems["ve"], ve_vg)
                        g.wait_ge(p.sems["pe"], pe_ph2)
                    if Q >= 2:
                        g.wait_ge(p.sems["pe"], pe_ntq[(Q - 2, 7)])
                    for k in range(8):
                        fh = q * 8 + k
                        ins = g.dma_start(out=w2q[:, q % 2, k, :],
                                          in_=w2_i[:, fh, :])
                        wd_w2[(qcb, q, k)] = p.inc(ins, "wd", 16)

                for k in range(8):
                    @blk.gpsimd
                    def _(g, qcb=qcb, q=q, Q=Q, k=k):
                        fh = q * 8 + k
                        if k < 2:
                            if Q >= 1:
                                prevq = (qcb, q - 1, k + 6) if q >= 1 else \
                                    (qcb - 1, 3, k + 6)
                                g.wait_ge(p.sems["pe"], pe_hid[prevq])
                        else:
                            g.wait_ge(p.sems["pe"], pe_hid[(qcb, q, k - 2)])
                        ins = g.dma_start(out=w1st[:, k % 2, :, :],
                                          in_=w1_i[:, :, fh * 128:(fh + 1) * 128])
                        wd_w1[(qcb, q, k)] = p.inc(ins, "wd", 16)

                    @blk.tensor
                    def _(t, qcb=qcb, q=q, Q=Q, k=k):
                        t.wait_ge(p.sems["wd"], wd_w1[(qcb, q, k)])
                        t.wait_ge(p.sems["ld"], ld_xnq[qcb])
                        if (Q, k) == (0, 0):
                            t.wait_ge(p.sems["ve"], ve_hz[NT - 1])
                        else:
                            prev = (qcb, q, k - 1) if k >= 1 else (
                                (qcb, q - 1, 7) if q >= 1 else (qcb - 1, 3, 7))
                            t.wait_ge(p.sems["ac"], ac_hid[prev])
                        last = None
                        for h2 in range(2):
                            PSel = [PR, PH][h2]
                            for ci in range(NCH):
                                last = t.matmul(PSel[:, :], w1st[:, k % 2, ci, :],
                                                xnq[:, ci, h2 * 512:(h2 + 1) * 512],
                                                start=(ci == 0), stop=(ci == NCH - 1))
                        pe_hid[(qcb, q, k)] = p.inc(last, "pe")

                    @blk.scalar
                    def _(s, qcb=qcb, q=q, k=k):
                        s.wait_ge(p.sems["pe"], pe_hid[(qcb, q, k)])
                        for h2 in range(2):
                            PSel = [PR, PH][h2]
                            ins = s.activation(out=hid[:, k, h2 * 512:(h2 + 1) * 512],
                                               in_=PSel[:, :], func=AF.Gelu)
                            ac_hid[(qcb, q, k)] = p.inc(ins, "ac")

                for ntq in range(8):
                    gtok = qcb * 1024 + ntq * 128

                    if q == 0:
                        @blk.sync
                        def _(sync, qcb=qcb, Q=Q, ntq=ntq):
                            if ntq >= 2:
                                sync.wait_ge(p.sems["ve"], ve_ev[(Q, ntq - 2)])
                            elif qcb == 0:
                                sync.wait_ge(p.sems["ve"], ve_apply[14 + ntq % 2])
                            else:
                                sync.wait_ge(p.sems["st"], st_x6[(0, 6 + ntq % 2)])
                                if want_ln2:
                                    sync.wait_ge(p.sems["ve"], ve_sq2[6 + ntq % 2])
                            ins = sync.dma_start(out=xio[:, ntq % 2, :],
                                                 in_=x_o[:, qcb * 8 + ntq, :])
                            ld_x[("p4", qcb, ntq)] = p.inc(ins, "ld", 16)

                    @blk.tensor
                    def _(t, qcb=qcb, q=q, Q=Q, ntq=ntq, gtok=gtok):
                        t.wait_ge(p.sems["ac"], ac_hid[(qcb, q, 7)])
                        t.wait_ge(p.sems["wd"], wd_w2[(qcb, q, 7)])
                        if ntq >= 2:
                            t.wait_ge(p.sems["ve"], ve_ev[(Q, ntq - 2)])
                        elif Q >= 1:
                            t.wait_ge(p.sems["ve"], ve_ev[(Q - 1, 6 + ntq)])
                        if q == 3 and ntq == 0:
                            t.wait_ge(p.sems["e2"], e2_p3)
                            t.wait_ge(p.sems["ve"], ve_vg)
                        PO = [P01, P02][ntq % 2]
                        last = None
                        for h2 in range(2):
                            for k in range(8):
                                last = t.matmul(
                                    PO[:, h2 * 512:(h2 + 1) * 512],
                                    hid[:, k, ntq * 128:(ntq + 1) * 128],
                                    w2q[:, q % 2, k, h2 * 512:(h2 + 1) * 512],
                                    start=(k == 0),
                                    stop=(k == 7 and q != 3))
                            if q == 3:
                                for jc in range(4):
                                    last = t.matmul(
                                        PO[:, h2 * 512:(h2 + 1) * 512],
                                        hzT[:, jc, gtok:gtok + 128],
                                        vg_sb[:, jc, h2 * 512:(h2 + 1) * 512],
                                        start=False, stop=(jc == 3))
                        pe_ntq[(Q, ntq)] = p.inc(last, "pe")

                    @blk.vector
                    def _(v, qcb=qcb, q=q, Q=Q, ntq=ntq):
                        v.wait_ge(p.sems["pe"], pe_ntq[(Q, ntq)])
                        PO = [P01, P02][ntq % 2]
                        if q == 0:
                            v.wait_ge(p.sems["ld"], ld_x[("p4", qcb, ntq)])
                            ins = v.tensor_tensor(out=acc[:, ntq, :], in0=PO[:, :],
                                                  in1=xio[:, ntq % 2, :], op=ALU.add)
                        elif q in (1, 2):
                            ins = v.tensor_tensor(out=acc[:, ntq, :], in0=PO[:, :],
                                                  in1=acc[:, ntq, :], op=ALU.add)
                        else:
                            if ntq >= 2:
                                v.wait_ge(p.sems["st"], st_x6[(qcb, ntq - 2)])
                            ins = v.tensor_tensor(out=xio[:, ntq % 2, :], in0=PO[:, :],
                                                  in1=acc[:, ntq, :], op=ALU.add)
                        ve_ev[(Q, ntq)] = p.inc(ins, "ve")
                        if q == 3 and want_ln2:
                            gnt = qcb * 8 + ntq
                            v.wait_ge(p.sems["ve"], ve_ev[(Q, ntq)])
                            p.inc(v.tensor_reduce(out=stats[:, gnt:gnt + 1],
                                                  in_=xio[:, ntq % 2, :],
                                                  axis=AX.X, op=ALU.add), "ve")
                            ins = v.tensor_tensor(out=scr[:, :], in0=xio[:, ntq % 2, :],
                                                  in1=xio[:, ntq % 2, :], op=ALU.mult)
                            v.wait_ge(p.sems["ve"], p.inc(ins, "ve"))
                            ins = v.tensor_reduce(out=stats[:, 16 + gnt:17 + gnt],
                                                  in_=scr[:, :], axis=AX.X, op=ALU.add)
                            ve_sq2[gnt] = p.inc(ins, "ve")

                    if q == 3:
                        @blk.sync
                        def _(sync, qcb=qcb, Q=Q, ntq=ntq):
                            sync.wait_ge(p.sems["ve"], ve_ev[(Q, ntq)])
                            ins = sync.dma_start(out=x_o[:, qcb * 8 + ntq, :],
                                                 in_=xio[:, ntq % 2, :])
                            st_x6[(qcb, ntq)] = p.inc(ins, "st", 16)

        if want_ln2:
            st_all6 = p.val("st")

            @blk.vector
            def _(v):
                v.wait_ge(p.sems["ve"], ve_sq2[NT - 1])
                ins = v.tensor_scalar(out=stats[:, 32:48], in0=stats[:, 0:16],
                                      scalar1=1.0 / 1024.0, scalar2=0.0,
                                      op0=ALU.mult, op1=ALU.add)
                v.wait_ge(p.sems["ve"], p.inc(ins, "ve"))
                ins = v.tensor_tensor(out=stats[:, 48:64], in0=stats[:, 32:48],
                                      in1=stats[:, 32:48], op=ALU.mult)
                v.wait_ge(p.sems["ve"], p.inc(ins, "ve"))
                ins = v.tensor_scalar(out=stats[:, 64:80], in0=stats[:, 16:32],
                                      scalar1=1.0 / 1024.0, scalar2=0.0,
                                      op0=ALU.mult, op1=ALU.add)
                v.wait_ge(p.sems["ve"], p.inc(ins, "ve"))
                ins = v.tensor_tensor(out=stats[:, 80:96], in0=stats[:, 64:80],
                                      in1=stats[:, 48:64], op=ALU.subtract)
                p.inc(ins, "ve")

            @blk.scalar
            def _(s):
                s.wait_ge(p.sems["ve"], p.val("ve"))
                ins = s.activation(out=stats[:, 96:112], in_=stats[:, 80:96],
                                   func=AF.Sqrt, bias=_f(1e-5), scale=_f(1.0))
                p.inc(ins, "ac")

            @blk.vector
            def _(v):
                v.wait_ge(p.sems["ac"], p.val("ac"))
                ins = v.reciprocal(out=stats[:, 48:64], in_=stats[:, 96:112])
                p.inc(ins, "ve")

            ve_fin2 = p.val("ve")
            e2_p5 = p.val("e2")
            ve_ap5 = {}
            e2_v = {}
            st_x2 = {}

            for nt in range(NT):
                @blk.sync
                def _(sync, nt=nt):
                    sync.wait_ge(p.sems["st"], st_all6)
                    if nt >= 2:
                        sync.wait_ge(p.sems["ve"], ve_ap5[nt - 2])
                    else:
                        sync.wait_ge(p.sems["ve"], ve_fin2)
                    ins = sync.dma_start(out=xio[:, nt % 2, :], in_=x_o[:, nt, :])
                    ld_x[("p5", nt)] = p.inc(ins, "ld", 16)

                @blk.vector
                def _(v, nt=nt):
                    v.wait_ge(p.sems["ld"], ld_x[("p5", nt)])
                    v.wait_ge(p.sems["ve"], ve_fin2)
                    if nt >= 1:
                        v.wait_ge(p.sems["e2"], e2_v[(nt - 1, 1)])
                    ins = v.tensor_scalar(out=stage[:, nt % 2, :],
                                          in0=xio[:, nt % 2, :],
                                          scalar1=stats[:, 32 + nt:33 + nt],
                                          scalar2=stats[:, 48 + nt:49 + nt],
                                          op0=ALU.subtract, op1=ALU.mult)
                    ve_ap5[nt] = p.inc(ins, "ve")

                for g in range(2):
                    u = 2 * nt + g
                    bank = PT[u % 2]

                    @blk.tensor
                    def _(t, nt=nt, g=g, u=u, bank=bank):
                        t.wait_ge(p.sems["ve"], ve_ap5[nt])
                        if u >= 2:
                            t.wait_ge(p.sems["e2"], e2_p5 + u - 1)
                        last = None
                        for j in range(4):
                            last = t.transpose(bank[:, j * 128:(j + 1) * 128],
                                               stage[:, nt % 2, (4 * g + j) * 128:
                                                     (4 * g + j + 1) * 128],
                                               id_sb[:, :])
                        p.inc(last, "pe")

                    pe_tr5 = p.val("pe")

                    @blk.vector
                    def _(v, nt=nt, g=g, bank=bank, pe_tr5=pe_tr5):
                        v.wait_ge(p.sems["pe"], pe_tr5)
                        if nt >= 1:
                            v.wait_ge(p.sems["st"], st_x2[(nt - 1, g)])
                        ins = v.tensor_copy(out=xnt2[:, nt % 2, 4 * g:4 * g + 4, :],
                                            in_=bank[:, :])
                        e2_v[(nt, g)] = p.inc(ins, "e2")

                    @blk.sync
                    def _(sync, nt=nt, g=g):
                        sync.wait_ge(p.sems["e2"], e2_v[(nt, g)])
                        ins = sync.dma_start(
                            out=xnT2_o[:, 4 * g:4 * g + 4, nt * 128:(nt + 1) * 128],
                            in_=xnt2[:, nt % 2, 4 * g:4 * g + 4, :])
                        st_x2[(nt, g)] = p.inc(ins, "st", 16)

        @blk.sync
        def _(sync):
            sync.wait_ge(p.sems["st"], p.val("st"))

    return p


def prep_PF(I, x_slice, ysl, wmat):
    W1 = np.asarray(I["L1_ffn_W1"], np.float32)
    W2 = np.asarray(I["L1_ffn_W2"], np.float32)
    rw = np.asarray(I["L1_ffn_router_W"], np.float32)
    rb = np.asarray(I["L1_ffn_router_b"], np.float32)
    U = np.asarray(I["L1_ffn_raw_U"], np.float32)
    V = np.asarray(I["L1_ffn_raw_V"], np.float32)
    xs = np.asarray(x_slice, np.float32)
    if xs.ndim == 2:
        xs = _tokmajor(xs)
    return dict(
        xs=xs,
        ysl=ysl,
        wp=_chunked(np.asarray(wmat, np.float32)),
        rwT=_chunked(np.ascontiguousarray(rw.T)),
        rb=np.ascontiguousarray(rb[None, :]),
        lng2=_colvec(I["L1_ffn_ln_g"]),
        lnb2=_colvec(I["L1_ffn_ln_b"]),
        UT=_chunked(np.ascontiguousarray(U.T)),
        VT=_chunked(np.ascontiguousarray(V.T)),
        Vr=np.ascontiguousarray(V.reshape(4, 128, D).transpose(1, 0, 2)),
        gam=np.ascontiguousarray(np.asarray(I["L1_ffn_gamma"], np.float32).reshape(1, D)),
        w1T=_chunked(np.ascontiguousarray(W1.T)),
        w2T=np.ascontiguousarray(W2.T.reshape(32, 128, D).transpose(1, 0, 2)),
        idm=_ident_bf(),
    )



def build_MLA():
    M = 4.0
    p = Prog()
    nc = p.nc
    xnT_i = p.dram_in("xnT", [128, NCH, SEQ], BF16)
    wdq_i = p.dram_in("wdq", [128, NCH, 256])
    wdk_i = p.dram_in("wdk", [128, NCH, 256])
    wdv_i = p.dram_in("wdv", [128, NCH, 256])
    wup_i = p.dram_in("wup", [128, 192])
    rt_i = p.dram_in("rt", [32, 32], BF16)
    selp_i = p.dram_in("selp", [32, 4, 128], BF16)
    cos_i = p.dram_in("cosT", [32, SEQ], BF16)
    sin_i = p.dram_in("sinT", [32, SEQ], BF16)
    msk_i = p.dram_in("msk", [128, 4, 1024], BF16)
    y_o = p.dram_out("y", [128, 2, SEQ], BF16)

    qsb = p.sb("qsb", [128, 2, SEQ], BF16)
    ksb = p.sb("ksb", [128, 2, SEQ], BF16)
    vsb = p.sb("vsb", [128, NKT, 256], BF16)
    ysb = p.sb("ysb", [128, 2, SEQ], BF16)
    xnc = p.sb("xnc", [128, 2, NCH, 512], BF16)
    wdq_sb = p.sb("wdqs", [128, NCH, 256], BF16)
    wdk_sb = p.sb("wdks", [128, NCH, 256], BF16)
    wdv_sb = p.sb("wdvs", [128, NCH, 256], BF16)
    wup_sb = p.sb("wups", [128, 192], BF16)
    rt_sb = p.sb("rts", [32, 32], BF16)
    selp_sb = p.sb("selps", [32, 4, 128], BF16)
    cosc = p.sb("cosc", [32, 2, 512], BF16)
    sinc = p.sb("sinc", [32, 2, 512], BF16)
    lqf = p.sb("lqf", [128, 512], BF16)
    lkf = p.sb("lkf", [128, 512], BF16)
    lvf = p.sb("lvf", [128, 512], BF16)
    cf = p.sb("cf", [32, 512], BF16)
    ef = p.sb("ef", [32, 512], BF16)
    erf = p.sb("erf", [32, 512], BF16)
    ropef = p.sb("ropef", [32, 512], BF16)
    msk_sb = p.sb("msks", [128, 4, 1024], BF16)
    attT = p.sb("attT", [128, 2, 2048], BF16)
    avf = p.sb("avf", [128, 512])
    rcp = p.sb("rcp", [1, 2, 512])
    ones_c = p.sb("ones_c", [128, 1], BF16)
    ones64 = p.sb("ones64", [64, 64])

    SCall = p.ps("SCall", [128, 2048])
    LA0 = _ColView(SCall, 0)
    LA1 = _ColView(SCall, 512)
    TC = _ColView(SCall, 1024)
    TE = _ColView(SCall, 1536)
    TR = p.ps("S4", [128, 512])
    AS = p.ps("S5", [128, 512])
    PV0 = p.ps("S6", [128, 512])
    PV1 = p.ps("S7", [128, 512])

    with nc.Block() as blk:
        @blk.gpsimd
        def _(g):
            for dst, srci in [(wdq_sb, wdq_i), (wdk_sb, wdk_i), (wdv_sb, wdv_i)]:
                g.dma_start(out=dst[:, :, :], in_=srci[:, :, :]).then_inc(p.sems["wd"], 16)
                p.cnt["wd"] += 16
            g.dma_start(out=wup_sb[:, :], in_=wup_i[:, :]).then_inc(p.sems["wd"], 16)
            p.cnt["wd"] += 16
            g.memset(ones_c[:, :], 1.0)
            ins = g.memset(ones64[:, :], 1.0)
            p.inc(ins, "wd", 1)

        wd0 = p.val("wd")

        @blk.sync
        def _(sync):
            sync.dma_start(out=msk_sb[:, :, :], in_=msk_i[:, :, :]).then_inc(p.sems["ld"], 16)
            sync.dma_start(out=rt_sb[:, :], in_=rt_i[:, :]).then_inc(p.sems["ld"], 16)
            ins = sync.dma_start(out=selp_sb[:, :, :], in_=selp_i[:, :, :])
            p.inc(ins, "ld", 16)
            p.cnt["ld"] += 32

        ld0 = p.val("ld")

        ld_tc = {}
        pe_lat_last = {}
        ve_rope_last = {}
        ve_done = {}

        for tc in range(16):
            @blk.sync
            def _(sync, tc=tc):
                if tc >= 2:
                    sync.wait_ge(p.sems["pe"], pe_lat_last[(tc - 2, 1)])
                    sync.wait_ge(p.sems["ve"], ve_rope_last[(tc - 2, 1)])
                sync.dma_start(out=xnc[:, tc % 2, :, :],
                               in_=xnT_i[:, :, tc * 512:(tc + 1) * 512]
                               ).then_inc(p.sems["ld"], 16)
                sync.dma_start(out=cosc[:, tc % 2, :],
                               in_=cos_i[:, tc * 512:(tc + 1) * 512]
                               ).then_inc(p.sems["ld"], 16)
                ins = sync.dma_start(out=sinc[:, tc % 2, :],
                                     in_=sin_i[:, tc * 512:(tc + 1) * 512])
                p.inc(ins, "ld", 16)
                p.cnt["ld"] += 32
                ld_tc[tc] = p.val("ld")

            for pr in range(2):
                ve_lat = {}
                for li, (wsb, dst) in enumerate([(wdq_sb, lqf), (wdk_sb, lkf),
                                                 (wdv_sb, lvf)]):
                    @blk.tensor
                    def _(t, tc=tc, pr=pr, li=li, wsb=wsb):
                        t.wait_ge(p.sems["ld"], ld_tc[tc])
                        t.wait_ge(p.sems["wd"], wd0)
                        if li >= 1:
                            t.wait_ge(p.sems["ve"], ve_lat[li - 1])
                        elif (tc, pr) != (0, 0):
                            prev = (tc, pr - 1) if pr == 1 else (tc - 1, 1)
                            t.wait_ge(p.sems["ve"], ve_done[prev])
                        last = None
                        for pos in range(2):
                            hl = 2 * pr + pos
                            PA = [LA0, LA1][pos]
                            for ci in range(NCH):
                                last = t.matmul(PA[pos * 64:pos * 64 + 64, :],
                                                wsb[:, ci, hl * 64:(hl + 1) * 64],
                                                xnc[:, tc % 2, ci, :],
                                                start=(ci == 0), stop=(ci == NCH - 1),
                                                tile_position=(0, pos * 64))
                        p.inc(last, "pe")
                        if li == 2:
                            pe_lat_last[(tc, pr)] = p.val("pe")

                    pe_lat = p.val("pe")

                    @blk.vector
                    def _(v, li=li, dst=dst, pe_lat=pe_lat):
                        v.wait_ge(p.sems["pe"], pe_lat)
                        ins = v.tensor_copy(out=dst[0:64, :], in_=LA0[0:64, :])
                        v.wait_ge(p.sems["ve"], p.inc(ins, "ve"))
                        ins = v.tensor_copy(out=dst[64:128, :], in_=LA1[64:128, :])
                        ve_lat[li] = p.inc(ins, "ve")

                for proj, (lf, co, ce, dst) in enumerate(
                        [(lqf, 0, 32, qsb), (lkf, 64, 96, ksb)]):
                    for e in range(2):
                        @blk.tensor
                        def _(t, tc=tc, pr=pr, proj=proj, e=e, lf=lf, co=co, ce=ce):
                            t.wait_ge(p.sems["ve"], ve_lat[2] if proj == 0 and e == 0
                                      else p.val("ve"))
                            b = e * 64
                            tp = (b, 0)
                            t.matmul(TC[0:32, :], wup_sb[b:b + 64, co:co + 32],
                                     lf[b:b + 64, :], start=True, stop=True,
                                     tile_position=tp)
                            ins = t.matmul(TE[0:32, :], wup_sb[b:b + 64, ce:ce + 32],
                                           lf[b:b + 64, :], start=True, stop=True,
                                           tile_position=tp)
                            p.inc(ins, "pe")

                        pe_ce = p.val("pe")

                        @blk.vector
                        def _(v, pe_ce=pe_ce):
                            v.wait_ge(p.sems["pe"], pe_ce)
                            p.inc(v.tensor_copy(out=cf[:, :], in_=TC[0:32, :]), "ve")
                            ins = v.tensor_copy(out=ef[:, :], in_=TE[0:32, :])
                            v.wait_ge(p.sems["ve"], p.inc(ins, "ve"))

                        ve_ce = p.val("ve")

                        @blk.tensor
                        def _(t, ve_ce=ve_ce):
                            t.wait_ge(p.sems["ld"], ld0)
                            t.wait_ge(p.sems["ve"], ve_ce)
                            ins = t.matmul(TR[0:32, :], rt_sb[0:32, :], ef[0:32, :],
                                           start=True, stop=True, tile_position=(0, 0))
                            p.inc(ins, "pe")

                        pe_rot = p.val("pe")

                        @blk.vector
                        def _(v, tc=tc, pe_rot=pe_rot):
                            v.wait_ge(p.sems["pe"], pe_rot)
                            ins = v.tensor_copy(out=erf[:, :], in_=TR[0:32, :])
                            v.wait_ge(p.sems["ve"], p.inc(ins, "ve"))
                            ins = v.tensor_tensor(out=ef[:, :], in0=ef[:, :],
                                                  in1=cosc[:, tc % 2, :], op=ALU.mult)
                            v.wait_ge(p.sems["ve"], p.inc(ins, "ve"))
                            ins = v.tensor_tensor(out=erf[:, :], in0=erf[:, :],
                                                  in1=sinc[:, tc % 2, :], op=ALU.mult)
                            v.wait_ge(p.sems["ve"], p.inc(ins, "ve"))
                            ins = v.tensor_tensor(out=ropef[:, :], in0=ef[:, :],
                                                  in1=erf[:, :], op=ALU.add)
                            v.wait_ge(p.sems["ve"], p.inc(ins, "ve"))

                        ve_rope = p.val("ve")
                        if proj == 1 and e == 1:
                            ve_rope_last[(tc, pr)] = ve_rope

                        @blk.tensor
                        def _(t, e=e, proj=proj, ve_rope=ve_rope):
                            t.wait_ge(p.sems["ve"], ve_rope)
                            t.matmul(AS[:, :], selp_sb[0:32, 2 * e, :], cf[0:32, :],
                                     start=(e == 0), stop=False, tile_position=(0, 0))
                            ins = t.matmul(AS[:, :], selp_sb[0:32, 2 * e + 1, :],
                                           ropef[0:32, :],
                                           start=False, stop=(e == 1),
                                           tile_position=(0, 0))
                            p.inc(ins, "pe")

                    pe_as = p.val("pe")

                    @blk.vector
                    def _(v, tc=tc, pr=pr, dst=dst, pe_as=pe_as):
                        v.wait_ge(p.sems["pe"], pe_as)
                        ins = v.tensor_copy(out=dst[:, pr, tc * 512:(tc + 1) * 512],
                                            in_=AS[:, :])
                        v.wait_ge(p.sems["ve"], p.inc(ins, "ve"))

                ve_v = {}
                for kt in range(4):
                    @blk.tensor
                    def _(t, tc=tc, pr=pr, kt=kt):
                        if kt == 0:
                            t.wait_ge(p.sems["ve"], ve_lat[2])
                        else:
                            t.wait_ge(p.sems["ve"], ve_v[kt - 1])
                        t.matmul(PV0[:, 0:64], lvf[0:64, kt * 128:(kt + 1) * 128],
                                 wup_sb[0:64, 128:192], start=True, stop=True,
                                 tile_position=(0, 0))
                        ins = t.matmul(PV1[:, 0:64], lvf[64:128, kt * 128:(kt + 1) * 128],
                                       wup_sb[64:128, 128:192], start=True, stop=True,
                                       tile_position=(64, 0))
                        p.inc(ins, "pe")

                    pe_vu = p.val("pe")

                    @blk.vector
                    def _(v, tc=tc, pr=pr, kt=kt, pe_vu=pe_vu):
                        v.wait_ge(p.sems["pe"], pe_vu)
                        gkt = tc * 4 + kt
                        p.inc(v.tensor_copy(out=vsb[:, gkt, 2 * pr * 64:2 * pr * 64 + 64],
                                            in_=PV0[:, 0:64]), "ve")
                        ins = v.tensor_copy(out=vsb[:, gkt, (2 * pr + 1) * 64:
                                                (2 * pr + 2) * 64],
                                            in_=PV1[:, 0:64])
                        ve_v[kt] = p.inc(ins, "ve")

                ve_done[(tc, pr)] = p.val("ve")

        phaseA_done = p.val("ve")

        emit_phaseB(p, blk, True, M, qsb, ksb, vsb, msk_sb, attT, avf, rcp,
                    ysb, ones_c, ones64, SCall, TR, AS, PV0, PV1,
                    "ve", phaseA_done, ld0)

        yB_done = p.val("e2")

        @blk.sync
        def _(sync):
            sync.wait_ge(p.sems["e2"], yB_done)
            ins = sync.dma_start(out=y_o[:, :, :], in_=ysb[:, :, :])
            p.inc(ins, "st", 16)
            sync.wait_ge(p.sems["st"], p.val("st"))
    return p


def prep_MLA(I, core, xnT_full):
    m = core % 4
    BFc = BF
    wdq = np.asarray(I["L0_attn_b_Wq_down"], np.float32)
    wdk = np.asarray(I["L0_attn_b_Wk_down"], np.float32)
    wdv = np.asarray(I["L0_attn_b_Wv_down"], np.float32)
    cols = slice(m * 256, (m + 1) * 256)
    wup = np.zeros((128, 192), np.float32)
    for r in (0, 64):
        wup[r:r + 64, 0:32] = np.asarray(I["L0_attn_b_Wq_up_c"], np.float32)
        wup[r:r + 64, 32:64] = np.asarray(I["L0_attn_b_Wq_up_e"], np.float32)
        wup[r:r + 64, 64:96] = np.asarray(I["L0_attn_b_Wk_up_c"], np.float32)
        wup[r:r + 64, 96:128] = np.asarray(I["L0_attn_b_Wk_up_e"], np.float32)
        wup[r:r + 64, 128:192] = np.asarray(I["L0_attn_b_Wv_up"], np.float32)
    R = np.zeros((32, 32), np.float32)
    for j in range(0, 32, 2):
        R[j, j + 1] = -1.0
        R[j + 1, j] = 1.0
    selp = np.zeros((32, 4, 128), np.float32)
    for s in range(4):
        for k in range(32):
            selp[k, s, 32 * s + k] = 1.0
    inv_freq = 1.0 / (10000.0 ** (np.arange(0, 32, 2, dtype=np.float32) / np.float32(32.0)))
    t_idx = np.arange(SEQ, dtype=np.float32)
    ang = t_idx[None, :] * inv_freq[:, None]
    ang = np.concatenate([ang, ang], 0)
    return dict(
        xnT=xnT_full,
        wdq=_chunked(np.ascontiguousarray(wdq[:, cols])),
        wdk=_chunked(np.ascontiguousarray(wdk[:, cols])),
        wdv=_chunked(np.ascontiguousarray(wdv[:, cols])),
        wup=wup,
        rt=np.ascontiguousarray(R.T).astype(BFc),
        selp=selp.astype(BFc),
        cosT=np.cos(ang).astype(BFc),
        sinT=np.sin(ang).astype(BFc),
        msk=_masks_bf(),
    )


_BUILD_CACHE = {}


def _get(name, builder):
    if name not in _BUILD_CACHE:
        _BUILD_CACHE[name] = builder()
    return _BUILD_CACHE[name]


def _bf(a):
    a = np.asarray(a)
    if a.dtype == BF:
        return a
    return a.view(np.uint16).view(BF)


def kernel(**inputs):
    I = inputs

    def run(p, in_maps):
        return run_bass_kernel_spmd(p.nc, in_maps, core_ids=list(range(8)))

    def parts_for(ps):
        out = []
        for b in range(2):
            pb = ps[4 * b:4 * b + 4]
            out.append(np.ascontiguousarray(
                pb.reshape(4, NCH, 128).transpose(2, 0, 1).reshape(128, 32)))
        return out

    def xnT_full(xns):
        return [np.ascontiguousarray(np.concatenate(
            [xns[4 * b + j] for j in range(4)], axis=2)) for b in range(2)]

    def ysls(r):
        yf = [_yfeat([_bf(r.results[4 * b + j]["y"]) for j in range(4)])
              for b in range(2)]
        return [_ysl_from_yfeat(yf[c // 4], c % 4) for c in range(8)]

    p1 = _get("L1", build_L1)
    r1 = run(p1, [prep_L1(I, c) for c in range(8)])
    pst = np.stack([np.asarray(r1.results[c]["pstates"], np.float32)
                    for c in range(8)])

    p2 = _get("L2", build_L2)
    maps = []
    for c in range(8):
        b = c // 4
        pstb = np.ascontiguousarray(pst[4 * b:4 * b + 4].transpose(1, 0, 2, 3))
        maps.append(prep_L2(I, c, np.ascontiguousarray(_bf(r1.results[c]["qn"])),
                            pstb))
    r2 = run(p2, maps)
    x1 = [np.asarray(r2.results[c]["x1"], np.float32) for c in range(8)]
    ps1 = np.stack([np.asarray(r2.results[c]["psums"], np.float32)[0]
                    for c in range(8)])

    p3 = _get("FFN0", build_FFN0)
    pw = parts_for(ps1)
    r3 = run(p3, [prep_FFN0(I, x1[c], pw[c // 4]) for c in range(8)])
    x2 = [np.asarray(r3.results[c]["x_out"], np.float32) for c in range(8)]
    xnTs = [_bf(r3.results[c]["xnT_out"]) for c in range(8)]

    p4 = _get("MLA", build_MLA)
    xf = xnT_full(xnTs)
    r4 = run(p4, [prep_MLA(I, c, xf[c // 4]) for c in range(8)])

    p5 = _get("PROJ", build_PROJ)
    ys = ysls(r4)
    wc_ch = _chunked(np.asarray(I["L0_attn_b_Wc"], np.float32))
    r5 = run(p5, [dict(xs=x2[c], ysl=ys[c], wp=wc_ch) for c in range(8)])
    x3 = [np.asarray(r5.results[c]["x_out"], np.float32) for c in range(8)]
    ps3 = np.stack([np.asarray(r5.results[c]["psums"], np.float32)[0]
                    for c in range(8)])

    pw = parts_for(ps3)
    r6 = run(p3, [prep_FFN0(I, x3[c], pw[c // 4]) for c in range(8)])
    x4 = [np.asarray(r6.results[c]["x_out"], np.float32) for c in range(8)]
    xnTs = [_bf(r6.results[c]["xnT_out"]) for c in range(8)]

    p7 = _get("ATTnc", lambda: build_ATT(False, 12.0))
    xf = xnT_full(xnTs)
    r7 = run(p7, [prep_ATT_L1aa(I, c, xf[c // 4]) for c in range(8)])

    p8 = _get("PF1", lambda: build_PF(True))
    ys = ysls(r7)
    wo = np.asarray(I["L1_attn_a_w_out"], np.float32)
    r8 = run(p8, [prep_PF(I, x4[c], ys[c], wo) for c in range(8)])
    x6 = [np.asarray(r8.results[c]["x_out"], np.float32) for c in range(8)]
    xnTs = [_bf(r8.results[c]["xnT2_out"]) for c in range(8)]

    p9 = _get("ATTc", lambda: build_ATT(True, 16.0))
    xf = xnT_full(xnTs)
    r9 = run(p9, [prep_ATT_L1ab(I, c, xf[c // 4]) for c in range(8)])

    p10 = _get("PF0", lambda: build_PF(False))
    ys = ysls(r9)
    wpt = np.ascontiguousarray(np.asarray(I["L1_attn_b_w_proj"], np.float32).T)
    r10 = run(p10, [prep_PF(I, x6[c], ys[c], wpt) for c in range(8)])

    out = np.empty((2, 8192, 1024), np.float32)
    for c in range(8):
        b, m = c // 4, c % 4
        out[b, m * TOK:(m + 1) * TOK, :] = _untokmajor(
            np.asarray(r10.results[c]["x_out"], np.float32))
    return out


# version 05 (final)
# speedup vs baseline: 1.1943x; 1.1943x over previous
import numpy as np
import ml_dtypes
from contextlib import ExitStack

import concourse.bass as bass
import concourse.mybir as mybir
from concourse.bass_utils import run_bass_kernel_spmd

F32 = mybir.dt.float32
BF16 = mybir.dt.bfloat16
AF = mybir.ActivationFunctionType
ALU = mybir.AluOpType
AX = mybir.AxisListType
BF = ml_dtypes.bfloat16

NCORE = 8
TOK = 2048
NT = TOK // 128
D = 1024
NCH = D // 128
SEQ = 8192
NKT = SEQ // 128
H = 16
DH = 64


def _f(x):
    return np.float32(x)


class Prog:

    def __init__(self):
        self.nc = bass.Bass("TRN2", target_bir_lowering=False)
        self.es = ExitStack()
        self.sems = {}
        self.cnt = {}
        for s in ["ld", "wd", "pe", "ve", "ac", "gp", "st", "ev", "e2"]:
            self.sem(s)

    def dram_in(self, name, shape, dt=F32):
        return self.nc.dram_tensor(name, list(shape), dt, kind="ExternalInput").ap()

    def dram_out(self, name, shape, dt=F32):
        return self.nc.dram_tensor(name, list(shape), dt, kind="ExternalOutput").ap()

    def sb(self, name, shape, dt=F32):
        return self.es.enter_context(self.nc.sbuf_tensor(name, list(shape), dt))

    def sb_scoped(self, name, shape, dt=F32):
        cm = self.nc.sbuf_tensor(name, list(shape), dt)
        return cm.__enter__(), cm

    def ps(self, name, shape, dt=F32):
        return self.es.enter_context(self.nc.psum_tensor(name, list(shape), dt))

    def sem(self, name):
        s = self.es.enter_context(self.nc.semaphore(name))
        self.sems[name] = s
        self.cnt[name] = 0
        return s

    def inc(self, ins, name, k=1):
        ins.then_inc(self.sems[name], k)
        self.cnt[name] += k
        return self.cnt[name]

    def val(self, name):
        return self.cnt[name]



def _chunked(w):
    F = w.shape[1]
    return np.ascontiguousarray(w.reshape(NCH, 128, F).transpose(1, 0, 2))


def _colvec(b):
    return np.ascontiguousarray(np.asarray(b, np.float32).reshape(NCH, 128).T)


def _tokmajor(x):
    return np.ascontiguousarray(np.asarray(x, np.float32).reshape(NT, 128, D).transpose(1, 0, 2))


def _untokmajor(x):
    return np.ascontiguousarray(np.asarray(x).transpose(1, 0, 2).reshape(TOK, D))


def _ident_bf():
    return np.eye(128, dtype=np.float32).astype(BF)



def emit_ln(p, blk, x_sb, xn_sb, wait_sem, wait_val, done_sem, stats, scr,
            xf_sb=None, nt_list=None):
    if nt_list is None:
        nt_list = list(range(NT))

    @blk.vector
    def _(v):
        v.wait_ge(p.sems[wait_sem], wait_val)
        for nt in range(NT):
            p.inc(v.tensor_reduce(out=stats[:, nt:nt + 1], in_=x_sb[:, nt, :],
                                  axis=AX.X, op=ALU.add), "ve")
            ins = v.tensor_tensor(out=scr[:, :], in0=x_sb[:, nt, :],
                                  in1=x_sb[:, nt, :], op=ALU.mult)
            v.wait_ge(p.sems["ve"], p.inc(ins, "ve"))
            p.inc(v.tensor_reduce(out=stats[:, 16 + nt:17 + nt], in_=scr[:, :],
                                  axis=AX.X, op=ALU.add), "ve")
        ins = v.tensor_scalar(out=stats[:, 32:48], in0=stats[:, 0:16],
                              scalar1=1.0 / 1024.0, scalar2=0.0,
                              op0=ALU.mult, op1=ALU.add)
        v.wait_ge(p.sems["ve"], p.inc(ins, "ve"))
        ins = v.tensor_tensor(out=stats[:, 48:64], in0=stats[:, 32:48],
                              in1=stats[:, 32:48], op=ALU.mult)
        v.wait_ge(p.sems["ve"], p.inc(ins, "ve"))
        ins = v.tensor_scalar(out=stats[:, 64:80], in0=stats[:, 16:32],
                              scalar1=1.0 / 1024.0, scalar2=0.0,
                              op0=ALU.mult, op1=ALU.add)
        v.wait_ge(p.sems["ve"], p.inc(ins, "ve"))
        ins = v.tensor_tensor(out=stats[:, 80:96], in0=stats[:, 64:80],
                              in1=stats[:, 48:64], op=ALU.subtract)
        p.inc(ins, "ve")

    @blk.scalar
    def _(s):
        s.wait_ge(p.sems["ve"], p.val("ve"))
        ins = s.activation(out=stats[:, 96:112], in_=stats[:, 80:96],
                           func=AF.Sqrt, bias=_f(1e-5), scale=_f(1.0))
        p.inc(ins, "ac")

    @blk.vector
    def _(v):
        v.wait_ge(p.sems["ac"], p.val("ac"))
        ins = v.reciprocal(out=stats[:, 48:64], in_=stats[:, 96:112])
        v.wait_ge(p.sems["ve"], p.inc(ins, "ve"))
        for nt in nt_list:
            if xn_sb is not None:
                p.inc(v.tensor_scalar(out=xn_sb[:, nt, :], in0=x_sb[:, nt, :],
                                      scalar1=stats[:, 32 + nt:33 + nt],
                                      scalar2=stats[:, 48 + nt:49 + nt],
                                      op0=ALU.subtract, op1=ALU.mult), done_sem)
            if xf_sb is not None:
                p.inc(v.tensor_scalar(out=xf_sb[:, nt, :], in0=x_sb[:, nt, :],
                                      scalar1=stats[:, 32 + nt:33 + nt],
                                      scalar2=stats[:, 48 + nt:49 + nt],
                                      op0=ALU.subtract, op1=ALU.mult), done_sem)


def emit_tr(p, blk, groups, src_of, dst_of, pbank, id_sb, wait_sem, wait_val,
            done_sem):
    done_base = p.val(done_sem)

    @blk.tensor
    def _(t):
        t.wait_ge(p.sems[wait_sem], wait_val)
        for gi, key in enumerate(groups):
            bank = pbank[gi % 2]
            if gi >= 2:
                t.wait_ge(p.sems[done_sem], done_base + gi - 1)
            last = None
            for j, src in enumerate(src_of(key)):
                last = t.transpose(bank[:, j * 128:(j + 1) * 128], src, id_sb[:, :])
            p.inc(last, "pe")

    pe_base = p.val("pe") - len(groups)

    @blk.vector
    def _(v):
        for gi, key in enumerate(groups):
            bank = pbank[gi % 2]
            v.wait_ge(p.sems["pe"], pe_base + gi + 1)
            out_ap, in_ap = dst_of(key, bank)
            ins = v.tensor_copy(out=out_ap, in_=in_ap)
            p.inc(ins, done_sem)


def emit_tm2fm(p, blk, src_tm, dst_fm, pbank, id_sb, wait_sem, wait_val, done_sem):
    groups = [(nt, cj) for nt in range(NT) for cj in (0, 4)]

    def src_of(key):
        nt, cj = key
        return [src_tm[:, nt, (cj + j) * 128:(cj + j + 1) * 128] for j in range(4)]

    def dst_of(key, bank):
        nt, cj = key
        return (dst_fm[:, cj:cj + 4, nt * 128:(nt + 1) * 128], bank[:, :])

    emit_tr(p, blk, groups, src_of, dst_of, pbank, id_sb, wait_sem, wait_val, done_sem)


def emit_fm2tm(p, blk, src_fm, dst_tm, pbank, id_sb, wait_sem, wait_val, done_sem):
    groups = [(nt, cj) for nt in range(NT) for cj in (0, 4)]

    def src_of(key):
        nt, cj = key
        return [src_fm[:, cj + j, nt * 128:(nt + 1) * 128] for j in range(4)]

    def dst_of(key, bank):
        nt, cj = key
        return (dst_tm[:, nt, cj * 128:(cj + 4) * 128], bank[:, :])

    emit_tr(p, blk, groups, src_of, dst_of, pbank, id_sb, wait_sem, wait_val, done_sem)



def build_L1():
    p = Prog()
    nc = p.nc
    xs = p.dram_in("xs", [128, NT, D])
    wqT = p.dram_in("wqT", [128, NCH, D])
    wkT = p.dram_in("wkT", [2, 128, NCH, D])
    wvT = p.dram_in("wvT", [2, 128, NCH, D])
    bq_i = p.dram_in("bq", [128, NCH])
    bkv_i = p.dram_in("bkv", [1, 4, D])
    sel_i = p.dram_in("sel", [128, 2])
    sel2_i = p.dram_in("sel2", [2, 128])
    idm = p.dram_in("idm", [128, 128], BF16)
    qn_o = p.dram_out("qn", [128, NCH, TOK], BF16)
    st_o = p.dram_out("pstates", [128, 2, 512])
    xn_o = p.dram_out("xn_dbg", [128, NT, D], BF16)
    xnT_o = p.dram_out("xnT_dbg", [128, NCH, TOK], BF16)

    xn_sb = p.sb("xn", [128, NT, D], BF16)
    xnT = p.sb("xnT", [128, NCH, TOK], BF16)
    qst = p.sb("qst", [128, 2, 512], BF16)
    stats = p.sb("stats", [128, 128])
    scr = p.sb("scr", [128, D])
    id_sb = p.sb("id", [128, 128], BF16)
    wq_sb = p.sb("wq", [128, NCH, D], BF16)
    bq_sb = p.sb("bqs", [128, NCH])
    bkv_sb = p.sb("bkvs", [1, 4, D], BF16)
    t1 = p.sb("t1", [128, 512])
    rn2 = p.sb("rn2", [2, 512])
    kv0 = p.sb("kv0", [128, D])
    kv1 = p.sb("kv1", [128, D])
    kvb = p.sb("kvb", [128, D], BF16)
    kvv = p.sb("kvv", [128, D], BF16)
    nrm = p.sb("nrm", [128, 16])
    sts = p.sb("sts", [128, 2, 512])
    ones_b = p.sb("ones_b", [128, 8], BF16)
    ones_r = p.sb("ones_r", [1, 128], BF16)
    sel_f = p.sb("sel_f", [128, 2])
    sel2_f = p.sb("sel2_f", [2, 128])
    t0 = scr[:, 0:512]
    t2 = scr[:, 512:1024]
    x_sb, x_cm = p.sb_scoped("x", [128, NT, D])

    P0 = p.ps("P0", [128, D])
    P1 = p.ps("P1", [128, D])
    P2 = p.ps("P2", [128, 512])
    P3 = p.ps("P3", [128, 512])
    pt_ctx = [nc.psum_tensor("PTa", [128, 512], BF16), nc.psum_tensor("PTb", [128, 512], BF16)]
    PT = [c.__enter__() for c in pt_ctx]

    with nc.Block() as blk:
        @blk.sync
        def _(sync):
            sync.dma_start(out=x_sb[:, :, :], in_=xs[:, :, :]).then_inc(p.sems["ld"], 16)
            p.cnt["ld"] += 16
            ins = sync.dma_start(out=id_sb[:, :], in_=idm[:, :])
            p.inc(ins, "ld", 16)

        @blk.gpsimd
        def _(g):
            g.dma_start(out=wq_sb[:, :, :], in_=wqT[:, :, :]).then_inc(p.sems["wd"], 16)
            g.dma_start(out=bq_sb[:, :], in_=bq_i[:, :]).then_inc(p.sems["wd"], 16)
            g.dma_start(out=bkv_sb[:, :, :], in_=bkv_i[:, :, :]).then_inc(p.sems["wd"], 16)
            p.cnt["wd"] += 48
            g.dma_start(out=sel_f[:, :], in_=sel_i[:, :]).then_inc(p.sems["wd"], 16)
            g.dma_start(out=sel2_f[:, :], in_=sel2_i[:, :]).then_inc(p.sems["wd"], 16)
            p.cnt["wd"] += 32
            g.memset(ones_b[:, :], 1.0)
            ins = g.memset(ones_r[:, :], 1.0)
            p.inc(ins, "wd", 1)

        emit_ln(p, blk, x_sb, xn_sb, "ld", 32, "ev", stats, scr)
        emit_tm2fm(p, blk, xn_sb, xnT, PT, id_sb, "ev", p.val("ev"), "e2")
        tr_done = p.val("e2")
        for c in reversed(pt_ctx):
            c.__exit__(None, None, None)
        x_cm.__exit__(None, None, None)
        w2_sb = p.sb("w2", [128, 2, NCH, D], BF16)
        P4 = p.ps("P4", [128, 512])
        P5 = p.ps("P5", [128, 512])

        @blk.sync
        def _(sync):
            sync.wait_ge(p.sems["e2"], tr_done)
            sync.dma_start(out=xn_o[:, :, :], in_=xn_sb[:, :, :]).then_inc(p.sems["st"], 16)
            ins = sync.dma_start(out=xnT_o[:, :, :], in_=xnT[:, :, :])
            p.inc(ins, "st", 16)
            p.cnt["st"] += 16

        ev_base = p.val("ev")
        its = [(fj, qc) for fj in range(NCH) for qc in range(4)]
        for it, (fj, qc) in enumerate(its):
            PA = [P2, P3][it % 2]

            @blk.tensor
            def _(t, fj=fj, qc=qc, PA=PA, it=it):
                t.wait_ge(p.sems["wd"], 81)
                t.wait_ge(p.sems["e2"], tr_done)
                if it >= 1:
                    t.wait_ge(p.sems["ev"], ev_base + it)
                last = None
                for ci in range(NCH):
                    last = t.matmul(PA[:, :], wq_sb[:, ci, fj * 128:(fj + 1) * 128],
                                    xnT[:, ci, qc * 512:(qc + 1) * 512],
                                    start=(ci == 0), stop=(ci == NCH - 1))
                p.inc(last, "pe")

            pe1 = p.val("pe")

            @blk.vector
            def _(v, fj=fj, qc=qc, PA=PA, pe1=pe1):
                v.wait_ge(p.sems["pe"], pe1)
                ins = v.scalar_tensor_tensor(out=t0[:, :], in0=PA[:, :],
                                             scalar=bq_sb[:, fj:fj + 1],
                                             in1=xnT[:, fj, qc * 512:(qc + 1) * 512],
                                             op0=ALU.add, op1=ALU.add)
                p.inc(ins, "ve")

            ve1 = p.val("ve")

            @blk.scalar
            def _(s, ve1=ve1):
                s.wait_ge(p.sems["ve"], ve1)
                p.inc(s.activation(out=t1[:, :], in_=t0[:, :], func=AF.Silu), "ac")
                ins = s.activation(out=t2[:, :], in_=t1[:, :], func=AF.Square,
                                   bias=_f(0.5), scale=_f(1.0))
                p.inc(ins, "ac")

            ac1 = p.val("ac")

            @blk.tensor
            def _(t, ac1=ac1):
                t.wait_ge(p.sems["ac"], ac1)
                ins = t.matmul(P4[0:2, :], sel_f[:, :], t2[:, :], start=True, stop=True)
                p.inc(ins, "pe")

            pe2 = p.val("pe")

            @blk.scalar
            def _(s, pe2=pe2):
                s.wait_ge(p.sems["pe"], pe2)
                ins = s.activation(out=rn2[:, :], in_=P4[0:2, :], func=AF.Sqrt)
                p.inc(ins, "ac")

            ac2 = p.val("ac")

            @blk.vector
            def _(v, ac2=ac2):
                v.wait_ge(p.sems["ac"], ac2)
                ins = v.tensor_scalar(out=rn2[:, :], in0=rn2[:, :], scalar1=1e-6,
                                      scalar2=0.0, op0=ALU.add, op1=ALU.add)
                v.wait_ge(p.sems["ve"], p.inc(ins, "ve"))
                ins = v.reciprocal(out=rn2[:, :], in_=rn2[:, :])
                p.inc(ins, "ve")

            ve2 = p.val("ve")

            @blk.tensor
            def _(t, ve2=ve2):
                t.wait_ge(p.sems["ve"], ve2)
                ins = t.matmul(P5[:, :], sel2_f[:, :], rn2[:, :], start=True, stop=True)
                p.inc(ins, "pe")

            pe3 = p.val("pe")

            @blk.vector
            def _(v, fj=fj, qc=qc, pe3=pe3, it=it):
                v.wait_ge(p.sems["pe"], pe3)
                if it >= 2:
                    v.wait_ge(p.sems["st"], 16 * (it - 1))
                ins = v.scalar_tensor_tensor(out=qst[:, it % 2, :],
                                             in0=t1[:, :], scalar=0.5, in1=P5[:, :],
                                             op0=ALU.add, op1=ALU.mult)
                p.inc(ins, "ev")

            ev_it = p.val("ev")

            @blk.sync
            def _(sync, fj=fj, qc=qc, it=it, ev_it=ev_it):
                sync.wait_ge(p.sems["ev"], ev_it)
                ins = sync.dma_start(out=qn_o[:, fj, qc * 512:(qc + 1) * 512],
                                     in_=qst[:, it % 2, :])
                p.inc(ins, "st", 16)

        q_done = p.val("ev")

        @blk.gpsimd
        def _(g):
            g.wait_ge(p.sems["ev"], q_done)
            g.dma_start(out=w2_sb[:, 0, :, :], in_=wkT[0, :, :, :]).then_inc(p.sems["wd"], 16)
            ins = g.dma_start(out=w2_sb[:, 1, :, :], in_=wvT[0, :, :, :])
            p.inc(ins, "wd", 16)
            p.cnt["wd"] += 16

        PS_A = [P2, P3]
        PS_B = [P4, P5]
        PP = [P0, P1]
        wd2 = p.val("wd")
        ve_p2_base = p.val("ve")
        for k in range(2):
            if k == 1:
                pe_after_k0 = p.val("pe")

                @blk.gpsimd
                def _(g, pe_after_k0=pe_after_k0):
                    g.wait_ge(p.sems["pe"], pe_after_k0)
                    g.dma_start(out=w2_sb[:, 0, :, :], in_=wkT[1, :, :, :]).then_inc(p.sems["wd"], 16)
                    ins = g.dma_start(out=w2_sb[:, 1, :, :], in_=wvT[1, :, :, :])
                    p.inc(ins, "wd", 16)
                    p.cnt["wd"] += 16
            wd_need = wd2 + 32 * k
            for nt in range(NT):
                PA = PP[nt % 2]
                idx = k * NT + nt

                @blk.tensor
                def _(t, k=k, nt=nt, PA=PA, wd_need=wd_need, idx=idx):
                    t.wait_ge(p.sems["wd"], wd_need)
                    if idx >= 1:
                        t.wait_ge(p.sems["ve"], ve_p2_base + 4 * idx)
                    last = None
                    for ci in range(NCH):
                        last = t.matmul(PA[:, 0:512], xnT[:, ci, nt * 128:(nt + 1) * 128],
                                        w2_sb[:, 0, ci, 0:512], start=(ci == 0), stop=False)
                        last = t.matmul(PA[:, 512:1024], xnT[:, ci, nt * 128:(nt + 1) * 128],
                                        w2_sb[:, 0, ci, 512:1024], start=(ci == 0), stop=False)
                    t.matmul(PA[:, 0:512], ones_r[:, :], bkv_sb[:, k, 0:512],
                             start=False, stop=True)
                    last = t.matmul(PA[:, 512:1024], ones_r[:, :], bkv_sb[:, k, 512:1024],
                                    start=False, stop=True)
                    p.inc(last, "pe")

                pe_ks = p.val("pe")

                @blk.vector
                def _(v, nt=nt, PA=PA, pe_ks=pe_ks):
                    v.wait_ge(p.sems["pe"], pe_ks)
                    ins = v.tensor_tensor(out=kv0[:, :], in0=PA[:, :],
                                          in1=xn_sb[:, nt, :], op=ALU.add)
                    p.inc(ins, "ve")

                ve_a = p.val("ve")

                @blk.scalar
                def _(s, ve_a=ve_a):
                    s.wait_ge(p.sems["ve"], ve_a)
                    p.inc(s.activation(out=kv1[:, :], in_=kv0[:, :], func=AF.Silu), "ac")
                    ins = s.activation(out=kv0[:, :], in_=kv1[:, :], func=AF.Square,
                                       bias=_f(0.5), scale=_f(1.0))
                    p.inc(ins, "ac")

                ac_a = p.val("ac")

                @blk.vector
                def _(v, ac_a=ac_a):
                    v.wait_ge(p.sems["ac"], ac_a)
                    ins = v.tensor_reduce(out=nrm[:, 0:16],
                                          in_=kv0[:, :].rearrange("p (h d) -> p h d", h=16),
                                          axis=AX.X, op=ALU.add)
                    p.inc(ins, "ve")

                ve_b = p.val("ve")

                @blk.scalar
                def _(s, ve_b=ve_b):
                    s.wait_ge(p.sems["ve"], ve_b)
                    ins = s.activation(out=nrm[:, 0:16], in_=nrm[:, 0:16], func=AF.Sqrt)
                    p.inc(ins, "ac")

                ac_b = p.val("ac")

                @blk.vector
                def _(v, ac_b=ac_b):
                    v.wait_ge(p.sems["ac"], ac_b)
                    ins = v.tensor_scalar(out=nrm[:, 0:16], in0=nrm[:, 0:16], scalar1=1e-6,
                                          scalar2=0.0, op0=ALU.add, op1=ALU.add)
                    v.wait_ge(p.sems["ve"], p.inc(ins, "ve"))
                    ins = v.reciprocal(out=nrm[:, 0:16], in_=nrm[:, 0:16])
                    v.wait_ge(p.sems["ve"], p.inc(ins, "ve"))
                    for h in range(16):
                        p.inc(v.tensor_scalar(out=kvb[:, h * 64:(h + 1) * 64],
                                              in0=kv1[:, h * 64:(h + 1) * 64],
                                              scalar1=0.5, scalar2=nrm[:, h:h + 1],
                                              op0=ALU.add, op1=ALU.mult), "ve")

                ve_c = p.val("ve")

                @blk.tensor
                def _(t, k=k, nt=nt, PA=PA, ve_c=ve_c):
                    t.wait_ge(p.sems["ve"], ve_c)
                    last = None
                    for ci in range(NCH):
                        last = t.matmul(PA[:, 0:512], xnT[:, ci, nt * 128:(nt + 1) * 128],
                                        w2_sb[:, 1, ci, 0:512], start=(ci == 0), stop=False)
                        last = t.matmul(PA[:, 512:1024], xnT[:, ci, nt * 128:(nt + 1) * 128],
                                        w2_sb[:, 1, ci, 512:1024], start=(ci == 0), stop=False)
                    t.matmul(PA[:, 0:512], ones_r[:, :], bkv_sb[:, 2 + k, 0:512],
                             start=False, stop=True)
                    last = t.matmul(PA[:, 512:1024], ones_r[:, :], bkv_sb[:, 2 + k, 512:1024],
                                    start=False, stop=True)
                    p.inc(last, "pe")

                pe_vs = p.val("pe")

                @blk.vector
                def _(v, nt=nt, PA=PA, pe_vs=pe_vs):
                    v.wait_ge(p.sems["pe"], pe_vs)
                    ins = v.tensor_tensor(out=kvv[:, :], in0=PA[:, :],
                                          in1=xn_sb[:, nt, :], op=ALU.add)
                    p.inc(ins, "ve")

                ve_d = p.val("ve")

                @blk.tensor
                def _(t, k=k, nt=nt, ve_d=ve_d):
                    t.wait_ge(p.sems["ve"], ve_d)
                    last = None
                    for hp in range(8):
                        t.matmul(PS_A[k][0:64, hp * 64:(hp + 1) * 64],
                                 kvb[:, (2 * hp) * 64:(2 * hp + 1) * 64],
                                 kvv[:, (2 * hp) * 64:(2 * hp + 1) * 64],
                                 start=(nt == 0 and hp == 0), stop=(nt == NT - 1),
                                 tile_position=(0, 0))
                        last = t.matmul(PS_B[k][64:128, hp * 64:(hp + 1) * 64],
                                        kvb[:, (2 * hp + 1) * 64:(2 * hp + 2) * 64],
                                        kvv[:, (2 * hp + 1) * 64:(2 * hp + 2) * 64],
                                        start=(nt == 0 and hp == 0), stop=(nt == NT - 1),
                                        tile_position=(0, 64))
                    p.inc(last, "pe")

        pe_states = p.val("pe")

        @blk.vector
        def _(v):
            v.wait_ge(p.sems["pe"], pe_states)
            p.inc(v.tensor_copy(out=sts[0:64, 0, :], in_=P2[0:64, :]), "ev")
            p.inc(v.tensor_copy(out=sts[64:128, 0, :], in_=P4[64:128, :]), "ev")
            p.inc(v.tensor_copy(out=sts[0:64, 1, :], in_=P3[0:64, :]), "ev")
            p.inc(v.tensor_copy(out=sts[64:128, 1, :], in_=P5[64:128, :]), "ev")

        @blk.sync
        def _(sync):
            sync.wait_ge(p.sems["ev"], p.val("ev"))
            ins = sync.dma_start(out=st_o[:, :, :], in_=sts[:, :, :])
            p.inc(ins, "st", 16)
            sync.wait_ge(p.sems["st"], p.val("st"))
    return p


def prep_L1(I, core):
    b, t4 = core // 4, core % 4
    xs = np.asarray(I["L0_attn_a_x"][b, t4 * TOK:(t4 + 1) * TOK, :], np.float32)
    return dict(
        xs=_tokmajor(xs),
        wqT=_chunked(np.ascontiguousarray(np.asarray(I["L0_attn_a_Wq"], np.float32).T)),
        wkT=np.stack([_chunked(np.ascontiguousarray(np.asarray(I["L0_attn_a_Wk"][k], np.float32).T))
                      for k in range(2)]),
        wvT=np.stack([_chunked(np.ascontiguousarray(np.asarray(I["L0_attn_a_Wv"][k], np.float32).T))
                      for k in range(2)]),
        bq=_colvec(I["L0_attn_a_bq"]),
        bkv=np.stack([np.asarray(I["L0_attn_a_bk"][0], np.float32),
                      np.asarray(I["L0_attn_a_bk"][1], np.float32),
                      np.asarray(I["L0_attn_a_bv"][0], np.float32),
                      np.asarray(I["L0_attn_a_bv"][1], np.float32)])[None, :, :],
        idm=_ident_bf(),
        sel=np.kron(np.eye(2, dtype=np.float32), np.ones((64, 1), np.float32)),
        sel2=np.kron(np.eye(2, dtype=np.float32), np.ones((1, 64), np.float32)),
    )



def build_L2():
    p = Prog()
    nc = p.nc
    xs = p.dram_in("xs", [128, NT, D])
    qn_i = p.dram_in("qn", [128, NCH, TOK], BF16)
    pst = p.dram_in("pst", [128, 4, 2, 512])
    woT = p.dram_in("woT", [128, NCH, D])
    lng = p.dram_in("lng", [128, NCH])
    lnb = p.dram_in("lnb", [128, NCH])
    bo_row = p.dram_in("bo_row", [1, D])
    idm = p.dram_in("idm", [128, 128], BF16)
    x1_o = p.dram_out("x1", [128, NT, D])
    psum_o = p.dram_out("psums", [1, D])

    x_sb = p.sb("x", [128, NT, D])
    stf = p.sb("stf", [128, 2, 512])
    stb = p.sb("stb", [128, 512], BF16)
    wo_sb = p.sb("wo", [128, NCH, D], BF16)
    lng_sb = p.sb("lngs", [128, NCH])
    lnb_sb = p.sb("lnbs", [128, NCH], BF16)
    bo_sb = p.sb("bos", [1, D])
    id_sb = p.sb("id", [128, 128], BF16)
    stats = p.sb("stats", [128, 128])
    scr = p.sb("scr", [128, D])
    brow = p.sb("brow", [1, D])
    brow_b = p.sb("browb", [1, D], BF16)
    ones_b = p.sb("ones_b", [128, 8], BF16)
    ones_r = p.sb("ones_r", [1, 128], BF16)
    psums_sb = p.sb("psums_s", [1, D])
    pst_sb = p.sb("pst_s", [128, 1, 2, 512])
    bufA = p.sb("bufA", [128, NCH, TOK], BF16)
    bufB = p.sb("bufB", [128, NCH, TOK], BF16)
    bufC = p.sb("bufC", [128, NCH, TOK], BF16)
    qstage = p.sb("qstage", [128, 2, 512], BF16)
    out_fm = bufA

    def _tiles(buf):
        return buf[:, :, :].rearrange("p c t -> p (c t)").rearrange("p (n d) -> p n d", n=NT)

    P0 = p.ps("P0", [128, D])
    P1 = p.ps("P1", [128, D])
    P2 = p.ps("P2", [128, 512])
    PT = [p.ps("PTa", [128, 512], BF16), p.ps("PTb", [128, 512], BF16)]

    with nc.Block() as blk:
        @blk.sync
        def _(sync):
            sync.dma_start(out=x_sb[:, :, :], in_=xs[:, :, :]).then_inc(p.sems["ld"], 16)
            sync.dma_start(out=pst_sb[:, 0, :, :], in_=pst[:, 0, :, :]).then_inc(p.sems["ld"], 16)
            p.cnt["ld"] += 32
            ins = sync.dma_start(out=id_sb[:, :], in_=idm[:, :])
            p.inc(ins, "ld", 16)

        @blk.gpsimd
        def _(g):
            g.dma_start(out=wo_sb[:, :, :], in_=woT[:, :, :]).then_inc(p.sems["wd"], 16)
            g.dma_start(out=lng_sb[:, :], in_=lng[:, :]).then_inc(p.sems["wd"], 16)
            g.dma_start(out=lnb_sb[:, :], in_=lnb[:, :]).then_inc(p.sems["wd"], 16)
            g.dma_start(out=bo_sb[:, :], in_=bo_row[:, :]).then_inc(p.sems["wd"], 16)
            p.cnt["wd"] += 64
            g.memset(ones_b[:, :], 1.0)
            ins = g.memset(ones_r[:, :], 1.0)
            p.inc(ins, "wd", 1)

        @blk.tensor
        def _(t):
            t.wait_ge(p.sems["wd"], 65)
            last = None
            for half in range(2):
                for ci in range(NCH):
                    last = t.matmul(P0[0:1, half * 512:(half + 1) * 512],
                                    lnb_sb[:, ci:ci + 1],
                                    wo_sb[:, ci, half * 512:(half + 1) * 512],
                                    start=(ci == 0), stop=(ci == NCH - 1))
            p.inc(last, "pe")

        pe_brow = p.val("pe")

        @blk.vector
        def _(v):
            v.wait_ge(p.sems["ld"], 48)
            v.wait_ge(p.sems["pe"], pe_brow)
            ins = v.tensor_copy(out=brow[:, :], in_=P0[0:1, 0:1024])
            p.inc(ins, "ve")

        @blk.vector
        def _(v):
            p.inc(v.tensor_copy(out=stf[:, 0, :], in_=pst_sb[:, 0, 0, :]), "ve")
            p.inc(v.tensor_copy(out=stf[:, 1, :], in_=pst_sb[:, 0, 1, :]), "ve")

        for c4 in range(1, 4):
            @blk.sync
            def _(sync, c4=c4):
                sync.wait_ge(p.sems["ve"], p.val("ve"))
                ins = sync.dma_start(out=pst_sb[:, 0, :, :], in_=pst[:, c4, :, :])
                p.inc(ins, "ld", 16)

            ld_c = p.val("ld")

            @blk.vector
            def _(v, ld_c=ld_c):
                v.wait_ge(p.sems["ld"], ld_c)
                p.inc(v.tensor_tensor(out=stf[:, 0, :], in0=stf[:, 0, :],
                                      in1=pst_sb[:, 0, 0, :], op=ALU.add), "ve")
                p.inc(v.tensor_tensor(out=stf[:, 1, :], in0=stf[:, 1, :],
                                      in1=pst_sb[:, 0, 1, :], op=ALU.add), "ve")

        @blk.vector
        def _(v):
            v.wait_ge(p.sems["ve"], p.val("ve"))
            ins = v.scalar_tensor_tensor(out=stb[:, :], in0=stf[:, 0, :],
                                         scalar=float(SEQ) ** -2.0, in1=stf[:, 1, :],
                                         op0=ALU.mult, op1=ALU.mult)
            v.wait_ge(p.sems["ve"], p.inc(ins, "ve"))
            ins = v.tensor_tensor(out=brow[:, :], in0=brow[:, :], in1=bo_sb[:, :], op=ALU.add)
            v.wait_ge(p.sems["ve"], p.inc(ins, "ve"))
            p.inc(v.tensor_copy(out=brow_b[:, :], in_=brow[:, :]), "ve")
            for ci in range(NCH):
                p.inc(v.tensor_scalar(out=wo_sb[:, ci, :], in0=wo_sb[:, ci, :],
                                      scalar1=lng_sb[:, ci:ci + 1], scalar2=0.0,
                                      op0=ALU.mult, op1=ALU.add), "ve")

        ve_prep = p.val("ve")

        ev_base = p.val("ev")
        pe_qs_base = p.val("pe")
        its = [(hp, qc) for hp in range(NCH) for qc in range(4)]
        for it, (hp, qc) in enumerate(its):
            PA = [P0, P1][it % 2]

            @blk.sync
            def _(sync, hp=hp, qc=qc, it=it):
                if it >= 2:
                    sync.wait_ge(p.sems["pe"], pe_qs_base + it - 1)
                ins = sync.dma_start(out=qstage[:, it % 2, :],
                                     in_=qn_i[:, hp, qc * 512:(qc + 1) * 512])
                p.inc(ins, "ld", 16)

            ld_it = p.val("ld")

            @blk.tensor
            def _(t, hp=hp, qc=qc, PA=PA, it=it, ld_it=ld_it):
                t.wait_ge(p.sems["ve"], ve_prep)
                t.wait_ge(p.sems["ld"], ld_it)
                if it >= 1:
                    t.wait_ge(p.sems["ev"], ev_base + it)
                t.matmul(PA[0:64, 0:512], stb[0:64, hp * 64:(hp + 1) * 64],
                         qstage[0:64, it % 2, :],
                         start=True, stop=True, tile_position=(0, 0))
                ins = t.matmul(PA[64:128, 0:512], stb[64:128, hp * 64:(hp + 1) * 64],
                               qstage[64:128, it % 2, :],
                               start=True, stop=True, tile_position=(64, 64))
                p.inc(ins, "pe")

            pe_o = p.val("pe")

            @blk.vector
            def _(v, hp=hp, qc=qc, PA=PA, pe_o=pe_o):
                v.wait_ge(p.sems["pe"], pe_o)
                ins = v.tensor_copy(out=out_fm[:, hp, qc * 512:(qc + 1) * 512],
                                    in_=PA[0:128, 0:512])
                p.inc(ins, "ev")

        out_tm = _tiles(bufB)
        emit_fm2tm(p, blk, out_fm, out_tm, PT, id_sb, "ev", p.val("ev"), "e2")
        oln = _tiles(bufC)
        emit_ln(p, blk, out_tm, oln, "e2", p.val("e2"), "ev", stats, scr)
        olnT = bufA
        emit_tm2fm(p, blk, oln, olnT, PT, id_sb, "ev", p.val("ev"), "e2")
        tr2 = p.val("e2")

        ev2_base = p.val("ev")
        for nt in range(NT):
            PA = [P0, P1][nt % 2]

            @blk.tensor
            def _(t, nt=nt, PA=PA):
                t.wait_ge(p.sems["e2"], tr2)
                if nt >= 1:
                    t.wait_ge(p.sems["ev"], ev2_base + nt)
                last = None
                for ci in range(NCH):
                    last = t.matmul(PA[:, 0:512], olnT[:, ci, nt * 128:(nt + 1) * 128],
                                    wo_sb[:, ci, 0:512], start=(ci == 0), stop=False)
                    last = t.matmul(PA[:, 512:1024], olnT[:, ci, nt * 128:(nt + 1) * 128],
                                    wo_sb[:, ci, 512:1024], start=(ci == 0), stop=False)
                t.matmul(PA[:, 0:512], ones_r[:, :], brow_b[:, 0:512],
                         start=False, stop=True)
                last = t.matmul(PA[:, 512:1024], ones_r[:, :], brow_b[:, 512:1024],
                                start=False, stop=True)
                p.inc(last, "pe")

            pe_p = p.val("pe")

            @blk.vector
            def _(v, nt=nt, PA=PA, pe_p=pe_p):
                v.wait_ge(p.sems["pe"], pe_p)
                ins = v.tensor_tensor(out=x_sb[:, nt, :], in0=PA[:, :],
                                      in1=x_sb[:, nt, :], op=ALU.add)
                p.inc(ins, "ev")

        x1_done = p.val("ev")
        xn1 = _tiles(bufB)

        @blk.sync
        def _(sync):
            sync.wait_ge(p.sems["ev"], x1_done)
            ins = sync.dma_start(out=x1_o[:, :, :], in_=x_sb[:, :, :])
            p.inc(ins, "st", 16)

        emit_ln(p, blk, x_sb, xn1, "ev", x1_done, "ev", stats, scr)
        ln2 = p.val("ev")

        @blk.tensor
        def _(t):
            t.wait_ge(p.sems["ev"], ln2)
            last = None
            for half in range(2):
                for nt in range(NT):
                    last = t.matmul(P0[0:1, half * 512:(half + 1) * 512], ones_b[:, 0:1],
                                    xn1[:, nt, half * 512:(half + 1) * 512],
                                    start=(nt == 0), stop=(nt == NT - 1))
            p.inc(last, "pe")

        pe_s = p.val("pe")

        @blk.vector
        def _(v):
            v.wait_ge(p.sems["pe"], pe_s)
            ins = v.tensor_copy(out=psums_sb[:, :], in_=P0[0:1, 0:1024])
            p.inc(ins, "ev")

        @blk.sync
        def _(sync):
            sync.wait_ge(p.sems["ev"], p.val("ev"))
            ins = sync.dma_start(out=psum_o[:, :], in_=psums_sb[:, :])
            p.inc(ins, "st", 16)
            sync.wait_ge(p.sems["st"], p.val("st"))
    return p


def prep_L2(I, core, qn, pstates):
    b, t4 = core // 4, core % 4
    xs = np.asarray(I["L0_attn_a_x"][b, t4 * TOK:(t4 + 1) * TOK, :], np.float32)
    return dict(
        xs=_tokmajor(xs),
        qn=qn,
        pst=pstates,
        woT=_chunked(np.ascontiguousarray(np.asarray(I["L0_attn_a_Wo"], np.float32).T)),
        lng=_colvec(I["L0_attn_a_ln_g"]),
        lnb=_colvec(I["L0_attn_a_ln_b"]),
        bo_row=np.asarray(I["L0_attn_a_bo"], np.float32)[None, :],
        idm=_ident_bf(),
    )




def emit_ln_T_dram(p, blk, x_sb, xnT_dram, stage, PTb, id_sb, wait_sem,
                   wait_val, stats, scr):
    emit_ln(p, blk, x_sb, None, wait_sem, wait_val, "ve", stats, scr, nt_list=[])
    ve_stats = p.val("ve")
    stg2 = p.sb("stg2lnT", [128, 2, 512], BF16)
    st_base = p.val("st")
    pe_after_nt = {}

    for nt in range(NT):
        @blk.vector
        def _(v, nt=nt):
            v.wait_ge(p.sems["ve"], ve_stats)
            if nt >= 2:
                v.wait_ge(p.sems["pe"], pe_after_nt[nt - 2])
            ins = v.tensor_scalar(out=stage[:, nt % 2, :], in0=x_sb[:, nt, :],
                                  scalar1=stats[:, 32 + nt:33 + nt],
                                  scalar2=stats[:, 48 + nt:49 + nt],
                                  op0=ALU.subtract, op1=ALU.mult)
            p.inc(ins, "ve")

        ve_ap = p.val("ve")
        for ch in range(2):
            cj = ch * 4
            gi = 2 * nt + ch

            @blk.tensor
            def _(t, nt=nt, cj=cj, gi=gi, ve_ap=ve_ap):
                t.wait_ge(p.sems["ve"], ve_ap)
                if gi >= 2:
                    t.wait_ge(p.sems["e2"], e2_of[gi - 2])
                bank = PTb[gi % 2]
                last = None
                for j in range(4):
                    last = t.transpose(bank[:, j * 128:(j + 1) * 128],
                                       stage[:, nt % 2, (cj + j) * 128:(cj + j + 1) * 128],
                                       id_sb[:, :])
                p.inc(last, "pe")

            pe_tr = p.val("pe")

            @blk.vector
            def _(v, gi=gi, pe_tr=pe_tr):
                v.wait_ge(p.sems["pe"], pe_tr)
                if gi >= 2:
                    v.wait_ge(p.sems["st"], st_base + 16 * (gi - 1))
                bank = PTb[gi % 2]
                ins = v.tensor_copy(out=stg2[:, gi % 2, :], in_=bank[:, :])
                p.inc(ins, "e2")

            if gi == 0:
                e2_of = {}
            e2_of[gi] = p.val("e2")
            e2_now = p.val("e2")

            @blk.sync
            def _(sync, nt=nt, cj=cj, gi=gi, e2_now=e2_now):
                sync.wait_ge(p.sems["e2"], e2_now)
                ins = sync.dma_start(out=xnT_dram[:, cj:cj + 4, nt * 128:(nt + 1) * 128],
                                     in_=stg2[:, gi % 2, :])
                p.inc(ins, "st", 16)

        pe_after_nt[nt] = p.val("pe")



def build_FFN0():
    p = Prog()
    nc = p.nc
    xs = p.dram_in("xs", [128, NT, D])
    parts = p.dram_in("parts", [128, 32])
    ar_i = p.dram_in("ar", [1, 1])
    cnw1 = p.dram_in("cnw1", [128, NCH, D])
    cnw2 = p.dram_in("cnw2", [128, NCH, 512])
    cnw3 = p.dram_in("cnw3", [128, 4, 256])
    cnw4 = p.dram_in("cnw4", [128, 2, 1])
    cnb1 = p.dram_in("cnb1", [128, NCH])
    cnb2 = p.dram_in("cnb2", [128, 4])
    cnb3 = p.dram_in("cnb3", [128, 2])
    cnb4 = p.dram_in("cnb4", [1, 1])
    gcw1 = p.dram_in("gcw1", [128, NCH, 512])
    gcw2 = p.dram_in("gcw2", [128, 4, 256])
    gcw3 = p.dram_in("gcw3", [128, 2, 1])
    gcb1 = p.dram_in("gcb1", [128, 4])
    gcb2 = p.dram_in("gcb2", [128, 2])
    gcb3 = p.dram_in("gcb3", [1, 1])
    idm = p.dram_in("idm", [128, 128], BF16)
    idf_i = p.dram_in("idf", [128, 128])
    x_o = p.dram_out("x_out", [128, NT, D])
    xnT_o = p.dram_out("xnT_out", [128, NCH, TOK], BF16)

    x_sb = p.sb("x", [128, NT, D])
    OA = p.sb("OA", [128, NT, D])
    tT = p.sb("tT", [128, NCH, 512], BF16)
    h1c = p.sb("h1c", [128, NCH, 512], BF16)
    h2c = p.sb("h2c", [128, 4, 512], BF16)
    h3c = p.sb("h3c", [128, 2, 512], BF16)
    stage = p.sb("stage", [128, 2, D], BF16)
    stats = p.sb("stats", [128, 128])
    scr = p.sb("scr", [128, D])
    id_sb = p.sb("id", [128, 128], BF16)
    idf_sb = p.sb("idf_s", [128, 128])
    w1 = p.sb("w1", [128, NCH, D], BF16)
    w2 = p.sb("w2c", [128, NCH, 512], BF16)
    w3 = p.sb("w3c", [128, 4, 256], BF16)
    w4 = p.sb("w4c", [128, 2, 1], BF16)
    b1 = p.sb("b1c", [128, NCH])
    b2 = p.sb("b2c", [128, 4])
    b3 = p.sb("b3c", [128, 2])
    gw1 = p.sb("gw1", [128, NCH, 512], BF16)
    gw2 = p.sb("gw2", [128, 4, 256], BF16)
    gw3 = p.sb("gw3", [128, 2, 1], BF16)
    gb1 = p.sb("gb1", [128, 4])
    gb2 = p.sb("gb2", [128, 2])
    rows = p.sb("rows", [128, 32])
    gcf = p.sb("gcf", [128, 16])
    onecol = p.sb("onecol", [1, 8])
    gcv = p.sb("gcv", [128, NCH], BF16)
    g1v = p.sb("g1v", [128, 4], BF16)
    g2v = p.sb("g2v", [128, 2], BF16)
    tfv = p.sb("tfv", [128, 16])
    gpos = p.sb("gpos", [128, 16])
    gneg = p.sb("gneg", [128, 16])
    acol = p.sb("acol", [128, 4])
    scrq = p.sb("scrq", [128, 512])
    ones_fr = p.sb("ones_fr", [1, 128])

    P0 = p.ps("P0", [128, 512])
    P1 = p.ps("P1", [128, 512])
    PTF = p.ps("PTF", [128, 16])
    PT = [p.ps("PTa", [128, 512], BF16), p.ps("PTb", [128, 512], BF16)]
    PF = [p.ps("PFa", [128, 512]), p.ps("PFb", [128, 512])]

    with nc.Block() as blk:
        @blk.sync
        def _(sync):
            sync.dma_start(out=x_sb[:, :, :], in_=xs[:, :, :]).then_inc(p.sems["ld"], 16)
            sync.dma_start(out=id_sb[:, :], in_=idm[:, :]).then_inc(p.sems["ld"], 16)
            sync.dma_start(out=idf_sb[:, :], in_=idf_i[:, :]).then_inc(p.sems["ld"], 16)
            p.cnt["ld"] += 16
            ins = sync.dma_start(out=rows[:, :], in_=parts[:, :])
            p.inc(ins, "ld", 16)
            p.cnt["ld"] += 32

        @blk.gpsimd
        def _(g):
            for dst, srci in [(w1, cnw1), (w2, cnw2), (w3, cnw3), (w4, cnw4),
                              (gw1, gcw1), (gw2, gcw2), (gw3, gcw3)]:
                g.dma_start(out=dst[:, :, :], in_=srci[:, :, :]).then_inc(p.sems["wd"], 16)
                p.cnt["wd"] += 16
            for dst, srci in [(b1, cnb1), (b2, cnb2), (b3, cnb3),
                              (gb1, gcb1), (gb2, gcb2)]:
                g.dma_start(out=dst[:, :], in_=srci[:, :]).then_inc(p.sems["wd"], 16)
                p.cnt["wd"] += 16
            g.dma_start(out=onecol[0:1, 0:1], in_=gcb3[:, :]).then_inc(p.sems["wd"], 16)
            g.dma_start(out=onecol[0:1, 1:2], in_=cnb4[:, :]).then_inc(p.sems["wd"], 16)
            g.dma_start(out=onecol[0:1, 2:3], in_=ar_i[:, :]).then_inc(p.sems["wd"], 16)
            p.cnt["wd"] += 48
            ins = g.memset(ones_fr[:, :], 1.0)
            p.inc(ins, "wd", 1)

        wd_all = p.val("wd")

        emit_ln(p, blk, x_sb, None, "ld", 48, "ev", stats, scr, xf_sb=OA)
        ln0 = p.val("ev")

        @blk.vector
        def _(v):
            v.wait_ge(p.sems["ld"], 48)
            ins = v.tensor_tensor(out=gcf[:, 0:8], in0=rows[:, 0:8],
                                  in1=rows[:, 8:16], op=ALU.add)
            v.wait_ge(p.sems["ve"], p.inc(ins, "ve"))
            ins = v.tensor_tensor(out=gcf[:, 8:16], in0=rows[:, 16:24],
                                  in1=rows[:, 24:32], op=ALU.add)
            v.wait_ge(p.sems["ve"], p.inc(ins, "ve"))
            ins = v.tensor_tensor(out=gcf[:, 0:8], in0=gcf[:, 0:8],
                                  in1=gcf[:, 8:16], op=ALU.add)
            v.wait_ge(p.sems["ve"], p.inc(ins, "ve"))
            ins = v.tensor_scalar(out=gcv[:, :], in0=gcf[:, 0:8],
                                  scalar1=1.0 / 8192.0, scalar2=0.0,
                                  op0=ALU.mult, op1=ALU.add)
            v.wait_ge(p.sems["ve"], p.inc(ins, "ve"))

        gcv_done = p.val("ve")

        @blk.tensor
        def _(t):
            t.wait_ge(p.sems["ve"], gcv_done)
            t.wait_ge(p.sems["wd"], wd_all)
            for fj in range(4):
                last = None
                for ci in range(NCH):
                    last = t.matmul(P0[:, fj:fj + 1], gw1[:, ci, fj * 128:(fj + 1) * 128],
                                    gcv[:, ci:ci + 1],
                                    start=(ci == 0 and fj == 0), stop=False)
                p.inc(last, "pe")

        pe_g1 = p.val("pe")

        @blk.scalar
        def _(s):
            s.wait_ge(p.sems["pe"], pe_g1)
            for fj in range(4):
                p.inc(s.activation(out=g1v[:, fj:fj + 1], in_=P0[:, fj:fj + 1],
                                   func=AF.Gelu, bias=gb1[:, fj:fj + 1],
                                   scale=_f(1.0)), "ac")

        ac_g1 = p.val("ac")

        @blk.tensor
        def _(t):
            t.wait_ge(p.sems["ac"], ac_g1)
            for fj in range(2):
                last = None
                for ci in range(4):
                    last = t.matmul(P1[:, fj:fj + 1], gw2[:, ci, fj * 128:(fj + 1) * 128],
                                    g1v[:, ci:ci + 1],
                                    start=(ci == 0 and fj == 0), stop=False)
                p.inc(last, "pe")

        pe_g2 = p.val("pe")

        @blk.scalar
        def _(s):
            s.wait_ge(p.sems["pe"], pe_g2)
            for fj in range(2):
                p.inc(s.activation(out=g2v[:, fj:fj + 1], in_=P1[:, fj:fj + 1],
                                   func=AF.Gelu, bias=gb2[:, fj:fj + 1],
                                   scale=_f(1.0)), "ac")

        ac_g2 = p.val("ac")

        @blk.tensor
        def _(t):
            t.wait_ge(p.sems["ac"], ac_g2)
            last = None
            for ci in range(2):
                last = t.matmul(P0[0:1, 8:9], gw3[:, ci, 0:1], g2v[:, ci:ci + 1],
                                start=(ci == 0), stop=(ci == 1))
            p.inc(last, "pe")

        pe_gf = p.val("pe")

        @blk.scalar
        def _(s):
            s.wait_ge(p.sems["pe"], pe_gf)
            p.inc(s.activation(out=onecol[0:1, 3:4], in_=P0[0:1, 8:9],
                               func=AF.Sigmoid, bias=onecol[0:1, 0:1],
                               scale=_f(1.0)), "ac")

        ac_gf = p.val("ac")

        @blk.vector
        def _(v):
            v.wait_ge(p.sems["ac"], ac_gf)
            ins = v.tensor_tensor(out=onecol[0:1, 4:5], in0=onecol[0:1, 2:3],
                                  in1=onecol[0:1, 3:4], op=ALU.mult)
            v.wait_ge(p.sems["ve"], p.inc(ins, "ve"))
            ins = v.tensor_scalar(out=onecol[0:1, 4:5], in0=onecol[0:1, 4:5],
                                  scalar1=1.4, scalar2=0.0, op0=ALU.mult, op1=ALU.add)
            v.wait_ge(p.sems["ve"], p.inc(ins, "ve"))
            ins = v.tensor_tensor(out=onecol[0:1, 4:5], in0=onecol[0:1, 4:5],
                                  in1=onecol[0:1, 2:3], op=ALU.subtract)
            v.wait_ge(p.sems["ve"], p.inc(ins, "ve"))
            ins = v.tensor_scalar(out=onecol[0:1, 4:5], in0=onecol[0:1, 4:5],
                                  scalar1=0.1, scalar2=0.0, op0=ALU.add, op1=ALU.add)
            v.wait_ge(p.sems["ve"], p.inc(ins, "ve"))
            ins = v.tensor_scalar(out=onecol[0:1, 5:6], in0=onecol[0:1, 2:3],
                                  scalar1=0.6, scalar2=0.0, op0=ALU.mult, op1=ALU.add)
            v.wait_ge(p.sems["ve"], p.inc(ins, "ve"))
            ins = v.tensor_copy(out=onecol[0:1, 6:7], in_=onecol[0:1, 1:2])
            v.wait_ge(p.sems["ve"], p.inc(ins, "ve"))

        ve_sc = p.val("ve")

        @blk.tensor
        def _(t):
            t.wait_ge(p.sems["ve"], ve_sc)
            ins = t.matmul(P1[:, 8:11], ones_fr[:, :], onecol[0:1, 4:7],
                           start=True, stop=True)
            p.inc(ins, "pe")

        pe_rep = p.val("pe")

        @blk.vector
        def _(v):
            v.wait_ge(p.sems["pe"], pe_rep)
            p.inc(v.tensor_copy(out=acol[:, 0:3], in_=P1[:, 8:11]), "ve")

        prep_done = p.val("ve")

        for cyc in range(3):
            for qc in range(4):
                e2_qbase = p.val("e2")

                @blk.tensor
                def _(t, qc=qc, cyc=cyc, e2_qbase=e2_qbase):
                    t.wait_ge(p.sems["ve"], prep_done)
                    for gi in range(8):
                        nt = qc * 4 + gi // 2
                        cj = (gi % 2) * 4
                        bank = PF[gi % 2]
                        if gi >= 2:
                            t.wait_ge(p.sems["e2"], e2_qbase + gi - 1)
                        last = None
                        for j in range(4):
                            last = t.transpose(bank[:, j * 128:(j + 1) * 128],
                                               OA[:, nt, (cj + j) * 128:(cj + j + 1) * 128],
                                               idf_sb[:, :])
                        p.inc(last, "pe")

                pe_trb = p.val("pe") - 8

                @blk.vector
                def _(v, qc=qc, pe_trb=pe_trb):
                    for gi in range(8):
                        nt_in_qc = gi // 2
                        cj = (gi % 2) * 4
                        bank = PF[gi % 2]
                        v.wait_ge(p.sems["pe"], pe_trb + gi + 1)
                        ins = v.tensor_copy(
                            out=tT[:, cj:cj + 4, nt_in_qc * 128:(nt_in_qc + 1) * 128],
                            in_=bank[:, :])
                        p.inc(ins, "e2")

                tr_q = p.val("e2")

                ac_h1base = p.val("ac")
                for fj in range(NCH):
                    PA = [P0, P1][fj % 2]

                    @blk.tensor
                    def _(t, fj=fj, PA=PA, tr_q=tr_q, ac_h1base=ac_h1base):
                        if fj == 0:
                            t.wait_ge(p.sems["e2"], tr_q)
                        if fj >= 2:
                            t.wait_ge(p.sems["ac"], ac_h1base + fj - 1)
                        last = None
                        for ci in range(NCH):
                            last = t.matmul(PA[:, :], w1[:, ci, fj * 128:(fj + 1) * 128],
                                            tT[:, ci, :], start=(ci == 0),
                                            stop=(ci == NCH - 1))
                        p.inc(last, "pe")

                    pe_h1 = p.val("pe")

                    @blk.scalar
                    def _(s, fj=fj, PA=PA, pe_h1=pe_h1):
                        s.wait_ge(p.sems["pe"], pe_h1)
                        p.inc(s.activation(out=h1c[:, fj, :], in_=PA[:, :],
                                           func=AF.Gelu, bias=b1[:, fj:fj + 1],
                                           scale=_f(1.0)), "ac")

                h1_done = p.val("ac")

                ac_h2base = p.val("ac")
                for fj in range(4):
                    PA = [P0, P1][fj % 2]

                    @blk.tensor
                    def _(t, fj=fj, PA=PA, h1_done=h1_done, ac_h2base=ac_h2base):
                        if fj == 0:
                            t.wait_ge(p.sems["ac"], h1_done)
                        if fj >= 2:
                            t.wait_ge(p.sems["ac"], ac_h2base + fj - 1)
                        last = None
                        for ci in range(NCH):
                            last = t.matmul(PA[:, :], w2[:, ci, fj * 128:(fj + 1) * 128],
                                            h1c[:, ci, :], start=(ci == 0),
                                            stop=(ci == NCH - 1))
                        p.inc(last, "pe")

                    pe_h2 = p.val("pe")

                    @blk.scalar
                    def _(s, fj=fj, PA=PA, pe_h2=pe_h2):
                        s.wait_ge(p.sems["pe"], pe_h2)
                        p.inc(s.activation(out=h2c[:, fj, :], in_=PA[:, :],
                                           func=AF.Gelu, bias=b2[:, fj:fj + 1],
                                           scale=_f(1.0)), "ac")

                h2_done = p.val("ac")

                for fj in range(2):
                    PA = [P0, P1][fj % 2]

                    @blk.tensor
                    def _(t, fj=fj, PA=PA, h2_done=h2_done):
                        if fj == 0:
                            t.wait_ge(p.sems["ac"], h2_done)
                        last = None
                        for ci in range(4):
                            last = t.matmul(PA[:, :], w3[:, ci, fj * 128:(fj + 1) * 128],
                                            h2c[:, ci, :], start=(ci == 0),
                                            stop=(ci == 3))
                        p.inc(last, "pe")

                    pe_h3 = p.val("pe")

                    @blk.scalar
                    def _(s, fj=fj, PA=PA, pe_h3=pe_h3):
                        s.wait_ge(p.sems["pe"], pe_h3)
                        p.inc(s.activation(out=h3c[:, fj, :], in_=PA[:, :],
                                           func=AF.Gelu, bias=b3[:, fj:fj + 1],
                                           scale=_f(1.0)), "ac")

                h3_done = p.val("ac")

                @blk.tensor
                def _(t, qc=qc, h3_done=h3_done):
                    t.wait_ge(p.sems["ac"], h3_done)
                    last = None
                    for ntq in range(4):
                        col = qc * 4 + ntq
                        for fj in range(2):
                            last = t.matmul(PTF[:, col:col + 1],
                                            h3c[:, fj, ntq * 128:(ntq + 1) * 128],
                                            w4[:, fj, 0:1],
                                            start=(qc == 0 and ntq == 0 and fj == 0),
                                            stop=False)
                    p.inc(last, "pe")

            pe_tf = p.val("pe")

            @blk.scalar
            def _(s, pe_tf=pe_tf):
                s.wait_ge(p.sems["pe"], pe_tf)
                p.inc(s.activation(out=tfv[:, :], in_=PTF[:, :], func=AF.Sigmoid,
                                   bias=acol[:, 2:3], scale=_f(1.0)), "ac")

            ac_tf = p.val("ac")

            @blk.vector
            def _(v, ac_tf=ac_tf):
                v.wait_ge(p.sems["ac"], ac_tf)
                ins = v.tensor_scalar(out=gpos[:, :], in0=tfv[:, :],
                                      scalar1=acol[:, 1:2], scalar2=acol[:, 0:1],
                                      op0=ALU.mult, op1=ALU.add)
                v.wait_ge(p.sems["ve"], p.inc(ins, "ve"))
                ins = v.tensor_scalar(out=gneg[:, :], in0=gpos[:, :],
                                      scalar1=-1.0, scalar2=0.0,
                                      op0=ALU.mult, op1=ALU.add)
                v.wait_ge(p.sems["ve"], p.inc(ins, "ve"))

            g_done = p.val("ve")

            @blk.vector
            def _(v, g_done=g_done):
                v.wait_ge(p.sems["ve"], g_done)
                Q = 256
                for nt in range(NT):
                    gp = gpos[:, nt:nt + 1]
                    gn = gneg[:, nt:nt + 1]
                    ins = v.tensor_copy(out=scrq[:, 0:Q], in_=OA[:, nt, 0:Q])
                    v.wait_ge(p.sems["ve"], p.inc(ins, "ve"))
                    ins = v.tensor_copy(out=scrq[:, Q:2 * Q], in_=OA[:, nt, Q:2 * Q])
                    v.wait_ge(p.sems["ve"], p.inc(ins, "ve"))
                    ins = v.scalar_tensor_tensor(out=OA[:, nt, 0:Q],
                                                 in0=OA[:, nt, 2 * Q:3 * Q], scalar=gp,
                                                 in1=OA[:, nt, 0:Q],
                                                 op0=ALU.mult, op1=ALU.add)
                    v.wait_ge(p.sems["ve"], p.inc(ins, "ve"))
                    ins = v.scalar_tensor_tensor(out=OA[:, nt, Q:2 * Q],
                                                 in0=OA[:, nt, 3 * Q:4 * Q], scalar=gn,
                                                 in1=OA[:, nt, Q:2 * Q],
                                                 op0=ALU.mult, op1=ALU.add)
                    v.wait_ge(p.sems["ve"], p.inc(ins, "ve"))
                    ins = v.scalar_tensor_tensor(out=OA[:, nt, 2 * Q:3 * Q],
                                                 in0=scrq[:, 0:Q], scalar=gn,
                                                 in1=OA[:, nt, 2 * Q:3 * Q],
                                                 op0=ALU.mult, op1=ALU.add)
                    v.wait_ge(p.sems["ve"], p.inc(ins, "ve"))
                    ins = v.scalar_tensor_tensor(out=OA[:, nt, 3 * Q:4 * Q],
                                                 in0=scrq[:, Q:2 * Q], scalar=gp,
                                                 in1=OA[:, nt, 3 * Q:4 * Q],
                                                 op0=ALU.mult, op1=ALU.add)
                    v.wait_ge(p.sems["ve"], p.inc(ins, "ve"))

        cycles_done = p.val("ve")

        @blk.vector
        def _(v):
            v.wait_ge(p.sems["ve"], cycles_done)
            for nt in range(NT):
                p.inc(v.tensor_tensor(out=x_sb[:, nt, :], in0=x_sb[:, nt, :],
                                      in1=OA[:, nt, :], op=ALU.add), "ev")

        x_done = p.val("ev")

        @blk.sync
        def _(sync):
            sync.wait_ge(p.sems["ev"], x_done)
            ins = sync.dma_start(out=x_o[:, :, :], in_=x_sb[:, :, :])
            p.inc(ins, "st", 16)

        emit_ln_T_dram(p, blk, x_sb, xnT_o, stage, PT, id_sb, "ev", x_done,
                       stats, scr)

        dbg_o = p.dram_out("dbg", [128, 256])
        dbg2_o = p.dram_out("dbg2", [128, 2, D], BF16)

        @blk.sync
        def _(sync):
            sync.wait_ge(p.sems["st"], p.val("st"))
            sync.dma_start(out=dbg_o[:, 0:128], in_=stats[:, :]).then_inc(p.sems["st"], 16)
            sync.dma_start(out=dbg2_o[:, :, :], in_=stage[:, :, :]).then_inc(p.sems["st"], 16)
            p.cnt["st"] += 32
            sync.wait_ge(p.sems["st"], p.val("st"))
    return p


def prep_FFN0(I, xs, parts):
    return dict(
        xs=xs, parts=parts,
        ar=np.asarray(I["L0_ffn_adaptive_range"], np.float32).reshape(1, 1),
        cnw1=_chunked(np.asarray(I["L0_ffn_cn_w1"], np.float32)),
        cnw2=_chunked(np.asarray(I["L0_ffn_cn_w2"], np.float32)),
        cnw3=np.ascontiguousarray(np.asarray(I["L0_ffn_cn_w3"], np.float32)
                                  .reshape(4, 128, 256).transpose(1, 0, 2)),
        cnw4=np.ascontiguousarray(np.asarray(I["L0_ffn_cn_w4"], np.float32)
                                  .reshape(2, 128, 1).transpose(1, 0, 2)),
        cnb1=_colvec(I["L0_ffn_cn_b1"]),
        cnb2=np.ascontiguousarray(np.asarray(I["L0_ffn_cn_b2"], np.float32).reshape(4, 128).T),
        cnb3=np.ascontiguousarray(np.asarray(I["L0_ffn_cn_b3"], np.float32).reshape(2, 128).T),
        cnb4=np.asarray(I["L0_ffn_cn_b4"], np.float32).reshape(1, 1),
        gcw1=_chunked(np.asarray(I["L0_ffn_gc_w1"], np.float32)),
        gcw2=np.ascontiguousarray(np.asarray(I["L0_ffn_gc_w2"], np.float32)
                                  .reshape(4, 128, 256).transpose(1, 0, 2)),
        gcw3=np.ascontiguousarray(np.asarray(I["L0_ffn_gc_w3"], np.float32)
                                  .reshape(2, 128, 1).transpose(1, 0, 2)),
        gcb1=np.ascontiguousarray(np.asarray(I["L0_ffn_gc_b1"], np.float32).reshape(4, 128).T),
        gcb2=np.ascontiguousarray(np.asarray(I["L0_ffn_gc_b2"], np.float32).reshape(2, 128).T),
        gcb3=np.asarray(I["L0_ffn_gc_b3"], np.float32).reshape(1, 1),
        idm=_ident_bf(),
        idf=np.eye(128, dtype=np.float32),
    )


class _ColView:

    def __init__(self, t, off, width=512):
        self.t = t
        self.off = off
        self.width = width

    def __getitem__(self, idx):
        ps, fs = idx
        start = self.off + (fs.start or 0)
        stop = self.off + (fs.stop if fs.stop is not None else self.width)
        return self.t[ps, start:stop]



def emit_phaseB(p, blk, causal, M, qsb, ksb, vsb, msk_sb, attT, avf, rcp, ysb,
                ones_c, ones64, SCall, PAV, PAVB, PDEN, PDENB,
                done_sem, done_val, ld_val):
    for pr in range(2):
        for qc in range(16):
            ktmax = 4 * (qc + 1) if causal else NKT
            nsk = ktmax // 2
            e2_prev = p.val("e2")
            ac_of = {}
            pe_sc_of = {}
            pe_av_of = {}
            ve_msk_of = {}

            def emit_sc_pair(skt):
                @blk.tensor
                def _(t, pr=pr, qc=qc, skt=skt):
                    if pr == 0 and qc == 0 and skt == 0:
                        t.wait_ge(p.sems[done_sem], done_val)
                        t.wait_ge(p.sems["ld"], ld_val)
                    if skt >= 1:
                        t.wait_ge(p.sems["ac"], ac_of[skt - 1])
                    elif not (pr == 0 and qc == 0):
                        t.wait_ge(p.sems["e2"], e2_prev)
                    last = None
                    for kt in (2 * skt, 2 * skt + 1):
                        half = (kt % 2) * 1024
                        t.matmul(SCall[:, half:half + 512],
                                 ksb[0:64, pr, kt * 128:(kt + 1) * 128],
                                 qsb[0:64, pr, qc * 512:(qc + 1) * 512],
                                 start=True, stop=True, tile_position=(0, 0))
                        last = t.matmul(SCall[:, half + 512:half + 1024],
                                        ksb[64:128, pr, kt * 128:(kt + 1) * 128],
                                        qsb[64:128, pr, qc * 512:(qc + 1) * 512],
                                        start=True, stop=True,
                                        tile_position=(64, 0))
                    pe_sc_of[skt] = p.inc(last, "pe")

            def emit_exp(skt):
                @blk.scalar
                def _(s, pr=pr, qc=qc, skt=skt):
                    s.wait_ge(p.sems["pe"], pe_sc_of[skt])
                    if skt >= 2:
                        s.wait_ge(p.sems["pe"], pe_av_of[skt - 2])
                    elif not (pr == 0 and qc == 0):
                        s.wait_ge(p.sems["e2"], e2_prev)
                    ins = s.activation(out=attT[:, skt % 2, :],
                                       in_=SCall[:, :], func=AF.Exp,
                                       bias=_f(-M), scale=_f(0.125))
                    ac_of[skt] = p.inc(ins, "ac")

            def emit_masks(skt):
                @blk.vector
                def _(v, qc=qc, skt=skt):
                    v.wait_ge(p.sems["ac"], ac_of[skt])
                    for kt in (2 * skt, 2 * skt + 1):
                        d = kt - 4 * qc
                        ins = v.tensor_tensor(
                            out=attT[:, skt % 2, (kt % 2) * 1024:(kt % 2) * 1024 + 1024],
                            in0=attT[:, skt % 2, (kt % 2) * 1024:(kt % 2) * 1024 + 1024],
                            in1=msk_sb[:, d, :], op=ALU.mult)
                    ve_msk_of[skt] = p.inc(ins, "ve")

            def emit_av_pair(skt):
                diag = causal and (2 * skt >= 4 * qc)

                @blk.tensor
                def _(t, pr=pr, qc=qc, skt=skt, diag=diag):
                    if diag:
                        t.wait_ge(p.sems["ve"], ve_msk_of[skt])
                    else:
                        t.wait_ge(p.sems["ac"], ac_of[skt])
                    if skt == 0 and not (pr == 0 and qc == 0):
                        t.wait_ge(p.sems["e2"], e2_prev)
                    last = None
                    for kt in (2 * skt, 2 * skt + 1):
                        st = (kt == 0)
                        sp = (kt == ktmax - 1)
                        a0 = attT[:, skt % 2, (kt % 2) * 1024:(kt % 2) * 1024 + 512]
                        a1 = attT[:, skt % 2,
                                  (kt % 2) * 1024 + 512:(kt % 2) * 1024 + 1024]
                        t.matmul(PAV[0:64, :], vsb[:, kt, pr * 128:pr * 128 + 64],
                                 a0, start=st, stop=sp, tile_position=(0, 0))
                        t.matmul(PAVB[64:128, :],
                                 vsb[:, kt, pr * 128 + 64:pr * 128 + 128],
                                 a1, start=st, stop=sp, tile_position=(0, 64))
                        t.matmul(PDEN[0:1, :], ones_c[:, :], a0,
                                 start=st, stop=sp, tile_position=(0, 0))
                        last = t.matmul(PDENB[0:1, :], ones_c[:, :], a1,
                                        start=st, stop=sp, tile_position=(0, 0))
                    pe_av_of[skt] = p.inc(last, "pe")

            for skt in range(nsk):
                emit_sc_pair(skt)
                emit_exp(skt)
                if causal and 2 * skt >= 4 * qc:
                    emit_masks(skt)
                if skt >= 1:
                    emit_av_pair(skt - 1)
            emit_av_pair(nsk - 1)

            pe_av = p.val("pe")

            @blk.vector
            def _(v, pe_av=pe_av):
                v.wait_ge(p.sems["pe"], pe_av)
                p.inc(v.reciprocal(out=rcp[0:1, 0, :], in_=PDEN[0:1, :]), "ve")
                p.inc(v.reciprocal(out=rcp[0:1, 1, :], in_=PDENB[0:1, :]), "ve")

            ve_rc = p.val("ve")

            @blk.tensor
            def _(t, ve_rc=ve_rc):
                t.wait_ge(p.sems["ve"], ve_rc)
                t.matmul(SCall[0:64, 0:512], ones64[0:1, :], rcp[0:1, 0, :],
                         start=True, stop=True, tile_position=(0, 0))
                ins = t.matmul(SCall[64:128, 1024:1536], ones64[0:1, :], rcp[0:1, 1, :],
                               start=True, stop=True, tile_position=(0, 64))
                p.inc(ins, "pe")

            pe_bc = p.val("pe")

            @blk.vector
            def _(v, pr=pr, qc=qc, pe_bc=pe_bc):
                v.wait_ge(p.sems["pe"], pe_bc)
                p.inc(v.tensor_copy(out=avf[0:64, :], in_=PAV[0:64, :]), "ve")
                p.inc(v.tensor_copy(out=avf[64:128, :], in_=PAVB[64:128, :]), "ve")
                ins = v.tensor_tensor(out=ysb[0:64, pr, qc * 512:(qc + 1) * 512],
                                      in0=avf[0:64, :], in1=SCall[0:64, 0:512],
                                      op=ALU.mult)
                p.inc(ins, "e2")
                ins = v.tensor_tensor(out=ysb[64:128, pr, qc * 512:(qc + 1) * 512],
                                      in0=avf[64:128, :], in1=SCall[64:128, 1024:1536],
                                      op=ALU.mult)
                p.inc(ins, "e2")



def build_ATT(causal, M):
    p = Prog()
    nc = p.nc
    xnT_i = p.dram_in("xnT", [128, NCH, SEQ], BF16)
    wq_i = p.dram_in("wq", [128, NCH, 256])
    wk_i = p.dram_in("wk", [128, NCH, 256])
    wv_i = p.dram_in("wv", [128, NCH, 256])
    msk_i = p.dram_in("msk", [128, 4, 1024], BF16)
    y_o = p.dram_out("y", [128, 2, SEQ], BF16)

    qsb = p.sb("qsb", [128, 2, SEQ], BF16)
    ksb = p.sb("ksb", [128, 2, SEQ], BF16)
    vsb = p.sb("vsb", [128, NKT, 256], BF16)
    ysb = p.sb("ysb", [128, 2, SEQ], BF16)
    xnc = p.sb("xnc", [128, 2, NCH, 512], BF16)
    wq_sb = p.sb("wqs", [128, NCH, 256], BF16)
    wk_sb = p.sb("wks", [128, NCH, 256], BF16)
    wv_sb = p.sb("wvs", [128, NCH, 256], BF16)
    msk_sb = p.sb("msks", [128, 4, 1024], BF16)
    attT = p.sb("attT", [128, 2, 2048], BF16)
    avf = p.sb("avf", [128, 512])
    rcp = p.sb("rcp", [1, 2, 512])
    ones_c = p.sb("ones_c", [128, 1], BF16)
    ones64 = p.sb("ones64", [64, 64])

    SCall = p.ps("SCall", [128, 2048])
    P0 = _ColView(SCall, 0)
    P1 = _ColView(SCall, 512)
    P2 = _ColView(SCall, 1024)
    P3 = _ColView(SCall, 1536)
    PAV = p.ps("PAV", [128, 512])
    PAVB = p.ps("PAVB", [128, 512])
    PDEN = p.ps("PDEN", [128, 512])
    PDENB = p.ps("PDENB", [128, 512])

    with nc.Block() as blk:
        @blk.gpsimd
        def _(g):
            g.dma_start(out=wq_sb[:, :, :], in_=wq_i[:, :, :]).then_inc(p.sems["wd"], 16)
            g.dma_start(out=wk_sb[:, :, :], in_=wk_i[:, :, :]).then_inc(p.sems["wd"], 16)
            g.dma_start(out=wv_sb[:, :, :], in_=wv_i[:, :, :]).then_inc(p.sems["wd"], 16)
            p.cnt["wd"] += 48
            ins = g.memset(ones_c[:, :], 1.0)
            p.inc(ins, "wd", 1)
            ins = g.memset(ones64[:, :], 1.0)
            p.inc(ins, "wd", 1)

        @blk.sync
        def _(sync):
            ins = sync.dma_start(out=msk_sb[:, :, :], in_=msk_i[:, :, :])
            p.inc(ins, "ld", 16)

        evA_base = p.val("ev")
        for qc in range(16):
            @blk.sync
            def _(sync, qc=qc):
                if qc >= 2:
                    sync.wait_ge(p.sems["ev"], evA_base + 8 * (qc - 1))
                ins = sync.dma_start(out=xnc[:, qc % 2, :, :],
                                     in_=xnT_i[:, :, qc * 512:(qc + 1) * 512])
                p.inc(ins, "ld", 16)

            ld_q = p.val("ld")

            for which, wsb, PA_pair in [("q", wq_sb, (P0, P1)), ("k", wk_sb, (P2, P3))]:
                for fj in range(2):
                    PA = PA_pair[fj]

                    @blk.tensor
                    def _(t, qc=qc, fj=fj, wsb=wsb, PA=PA, ld_q=ld_q):
                        t.wait_ge(p.sems["ld"], ld_q)
                        t.wait_ge(p.sems["wd"], 50)
                        if qc >= 1:
                            t.wait_ge(p.sems["ev"], evA_base + 8 * qc)
                        last = None
                        for ci in range(NCH):
                            last = t.matmul(PA[:, :], wsb[:, ci, fj * 128:(fj + 1) * 128],
                                            xnc[:, qc % 2, ci, :],
                                            start=(ci == 0), stop=(ci == NCH - 1))
                        p.inc(last, "pe")

                    pe_q = p.val("pe")

                    @blk.vector
                    def _(v, qc=qc, fj=fj, which=which, PA=PA, pe_q=pe_q):
                        v.wait_ge(p.sems["pe"], pe_q)
                        dst = qsb if which == "q" else ksb
                        ins = v.tensor_copy(out=dst[:, fj, qc * 512:(qc + 1) * 512],
                                            in_=PA[:, :])
                        p.inc(ins, "ev")

            for nt in range(4):
                PA = [P0, P1, P2, P3][nt]

                @blk.tensor
                def _(t, qc=qc, nt=nt, PA=PA):
                    t.wait_ge(p.sems["ev"], evA_base + 8 * qc + 4)
                    last = None
                    for ci in range(NCH):
                        last = t.matmul(PA[:, 0:256],
                                        xnc[:, qc % 2, ci, nt * 128:(nt + 1) * 128],
                                        wv_sb[:, ci, :],
                                        start=(ci == 0), stop=(ci == NCH - 1))
                    p.inc(last, "pe")

                pe_v = p.val("pe")

                @blk.vector
                def _(v, qc=qc, nt=nt, PA=PA, pe_v=pe_v):
                    v.wait_ge(p.sems["pe"], pe_v)
                    ins = v.tensor_copy(out=vsb[:, qc * 4 + nt, :], in_=PA[:, 0:256])
                    p.inc(ins, "ev")

        phaseA_done = p.val("ev")

        emit_phaseB(p, blk, causal, M, qsb, ksb, vsb, msk_sb, attT, avf, rcp,
                    ysb, ones_c, ones64, SCall, PAV, PAVB, PDEN, PDENB,
                    "ev", phaseA_done, 16)

        yB_done = p.val("e2")

        @blk.sync
        def _(sync):
            sync.wait_ge(p.sems["e2"], yB_done)
            ins = sync.dma_start(out=y_o[:, :, :], in_=ysb[:, :, :])
            p.inc(ins, "st", 16)
            sync.wait_ge(p.sems["st"], p.val("st"))
    return p


def prep_ATT_L1aa(I, core, xnT_full):
    m = core % 4
    w = np.asarray(I["L1_attn_a_w_qkv"], np.float32)
    cols = slice(m * 256, (m + 1) * 256)
    return dict(
        xnT=xnT_full,
        wq=_chunked(np.ascontiguousarray(w[:, 0:1024][:, cols])),
        wk=_chunked(np.ascontiguousarray(w[:, 1024:2048][:, cols])),
        wv=_chunked(np.ascontiguousarray(w[:, 2048:3072][:, cols])),
        msk=_masks_bf(),
    )


def prep_ATT_L1ab(I, core, xnT_full):
    m = core % 4
    wt = np.ascontiguousarray(np.asarray(I["L1_attn_b_w_qkv"], np.float32).T)
    cols = slice(m * 256, (m + 1) * 256)
    return dict(
        xnT=xnT_full,
        wq=_chunked(np.ascontiguousarray(wt[:, 0:1024][:, cols])),
        wk=_chunked(np.ascontiguousarray(wt[:, 1024:2048][:, cols])),
        wv=_chunked(np.ascontiguousarray(wt[:, 2048:3072][:, cols])),
        msk=_masks_bf(),
    )


def _masks_bf():
    kk = np.arange(128)[:, None]
    qq = np.arange(512)[None, :]
    m = np.stack([(kk + d <= qq) for d in (0, 128, 256, 384)]).astype(BF)
    m = np.ascontiguousarray(m.transpose(1, 0, 2))
    return np.ascontiguousarray(np.concatenate([m, m], axis=2))



def build_PROJ():
    p = Prog()
    nc = p.nc
    xs = p.dram_in("xs", [128, NT, D])
    ysl_i = p.dram_in("ysl", [128, NCH, TOK], BF16)
    wp_i = p.dram_in("wp", [128, NCH, D])
    x_o = p.dram_out("x_out", [128, NT, D])
    ps_o = p.dram_out("psums", [1, D])

    x_sb = p.sb("x", [128, NT, D])
    ysl = p.sb("ysl_s", [128, NCH, TOK], BF16)
    wp = p.sb("wp_s", [128, NCH, D], BF16)
    xn_sb = p.sb("xn", [128, NT, D], BF16)
    stats = p.sb("stats", [128, 128])
    scr = p.sb("scr", [128, D])
    ones_c = p.sb("ones_c", [128, 1], BF16)
    pss = p.sb("pss", [1, D])
    P0 = p.ps("P0", [128, D])
    P1 = p.ps("P1", [128, D])

    with nc.Block() as blk:
        @blk.sync
        def _(sync):
            sync.dma_start(out=x_sb[:, :, :], in_=xs[:, :, :]).then_inc(p.sems["ld"], 16)
            ins = sync.dma_start(out=ysl[:, :, :], in_=ysl_i[:, :, :])
            p.inc(ins, "ld", 16)
            p.cnt["ld"] += 16

        @blk.gpsimd
        def _(g):
            g.dma_start(out=wp[:, :, :], in_=wp_i[:, :, :]).then_inc(p.sems["wd"], 16)
            p.cnt["wd"] += 16
            ins = g.memset(ones_c[:, :], 1.0)
            p.inc(ins, "wd", 1)

        ev_base = p.val("ev")
        for nt in range(NT):
            PA = [P0, P1][nt % 2]

            @blk.tensor
            def _(t, nt=nt, PA=PA):
                t.wait_ge(p.sems["ld"], 32)
                t.wait_ge(p.sems["wd"], 17)
                if nt >= 2:
                    t.wait_ge(p.sems["ev"], ev_base + nt - 1)
                last = None
                for ci in range(NCH):
                    last = t.matmul(PA[:, 0:512], ysl[:, ci, nt * 128:(nt + 1) * 128],
                                    wp[:, ci, 0:512], start=(ci == 0), stop=(ci == NCH - 1))
                for ci in range(NCH):
                    last = t.matmul(PA[:, 512:1024], ysl[:, ci, nt * 128:(nt + 1) * 128],
                                    wp[:, ci, 512:1024], start=(ci == 0), stop=(ci == NCH - 1))
                p.inc(last, "pe")

            pe_p = p.val("pe")

            @blk.vector
            def _(v, nt=nt, PA=PA, pe_p=pe_p):
                v.wait_ge(p.sems["pe"], pe_p)
                ins = v.tensor_tensor(out=x_sb[:, nt, :], in0=PA[:, :],
                                      in1=x_sb[:, nt, :], op=ALU.add)
                p.inc(ins, "ev")

        x_done = p.val("ev")

        @blk.sync
        def _(sync):
            sync.wait_ge(p.sems["ev"], x_done)
            ins = sync.dma_start(out=x_o[:, :, :], in_=x_sb[:, :, :])
            p.inc(ins, "st", 16)

        emit_ln(p, blk, x_sb, xn_sb, "ev", x_done, "ev", stats, scr)
        ln_done = p.val("ev")

        @blk.tensor
        def _(t):
            t.wait_ge(p.sems["ev"], ln_done)
            last = None
            for half in range(2):
                for nt in range(NT):
                    last = t.matmul(P0[0:1, half * 512:(half + 1) * 512], ones_c[:, :],
                                    xn_sb[:, nt, half * 512:(half + 1) * 512],
                                    start=(nt == 0), stop=(nt == NT - 1))
            p.inc(last, "pe")

        pe_s = p.val("pe")

        @blk.vector
        def _(v):
            v.wait_ge(p.sems["pe"], pe_s)
            ins = v.tensor_copy(out=pss[:, :], in_=P0[0:1, 0:1024])
            p.inc(ins, "e2")

        @blk.sync
        def _(sync):
            sync.wait_ge(p.sems["e2"], p.val("e2"))
            ins = sync.dma_start(out=ps_o[:, :], in_=pss[:, :])
            p.inc(ins, "st", 16)
            sync.wait_ge(p.sems["st"], p.val("st"))
    return p




def prep_PROJ(x_slice, ysl, wmat):
    return dict(xs=_tokmajor(x_slice), ysl=ysl, wp=_chunked(np.asarray(wmat, np.float32)))


def _yfeat(y_cores_of_batch):
    y_feat = np.empty((1024, SEQ), BF)
    for m, yd in enumerate(y_cores_of_batch):
        yd = np.asarray(yd).view(np.uint16).view(BF)
        for hl in range(4):
            h = 4 * m + hl
            y_feat[h * 64:(h + 1) * 64] = yd[(hl % 2) * 64:(hl % 2) * 64 + 64, hl // 2, :]
    return y_feat


def _ysl_from_yfeat(y_feat, m):
    sl = y_feat[:, m * TOK:(m + 1) * TOK]
    return np.ascontiguousarray(sl.reshape(NCH, 128, TOK).transpose(1, 0, 2))



def build_PF(want_ln2=True):
    p = Prog()
    nc = p.nc
    xs = p.dram_in("xs", [128, NT, D])
    ysl_i = p.dram_in("ysl", [128, NCH, TOK], BF16)
    wp_i = p.dram_in("wp", [128, NCH, D])
    rw_i = p.dram_in("rwT", [128, NCH, 512])
    rb_i = p.dram_in("rb", [1, 512])
    lng_i = p.dram_in("lng2", [128, NCH])
    lnb_i = p.dram_in("lnb2", [128, NCH])
    ut_i = p.dram_in("UT", [128, NCH, 512])
    vt_i = p.dram_in("VT", [128, NCH, 512])
    vr_i = p.dram_in("Vr", [128, 4, D])
    gam_i = p.dram_in("gam", [1, D])
    w1_i = p.dram_in("w1T", [128, NCH, 4096])
    w2_i = p.dram_in("w2T", [128, 32, D])
    idm = p.dram_in("idm", [128, 128], BF16)
    x_o = p.dram_out("x_out", [128, NT, D])
    xnT_o = p.dram_out("xnT_out", [128, NCH, TOK], BF16)
    xnT2_o = p.dram_out("xnT2_out", [128, NCH, TOK], BF16) if want_ln2 else None

    xio = p.sb("xio", [128, 2, D])
    yst = p.sb("yst", [128, 2, NCH, 128], BF16)
    stage = p.sb("stage", [128, 2, D], BF16)
    stats = p.sb("stats", [128, 128])
    scr = p.sb("scr", [128, D])
    xnt2 = p.sb("xnt2", [128, 2, NCH, 128], BF16)
    hzT = p.sb("hzT", [128, 4, TOK], BF16)
    ut_sb = p.sb("uts", [128, NCH, 512], BF16)
    rw_sb = p.sb("rws", [128, NCH, 512], BF16)
    vg_sb = p.sb("vgs", [128, 4, D], BF16)
    rsb = p.sb("rsb", [128, 512])
    alp = p.sb("alp", [128, 512])
    brow_bc = p.sb("brow_bc", [128, 512])
    rnuv_bc = p.sb("rnuv_bc", [128, 512])
    rowp = p.sb("rowp", [1, 3, 512])
    gam_sb = p.sb("gam_s", [1, D])
    lng_sb = p.sb("lng2s", [128, NCH])
    lnb_sb = p.sb("lnb2s", [128, NCH], BF16)
    r2c = p.sb("r2c", [128, 16])
    phi = p.sb("phi", [128, 8])
    svec = p.sb("svec", [128, 4])
    id_sb = p.sb("idsb", [128, 128], BF16)
    ones_cf = p.sb("ones_cf", [128, 1])
    ones_fr = p.sb("ones_fr", [1, 128])
    tg1 = scr[:, 0:512]
    tg2 = scr[:, 512:1024]

    wp_sb, wp_cm = p.sb_scoped("wps", [128, NCH, D], BF16)
    vt_sb, vt_cm = p.sb_scoped("vts", [128, NCH, 512], BF16)
    vgf, vgf_cm = p.sb_scoped("vgf", [128, 4, D])

    P01 = p.ps("P01", [128, D])
    P02 = p.ps("P02", [128, D])
    PR = p.ps("PR", [128, 512])
    PH = p.ps("PH", [128, 512])
    PT = [p.ps("PTa", [128, 512], BF16), p.ps("PTb", [128, 512], BF16)]

    with nc.Block() as blk:
        @blk.gpsimd
        def _(g):
            for dst, srci in [(wp_sb, wp_i), (ut_sb, ut_i), (vt_sb, vt_i),
                              (rw_sb, rw_i)]:
                g.dma_start(out=dst[:, :, :], in_=srci[:, :, :]).then_inc(p.sems["wd"], 16)
                p.cnt["wd"] += 16
            g.dma_start(out=lnb_sb[:, :], in_=lnb_i[:, :]).then_inc(p.sems["wd"], 16)
            p.cnt["wd"] += 16
            g.memset(ones_cf[:, :], 1.0)
            ins = g.memset(ones_fr[:, :], 1.0)
            p.inc(ins, "wd", 1)

        wd_ld = p.val("wd")

        @blk.sync
        def _(sync):
            sync.dma_start(out=id_sb[:, :], in_=idm[:, :]).then_inc(p.sems["ld"], 16)
            sync.dma_start(out=rowp[0:1, 0, :], in_=rb_i[:, :]).then_inc(p.sems["ld"], 16)
            sync.dma_start(out=lng_sb[:, :], in_=lng_i[:, :]).then_inc(p.sems["ld"], 16)
            sync.dma_start(out=gam_sb[:, :], in_=gam_i[:, :]).then_inc(p.sems["ld"], 16)
            ins = sync.dma_start(out=vgf[:, :, :], in_=vr_i[:, :, :])
            p.inc(ins, "ld", 16)
            p.cnt["ld"] += 64

        ld0 = p.val("ld")

        ve_add = {}
        ve_sq = {}
        pe_proj = {}
        st_store = {}
        ld_x = {}
        ld_y = {}
        for nt in range(NT):
            @blk.sync
            def _(sync, nt=nt):
                if nt >= 2:
                    sync.wait_ge(p.sems["st"], st_store[nt - 2])
                    sync.wait_ge(p.sems["ve"], ve_sq[nt - 2])
                ins = sync.dma_start(out=xio[:, nt % 2, :], in_=xs[:, nt, :])
                ld_x[nt] = p.inc(ins, "ld", 16)
                if nt >= 2:
                    sync.wait_ge(p.sems["pe"], pe_proj[nt - 2])
                ins = sync.dma_start(out=yst[:, nt % 2, :, :],
                                     in_=ysl_i[:, :, nt * 128:(nt + 1) * 128])
                ld_y[nt] = p.inc(ins, "ld", 16)

            @blk.tensor
            def _(t, nt=nt):
                t.wait_ge(p.sems["ld"], ld_y[nt])
                t.wait_ge(p.sems["wd"], wd_ld)
                if nt >= 2:
                    t.wait_ge(p.sems["ve"], ve_add[nt - 2])
                PA = [P01, P02][nt % 2]
                last = None
                for half in range(2):
                    for ci in range(NCH):
                        last = t.matmul(PA[:, half * 512:(half + 1) * 512],
                                        yst[:, nt % 2, ci, :],
                                        wp_sb[:, ci, half * 512:(half + 1) * 512],
                                        start=(ci == 0), stop=(ci == NCH - 1))
                pe_proj[nt] = p.inc(last, "pe")

            @blk.vector
            def _(v, nt=nt):
                v.wait_ge(p.sems["pe"], pe_proj[nt])
                v.wait_ge(p.sems["ld"], ld_x[nt])
                PA = [P01, P02][nt % 2]
                ins = v.tensor_tensor(out=xio[:, nt % 2, :], in0=PA[:, :],
                                      in1=xio[:, nt % 2, :], op=ALU.add)
                ve_add[nt] = p.inc(ins, "ve")
                v.wait_ge(p.sems["ve"], ve_add[nt])
                p.inc(v.tensor_reduce(out=stats[:, nt:nt + 1], in_=xio[:, nt % 2, :],
                                      axis=AX.X, op=ALU.add), "ve")
                ins = v.tensor_tensor(out=scr[:, :], in0=xio[:, nt % 2, :],
                                      in1=xio[:, nt % 2, :], op=ALU.mult)
                v.wait_ge(p.sems["ve"], p.inc(ins, "ve"))
                ins = v.tensor_reduce(out=stats[:, 16 + nt:17 + nt], in_=scr[:, :],
                                      axis=AX.X, op=ALU.add)
                ve_sq[nt] = p.inc(ins, "ve")

            @blk.sync
            def _(sync, nt=nt):
                sync.wait_ge(p.sems["ve"], ve_add[nt])
                ins = sync.dma_start(out=x_o[:, nt, :], in_=xio[:, nt % 2, :])
                st_store[nt] = p.inc(ins, "st", 16)

        st_x5 = p.val("st")

        @blk.vector
        def _(v):
            v.wait_ge(p.sems["ve"], ve_sq[NT - 1])
            ins = v.tensor_scalar(out=stats[:, 32:48], in0=stats[:, 0:16],
                                  scalar1=1.0 / 1024.0, scalar2=0.0,
                                  op0=ALU.mult, op1=ALU.add)
            v.wait_ge(p.sems["ve"], p.inc(ins, "ve"))
            ins = v.tensor_tensor(out=stats[:, 48:64], in0=stats[:, 32:48],
                                  in1=stats[:, 32:48], op=ALU.mult)
            v.wait_ge(p.sems["ve"], p.inc(ins, "ve"))
            ins = v.tensor_scalar(out=stats[:, 64:80], in0=stats[:, 16:32],
                                  scalar1=1.0 / 1024.0, scalar2=0.0,
                                  op0=ALU.mult, op1=ALU.add)
            v.wait_ge(p.sems["ve"], p.inc(ins, "ve"))
            ins = v.tensor_tensor(out=stats[:, 80:96], in0=stats[:, 64:80],
                                  in1=stats[:, 48:64], op=ALU.subtract)
            p.inc(ins, "ve")

        @blk.scalar
        def _(s):
            s.wait_ge(p.sems["ve"], p.val("ve"))
            ins = s.activation(out=stats[:, 96:112], in_=stats[:, 80:96],
                               func=AF.Sqrt, bias=_f(1e-5), scale=_f(1.0))
            p.inc(ins, "ac")

        @blk.vector
        def _(v):
            v.wait_ge(p.sems["ac"], p.val("ac"))
            ins = v.reciprocal(out=stats[:, 48:64], in_=stats[:, 96:112])
            v.wait_ge(p.sems["ve"], p.inc(ins, "ve"))
            ins = v.tensor_scalar(out=stats[:, 112:128], in0=stats[:, 80:96],
                                  scalar1=1e-5, scalar2=0.0, op0=ALU.add, op1=ALU.add)
            v.wait_ge(p.sems["ve"], p.inc(ins, "ve"))
            ins = v.reciprocal(out=stats[:, 112:128], in_=stats[:, 112:128])
            v.wait_ge(p.sems["ve"], p.inc(ins, "ve"))
            ins = v.tensor_tensor(out=stats[:, 112:128], in0=stats[:, 80:96],
                                  in1=stats[:, 112:128], op=ALU.mult)
            p.inc(ins, "ve")

        @blk.scalar
        def _(s):
            s.wait_ge(p.sems["ve"], p.val("ve"))
            ins = s.activation(out=r2c[:, :], in_=stats[:, 112:128], func=AF.Sqrt,
                               bias=_f(1e-5), scale=_f(1.0))
            p.inc(ins, "ac")

        @blk.vector
        def _(v):
            v.wait_ge(p.sems["ac"], p.val("ac"))
            ins = v.reciprocal(out=r2c[:, :], in_=r2c[:, :])
            p.inc(ins, "ve")

        ve_r2 = p.val("ve")

        @blk.tensor
        def _(t):
            t.wait_ge(p.sems["wd"], wd_ld)
            last = None
            for ci in range(NCH):
                last = t.matmul(PR[0:1, :], lnb_sb[:, ci:ci + 1], rw_sb[:, ci, :],
                                start=(ci == 0), stop=(ci == NCH - 1))
            p.inc(last, "pe")

        pe_brow = p.val("pe")

        @blk.vector
        def _(v):
            v.wait_ge(p.sems["pe"], pe_brow)
            v.wait_ge(p.sems["ld"], ld0)
            ins = v.tensor_tensor(out=rowp[0:1, 1, :], in0=PR[0:1, :],
                                  in1=rowp[0:1, 0, :], op=ALU.add)
            v.wait_ge(p.sems["ve"], p.inc(ins, "ve"))
            for ci in range(NCH):
                p.inc(v.tensor_scalar(out=rw_sb[:, ci, :], in0=rw_sb[:, ci, :],
                                      scalar1=lng_sb[:, ci:ci + 1], scalar2=0.0,
                                      op0=ALU.mult, op1=ALU.add), "ve")

        ve_rw = p.val("ve")

        ac_norm = {}
        pe_nsq_prev = None
        for wi, (wsb, dst_row) in enumerate([(ut_sb, 2), (vt_sb, 0)]):
            pe_nsq = {}
            ve_sqr = {}
            for ci in range(NCH):
                @blk.vector
                def _(v, ci=ci, wsb=wsb, wi=wi):
                    if ci >= 2:
                        v.wait_ge(p.sems["pe"], pe_nsq[ci - 2])
                    elif wi == 1 and pe_nsq_prev is not None:
                        v.wait_ge(p.sems["pe"], pe_nsq_prev[NCH - 1])
                    ins = v.tensor_tensor(out=scr[:, (ci % 2) * 512:(ci % 2 + 1) * 512],
                                          in0=wsb[:, ci, :], in1=wsb[:, ci, :],
                                          op=ALU.mult)
                    ve_sqr[ci] = p.inc(ins, "ve")

                @blk.tensor
                def _(t, ci=ci, wi=wi):
                    t.wait_ge(p.sems["ve"], ve_sqr[ci])
                    if wi == 1 and ci == 0:
                        t.wait_ge(p.sems["ac"], ac_norm[0])
                    ins = t.matmul(PH[0:1, :], ones_cf[:, :],
                                   scr[:, (ci % 2) * 512:(ci % 2 + 1) * 512],
                                   start=(ci == 0), stop=(ci == NCH - 1))
                    pe_nsq[ci] = p.inc(ins, "pe")

            pe_nsq_prev = pe_nsq

            @blk.scalar
            def _(s, wi=wi, dst_row=dst_row, pe_nsq=pe_nsq):
                s.wait_ge(p.sems["pe"], pe_nsq[NCH - 1])
                ins = s.activation(out=rowp[0:1, dst_row, :], in_=PH[0:1, :],
                                   func=AF.Sqrt)
                ac_norm[wi] = p.inc(ins, "ac")

            @blk.vector
            def _(v, wi=wi, dst_row=dst_row):
                v.wait_ge(p.sems["ac"], ac_norm[wi])
                ins = v.tensor_scalar(out=rowp[0:1, dst_row, :],
                                      in0=rowp[0:1, dst_row, :],
                                      scalar1=1e-6, scalar2=0.0,
                                      op0=ALU.max, op1=ALU.add)
                v.wait_ge(p.sems["ve"], p.inc(ins, "ve"))
                ins = v.reciprocal(out=rowp[0:1, dst_row, :], in_=rowp[0:1, dst_row, :])
                v.wait_ge(p.sems["ve"], p.inc(ins, "ve"))

        @blk.vector
        def _(v):
            ins = v.tensor_tensor(out=rowp[0:1, 2, :], in0=rowp[0:1, 2, :],
                                  in1=rowp[0:1, 0, :], op=ALU.mult)
            v.wait_ge(p.sems["ve"], p.inc(ins, "ve"))

        ve_rows = p.val("ve")

        @blk.tensor
        def _(t):
            t.wait_ge(p.sems["ve"], ve_rows)
            t.matmul(PR[:, :], ones_fr[:, :], rowp[0:1, 1, :], start=True, stop=True)
            ins = t.matmul(PH[:, :], ones_fr[:, :], rowp[0:1, 2, :], start=True,
                           stop=True)
            p.inc(ins, "pe")

        pe_bc = p.val("pe")

        @blk.vector
        def _(v):
            v.wait_ge(p.sems["pe"], pe_bc)
            p.inc(v.tensor_copy(out=brow_bc[:, :], in_=PR[:, :]), "ve")
            p.inc(v.tensor_copy(out=rnuv_bc[:, :], in_=PH[:, :]), "ve")

        ve_bc = p.val("ve")

        @blk.tensor
        def _(t):
            t.wait_ge(p.sems["ld"], ld0)
            t.wait_ge(p.sems["ve"], ve_add[NT - 1])
            t.matmul(P01[:, 0:512], ones_fr[:, :], gam_sb[0:1, 0:512],
                     start=True, stop=True)
            ins = t.matmul(P01[:, 512:1024], ones_fr[:, :], gam_sb[0:1, 512:1024],
                           start=True, stop=True)
            p.inc(ins, "pe")

        pe_gam = p.val("pe")

        @blk.vector
        def _(v):
            v.wait_ge(p.sems["pe"], pe_gam)
            for jc in range(4):
                p.inc(v.tensor_tensor(out=vg_sb[:, jc, :], in0=vgf[:, jc, :],
                                      in1=P01[:, :], op=ALU.mult), "ve")

        ve_vg = p.val("ve")
        pe_ph2 = p.val("pe")

        vgf_cm.__exit__(None, None, None)
        vt_cm.__exit__(None, None, None)
        wp_cm.__exit__(None, None, None)

        hid = p.sb("hid", [128, 8, 1024], BF16)
        w2q = p.sb("w2q", [128, 2, 8, 1024], BF16)
        xnq = p.sb("xnq", [128, NCH, 1024], BF16)
        acc = p.sb("acc", [128, 8, 1024])
        w1st = p.sb("w1st", [128, 2, NCH, 128], BF16)

        e2_base = p.val("e2")
        ve_apply = {}
        ve_rstt = {}
        ve_hz = {}
        pe_rh = {}
        st_xnt = {}
        e2_u = {}

        def pt_use(u):
            return PT[u % 2], e2_base + u - 1

        for nt in range(NT):
            @blk.sync
            def _(sync, nt=nt):
                sync.wait_ge(p.sems["st"], st_x5)
                if nt >= 2:
                    sync.wait_ge(p.sems["ve"], ve_apply[nt - 2])
                else:
                    sync.wait_ge(p.sems["ve"], ve_r2)
                ins = sync.dma_start(out=xio[:, nt % 2, :], in_=x_o[:, nt, :])
                ld_x[("p3", nt)] = p.inc(ins, "ld", 16)

            @blk.vector
            def _(v, nt=nt):
                v.wait_ge(p.sems["ld"], ld_x[("p3", nt)])
                v.wait_ge(p.sems["ve"], ve_r2)
                if nt >= 2:
                    v.wait_ge(p.sems["e2"], e2_u[(nt - 2, 2)])
                ins = v.tensor_scalar(out=stage[:, nt % 2, :], in0=xio[:, nt % 2, :],
                                      scalar1=stats[:, 32 + nt:33 + nt],
                                      scalar2=stats[:, 48 + nt:49 + nt],
                                      op0=ALU.subtract, op1=ALU.mult)
                ve_apply[nt] = p.inc(ins, "ve")

            for g in range(2):
                u = 3 * nt + g
                bank, free_e2 = pt_use(u)

                @blk.tensor
                def _(t, nt=nt, g=g, u=u, bank=bank, free_e2=free_e2):
                    t.wait_ge(p.sems["ve"], ve_apply[nt])
                    if u >= 2:
                        t.wait_ge(p.sems["e2"], free_e2)
                    last = None
                    for j in range(4):
                        last = t.transpose(bank[:, j * 128:(j + 1) * 128],
                                           stage[:, nt % 2, (4 * g + j) * 128:
                                                 (4 * g + j + 1) * 128],
                                           id_sb[:, :])
                    p.inc(last, "pe")

                pe_tr = p.val("pe")

                @blk.vector
                def _(v, nt=nt, g=g, bank=bank, pe_tr=pe_tr):
                    v.wait_ge(p.sems["pe"], pe_tr)
                    if nt >= 2:
                        v.wait_ge(p.sems["pe"], pe_rh[nt - 2])
                        v.wait_ge(p.sems["st"], st_xnt[(nt - 2, g)])
                    ins = v.tensor_copy(out=xnt2[:, nt % 2, 4 * g:4 * g + 4, :],
                                        in_=bank[:, :])
                    e2_u[(nt, g)] = p.inc(ins, "e2")

                @blk.sync
                def _(sync, nt=nt, g=g):
                    sync.wait_ge(p.sems["e2"], e2_u[(nt, g)])
                    ins = sync.dma_start(
                        out=xnT_o[:, 4 * g:4 * g + 4, nt * 128:(nt + 1) * 128],
                        in_=xnt2[:, nt % 2, 4 * g:4 * g + 4, :])
                    st_xnt[(nt, g)] = p.inc(ins, "st", 16)

            @blk.tensor
            def _(t, nt=nt):
                t.wait_ge(p.sems["e2"], e2_u[(nt, 1)])
                t.wait_ge(p.sems["ve"], ve_rw)
                if nt == 0:
                    t.wait_ge(p.sems["ve"], ve_bc)
                if nt >= 1:
                    t.wait_ge(p.sems["ve"], ve_rstt[nt - 1])
                last = None
                for ci in range(NCH):
                    last = t.matmul(PR[:, :], xnt2[:, nt % 2, ci, :], rw_sb[:, ci, :],
                                    start=(ci == 0), stop=(ci == NCH - 1))
                p.inc(last, "pe")
                if nt >= 1:
                    t.wait_ge(p.sems["ve"], ve_hz[nt - 1])
                last = None
                for ci in range(NCH):
                    last = t.matmul(PH[:, :], xnt2[:, nt % 2, ci, :], ut_sb[:, ci, :],
                                    start=(ci == 0), stop=(ci == NCH - 1))
                pe_rh[nt] = p.inc(last, "pe")

            @blk.vector
            def _(v, nt=nt):
                v.wait_ge(p.sems["pe"], pe_rh[nt] - 1)
                v.wait_ge(p.sems["ve"], ve_bc)
                ins = v.scalar_tensor_tensor(out=rsb[:, :], in0=PR[:, :],
                                             scalar=r2c[:, nt:nt + 1],
                                             in1=brow_bc[:, :],
                                             op0=ALU.mult, op1=ALU.add)
                ve_rstt[nt] = p.inc(ins, "ve")
                v.wait_ge(p.sems["ve"], ve_rstt[nt])
                ins = v.tensor_scalar(out=rsb[:, :], in0=rsb[:, :], scalar1=10.0,
                                      scalar2=-10.0, op0=ALU.min, op1=ALU.max)
                v.wait_ge(p.sems["ve"], p.inc(ins, "ve"))

            ve_clip = p.val("ve")

            @blk.scalar
            def _(s, ve_clip=ve_clip):
                s.wait_ge(p.sems["ve"], ve_clip)
                ins = s.activation(out=alp[:, :], in_=rsb[:, :], func=AF.Exp)
                p.inc(ins, "ac")
                s.wait_ge(p.sems["ac"], p.val("ac"))
                ins = s.activation(out=rsb[:, :], in_=alp[:, :], func=AF.Ln,
                                   bias=_f(1.0), scale=_f(1.0))
                p.inc(ins, "ac")

            ac_alpha = p.val("ac")

            @blk.vector
            def _(v, ac_alpha=ac_alpha):
                v.wait_ge(p.sems["ac"], ac_alpha)
                ins = v.max(out=phi[:, :], in_=rsb[:, :])
                v.wait_ge(p.sems["ve"], p.inc(ins, "ve"))
                ins = v.tensor_reduce(out=svec[:, 0:1], in_=phi[:, :], axis=AX.X,
                                      op=ALU.add)
                v.wait_ge(p.sems["ve"], p.inc(ins, "ve"))

            ve_svec = p.val("ve")

            @blk.scalar
            def _(s, ve_svec=ve_svec):
                s.wait_ge(p.sems["ve"], ve_svec)
                ins = s.activation(out=svec[:, 1:2], in_=svec[:, 0:1], func=AF.Tanh)
                p.inc(ins, "ac")

            ac_tanh = p.val("ac")

            @blk.vector
            def _(v, nt=nt, ac_tanh=ac_tanh):
                v.wait_ge(p.sems["ac"], ac_tanh)
                ins = v.tensor_scalar(out=svec[:, 2:3], in0=svec[:, 0:1],
                                      scalar1=1e-6, scalar2=0.0,
                                      op0=ALU.add, op1=ALU.add)
                v.wait_ge(p.sems["ve"], p.inc(ins, "ve"))
                ins = v.reciprocal(out=svec[:, 2:3], in_=svec[:, 2:3])
                v.wait_ge(p.sems["ve"], p.inc(ins, "ve"))
                ins = v.tensor_tensor(out=svec[:, 1:2], in0=svec[:, 1:2],
                                      in1=svec[:, 2:3], op=ALU.mult)
                v.wait_ge(p.sems["ve"], p.inc(ins, "ve"))
                ins = v.tensor_scalar(out=tg1, in0=rsb[:, :],
                                      scalar1=phi[:, 7:8], scalar2=svec[:, 1:2],
                                      op0=ALU.is_ge, op1=ALU.mult)
                v.wait_ge(p.sems["ve"], p.inc(ins, "ve"))
                ins = v.tensor_tensor(out=tg2, in0=tg1, in1=rsb[:, :], op=ALU.mult)
                v.wait_ge(p.sems["ve"], p.inc(ins, "ve"))
                ins = v.tensor_tensor(out=tg1, in0=tg2, in1=rnuv_bc[:, :],
                                      op=ALU.mult)
                v.wait_ge(p.sems["ve"], p.inc(ins, "ve"))
                ins = v.tensor_tensor(out=stage[:, nt % 2, 0:512], in0=tg1,
                                      in1=PH[:, :], op=ALU.mult)
                ve_hz[nt] = p.inc(ins, "ve")

            u = 3 * nt + 2
            bank, free_e2 = pt_use(u)

            @blk.tensor
            def _(t, nt=nt, u=u, bank=bank, free_e2=free_e2):
                t.wait_ge(p.sems["ve"], ve_hz[nt])
                if u >= 2:
                    t.wait_ge(p.sems["e2"], free_e2)
                last = None
                for j in range(4):
                    last = t.transpose(bank[:, j * 128:(j + 1) * 128],
                                       stage[:, nt % 2, j * 128:(j + 1) * 128],
                                       id_sb[:, :])
                p.inc(last, "pe")

            pe_hztr = p.val("pe")

            @blk.vector
            def _(v, nt=nt, bank=bank, pe_hztr=pe_hztr):
                v.wait_ge(p.sems["pe"], pe_hztr)
                ins = v.tensor_copy(out=hzT[:, 0:4, nt * 128:(nt + 1) * 128],
                                    in_=bank[:, :])
                e2_u[(nt, 2)] = p.inc(ins, "e2")

        st_p3 = p.val("st")
        e2_p3 = p.val("e2")

        pe_hid = {}
        ac_hid = {}
        ve_sq2 = {}
        pe_ntq = {}
        ve_ev = {}
        st_x6 = {}
        ld_xnq = {}
        wd_w1 = {}
        wd_w2 = {}

        for qcb in range(2):
            @blk.sync
            def _(sync, qcb=qcb):
                sync.wait_ge(p.sems["st"], st_p3)
                if qcb == 1:
                    sync.wait_ge(p.sems["pe"], pe_hid[(0, 3, 7)])
                ins = sync.dma_start(out=xnq[:, :, :],
                                     in_=xnT_o[:, :, qcb * 1024:(qcb + 1) * 1024])
                ld_xnq[qcb] = p.inc(ins, "ld", 16)

            for q in range(4):
                Q = 4 * qcb + q

                @blk.gpsimd
                def _(g, qcb=qcb, q=q, Q=Q):
                    if Q == 0:
                        g.wait_ge(p.sems["ve"], ve_vg)
                        g.wait_ge(p.sems["pe"], pe_ph2)
                    if Q >= 2:
                        g.wait_ge(p.sems["pe"], pe_ntq[(Q - 2, 7)])
                    for k in range(8):
                        fh = q * 8 + k
                        ins = g.dma_start(out=w2q[:, q % 2, k, :],
                                          in_=w2_i[:, fh, :])
                        wd_w2[(qcb, q, k)] = p.inc(ins, "wd", 16)

                for k in range(8):
                    @blk.gpsimd
                    def _(g, qcb=qcb, q=q, Q=Q, k=k):
                        fh = q * 8 + k
                        if k < 2:
                            if Q >= 1:
                                prevq = (qcb, q - 1, k + 6) if q >= 1 else \
                                    (qcb - 1, 3, k + 6)
                                g.wait_ge(p.sems["pe"], pe_hid[prevq])
                        else:
                            g.wait_ge(p.sems["pe"], pe_hid[(qcb, q, k - 2)])
                        ins = g.dma_start(out=w1st[:, k % 2, :, :],
                                          in_=w1_i[:, :, fh * 128:(fh + 1) * 128])
                        wd_w1[(qcb, q, k)] = p.inc(ins, "wd", 16)

                    @blk.tensor
                    def _(t, qcb=qcb, q=q, Q=Q, k=k):
                        t.wait_ge(p.sems["wd"], wd_w1[(qcb, q, k)])
                        t.wait_ge(p.sems["ld"], ld_xnq[qcb])
                        if (Q, k) == (0, 0):
                            t.wait_ge(p.sems["ve"], ve_hz[NT - 1])
                        else:
                            prev = (qcb, q, k - 1) if k >= 1 else (
                                (qcb, q - 1, 7) if q >= 1 else (qcb - 1, 3, 7))
                            t.wait_ge(p.sems["ac"], ac_hid[prev])
                        last = None
                        for h2 in range(2):
                            PSel = [PR, PH][h2]
                            for ci in range(NCH):
                                last = t.matmul(PSel[:, :], w1st[:, k % 2, ci, :],
                                                xnq[:, ci, h2 * 512:(h2 + 1) * 512],
                                                start=(ci == 0), stop=(ci == NCH - 1))
                        pe_hid[(qcb, q, k)] = p.inc(last, "pe")

                    @blk.scalar
                    def _(s, qcb=qcb, q=q, k=k):
                        s.wait_ge(p.sems["pe"], pe_hid[(qcb, q, k)])
                        for h2 in range(2):
                            PSel = [PR, PH][h2]
                            ins = s.activation(out=hid[:, k, h2 * 512:(h2 + 1) * 512],
                                               in_=PSel[:, :], func=AF.Gelu)
                            ac_hid[(qcb, q, k)] = p.inc(ins, "ac")

                for ntq in range(8):
                    gtok = qcb * 1024 + ntq * 128

                    if q == 0:
                        @blk.sync
                        def _(sync, qcb=qcb, Q=Q, ntq=ntq):
                            if ntq >= 2:
                                sync.wait_ge(p.sems["ve"], ve_ev[(Q, ntq - 2)])
                            elif qcb == 0:
                                sync.wait_ge(p.sems["ve"], ve_apply[14 + ntq % 2])
                            else:
                                sync.wait_ge(p.sems["st"], st_x6[(0, 6 + ntq % 2)])
                                if want_ln2:
                                    sync.wait_ge(p.sems["ve"], ve_sq2[6 + ntq % 2])
                            ins = sync.dma_start(out=xio[:, ntq % 2, :],
                                                 in_=x_o[:, qcb * 8 + ntq, :])
                            ld_x[("p4", qcb, ntq)] = p.inc(ins, "ld", 16)

                    @blk.tensor
                    def _(t, qcb=qcb, q=q, Q=Q, ntq=ntq, gtok=gtok):
                        t.wait_ge(p.sems["ac"], ac_hid[(qcb, q, 7)])
                        t.wait_ge(p.sems["wd"], wd_w2[(qcb, q, 7)])
                        if ntq >= 2:
                            t.wait_ge(p.sems["ve"], ve_ev[(Q, ntq - 2)])
                        elif Q >= 1:
                            t.wait_ge(p.sems["ve"], ve_ev[(Q - 1, 6 + ntq)])
                        if q == 3 and ntq == 0:
                            t.wait_ge(p.sems["e2"], e2_p3)
                            t.wait_ge(p.sems["ve"], ve_vg)
                        PO = [P01, P02][ntq % 2]
                        last = None
                        for h2 in range(2):
                            for k in range(8):
                                last = t.matmul(
                                    PO[:, h2 * 512:(h2 + 1) * 512],
                                    hid[:, k, ntq * 128:(ntq + 1) * 128],
                                    w2q[:, q % 2, k, h2 * 512:(h2 + 1) * 512],
                                    start=(k == 0),
                                    stop=(k == 7 and q != 3))
                            if q == 3:
                                for jc in range(4):
                                    last = t.matmul(
                                        PO[:, h2 * 512:(h2 + 1) * 512],
                                        hzT[:, jc, gtok:gtok + 128],
                                        vg_sb[:, jc, h2 * 512:(h2 + 1) * 512],
                                        start=False, stop=(jc == 3))
                        pe_ntq[(Q, ntq)] = p.inc(last, "pe")

                    @blk.vector
                    def _(v, qcb=qcb, q=q, Q=Q, ntq=ntq):
                        v.wait_ge(p.sems["pe"], pe_ntq[(Q, ntq)])
                        PO = [P01, P02][ntq % 2]
                        if q == 0:
                            v.wait_ge(p.sems["ld"], ld_x[("p4", qcb, ntq)])
                            ins = v.tensor_tensor(out=acc[:, ntq, :], in0=PO[:, :],
                                                  in1=xio[:, ntq % 2, :], op=ALU.add)
                        elif q in (1, 2):
                            ins = v.tensor_tensor(out=acc[:, ntq, :], in0=PO[:, :],
                                                  in1=acc[:, ntq, :], op=ALU.add)
                        else:
                            if ntq >= 2:
                                v.wait_ge(p.sems["st"], st_x6[(qcb, ntq - 2)])
                            ins = v.tensor_tensor(out=xio[:, ntq % 2, :], in0=PO[:, :],
                                                  in1=acc[:, ntq, :], op=ALU.add)
                        ve_ev[(Q, ntq)] = p.inc(ins, "ve")
                        if q == 3 and want_ln2:
                            gnt = qcb * 8 + ntq
                            v.wait_ge(p.sems["ve"], ve_ev[(Q, ntq)])
                            p.inc(v.tensor_reduce(out=stats[:, gnt:gnt + 1],
                                                  in_=xio[:, ntq % 2, :],
                                                  axis=AX.X, op=ALU.add), "ve")
                            ins = v.tensor_tensor(out=scr[:, :], in0=xio[:, ntq % 2, :],
                                                  in1=xio[:, ntq % 2, :], op=ALU.mult)
                            v.wait_ge(p.sems["ve"], p.inc(ins, "ve"))
                            ins = v.tensor_reduce(out=stats[:, 16 + gnt:17 + gnt],
                                                  in_=scr[:, :], axis=AX.X, op=ALU.add)
                            ve_sq2[gnt] = p.inc(ins, "ve")

                    if q == 3:
                        @blk.sync
                        def _(sync, qcb=qcb, Q=Q, ntq=ntq):
                            sync.wait_ge(p.sems["ve"], ve_ev[(Q, ntq)])
                            ins = sync.dma_start(out=x_o[:, qcb * 8 + ntq, :],
                                                 in_=xio[:, ntq % 2, :])
                            st_x6[(qcb, ntq)] = p.inc(ins, "st", 16)

        if want_ln2:
            st_all6 = p.val("st")

            @blk.vector
            def _(v):
                v.wait_ge(p.sems["ve"], ve_sq2[NT - 1])
                ins = v.tensor_scalar(out=stats[:, 32:48], in0=stats[:, 0:16],
                                      scalar1=1.0 / 1024.0, scalar2=0.0,
                                      op0=ALU.mult, op1=ALU.add)
                v.wait_ge(p.sems["ve"], p.inc(ins, "ve"))
                ins = v.tensor_tensor(out=stats[:, 48:64], in0=stats[:, 32:48],
                                      in1=stats[:, 32:48], op=ALU.mult)
                v.wait_ge(p.sems["ve"], p.inc(ins, "ve"))
                ins = v.tensor_scalar(out=stats[:, 64:80], in0=stats[:, 16:32],
                                      scalar1=1.0 / 1024.0, scalar2=0.0,
                                      op0=ALU.mult, op1=ALU.add)
                v.wait_ge(p.sems["ve"], p.inc(ins, "ve"))
                ins = v.tensor_tensor(out=stats[:, 80:96], in0=stats[:, 64:80],
                                      in1=stats[:, 48:64], op=ALU.subtract)
                p.inc(ins, "ve")

            @blk.scalar
            def _(s):
                s.wait_ge(p.sems["ve"], p.val("ve"))
                ins = s.activation(out=stats[:, 96:112], in_=stats[:, 80:96],
                                   func=AF.Sqrt, bias=_f(1e-5), scale=_f(1.0))
                p.inc(ins, "ac")

            @blk.vector
            def _(v):
                v.wait_ge(p.sems["ac"], p.val("ac"))
                ins = v.reciprocal(out=stats[:, 48:64], in_=stats[:, 96:112])
                p.inc(ins, "ve")

            ve_fin2 = p.val("ve")
            e2_p5 = p.val("e2")
            ve_ap5 = {}
            e2_v = {}
            st_x2 = {}

            for nt in range(NT):
                @blk.sync
                def _(sync, nt=nt):
                    sync.wait_ge(p.sems["st"], st_all6)
                    if nt >= 2:
                        sync.wait_ge(p.sems["ve"], ve_ap5[nt - 2])
                    else:
                        sync.wait_ge(p.sems["ve"], ve_fin2)
                    ins = sync.dma_start(out=xio[:, nt % 2, :], in_=x_o[:, nt, :])
                    ld_x[("p5", nt)] = p.inc(ins, "ld", 16)

                @blk.vector
                def _(v, nt=nt):
                    v.wait_ge(p.sems["ld"], ld_x[("p5", nt)])
                    v.wait_ge(p.sems["ve"], ve_fin2)
                    if nt >= 1:
                        v.wait_ge(p.sems["e2"], e2_v[(nt - 1, 1)])
                    ins = v.tensor_scalar(out=stage[:, nt % 2, :],
                                          in0=xio[:, nt % 2, :],
                                          scalar1=stats[:, 32 + nt:33 + nt],
                                          scalar2=stats[:, 48 + nt:49 + nt],
                                          op0=ALU.subtract, op1=ALU.mult)
                    ve_ap5[nt] = p.inc(ins, "ve")

                for g in range(2):
                    u = 2 * nt + g
                    bank = PT[u % 2]

                    @blk.tensor
                    def _(t, nt=nt, g=g, u=u, bank=bank):
                        t.wait_ge(p.sems["ve"], ve_ap5[nt])
                        if u >= 2:
                            t.wait_ge(p.sems["e2"], e2_p5 + u - 1)
                        last = None
                        for j in range(4):
                            last = t.transpose(bank[:, j * 128:(j + 1) * 128],
                                               stage[:, nt % 2, (4 * g + j) * 128:
                                                     (4 * g + j + 1) * 128],
                                               id_sb[:, :])
                        p.inc(last, "pe")

                    pe_tr5 = p.val("pe")

                    @blk.vector
                    def _(v, nt=nt, g=g, bank=bank, pe_tr5=pe_tr5):
                        v.wait_ge(p.sems["pe"], pe_tr5)
                        if nt >= 1:
                            v.wait_ge(p.sems["st"], st_x2[(nt - 1, g)])
                        ins = v.tensor_copy(out=xnt2[:, nt % 2, 4 * g:4 * g + 4, :],
                                            in_=bank[:, :])
                        e2_v[(nt, g)] = p.inc(ins, "e2")

                    @blk.sync
                    def _(sync, nt=nt, g=g):
                        sync.wait_ge(p.sems["e2"], e2_v[(nt, g)])
                        ins = sync.dma_start(
                            out=xnT2_o[:, 4 * g:4 * g + 4, nt * 128:(nt + 1) * 128],
                            in_=xnt2[:, nt % 2, 4 * g:4 * g + 4, :])
                        st_x2[(nt, g)] = p.inc(ins, "st", 16)

        @blk.sync
        def _(sync):
            sync.wait_ge(p.sems["st"], p.val("st"))

    return p


def prep_PF(I, x_slice, ysl, wmat):
    W1 = np.asarray(I["L1_ffn_W1"], np.float32)
    W2 = np.asarray(I["L1_ffn_W2"], np.float32)
    rw = np.asarray(I["L1_ffn_router_W"], np.float32)
    rb = np.asarray(I["L1_ffn_router_b"], np.float32)
    U = np.asarray(I["L1_ffn_raw_U"], np.float32)
    V = np.asarray(I["L1_ffn_raw_V"], np.float32)
    xs = np.asarray(x_slice, np.float32)
    if xs.ndim == 2:
        xs = _tokmajor(xs)
    return dict(
        xs=xs,
        ysl=ysl,
        wp=_chunked(np.asarray(wmat, np.float32)),
        rwT=_chunked(np.ascontiguousarray(rw.T)),
        rb=np.ascontiguousarray(rb[None, :]),
        lng2=_colvec(I["L1_ffn_ln_g"]),
        lnb2=_colvec(I["L1_ffn_ln_b"]),
        UT=_chunked(np.ascontiguousarray(U.T)),
        VT=_chunked(np.ascontiguousarray(V.T)),
        Vr=np.ascontiguousarray(V.reshape(4, 128, D).transpose(1, 0, 2)),
        gam=np.ascontiguousarray(np.asarray(I["L1_ffn_gamma"], np.float32).reshape(1, D)),
        w1T=_chunked(np.ascontiguousarray(W1.T)),
        w2T=np.ascontiguousarray(W2.T.reshape(32, 128, D).transpose(1, 0, 2)),
        idm=_ident_bf(),
    )



def build_MLA():
    M = 4.0
    p = Prog()
    nc = p.nc
    xnT_i = p.dram_in("xnT", [128, NCH, SEQ], BF16)
    wdq_i = p.dram_in("wdq", [128, NCH, 256])
    wdk_i = p.dram_in("wdk", [128, NCH, 256])
    wdv_i = p.dram_in("wdv", [128, NCH, 256])
    wup_i = p.dram_in("wup", [128, 192])
    rt_i = p.dram_in("rt", [32, 32], BF16)
    selp_i = p.dram_in("selp", [32, 4, 128], BF16)
    cos_i = p.dram_in("cosT", [32, SEQ], BF16)
    sin_i = p.dram_in("sinT", [32, SEQ], BF16)
    msk_i = p.dram_in("msk", [128, 4, 1024], BF16)
    y_o = p.dram_out("y", [128, 2, SEQ], BF16)

    qsb = p.sb("qsb", [128, 2, SEQ], BF16)
    ksb = p.sb("ksb", [128, 2, SEQ], BF16)
    vsb = p.sb("vsb", [128, NKT, 256], BF16)
    ysb = p.sb("ysb", [128, 2, SEQ], BF16)
    xnc = p.sb("xnc", [128, 2, NCH, 512], BF16)
    wdq_sb = p.sb("wdqs", [128, NCH, 256], BF16)
    wdk_sb = p.sb("wdks", [128, NCH, 256], BF16)
    wdv_sb = p.sb("wdvs", [128, NCH, 256], BF16)
    wup_sb = p.sb("wups", [128, 192], BF16)
    rt_sb = p.sb("rts", [32, 32], BF16)
    selp_sb = p.sb("selps", [32, 4, 128], BF16)
    cosc = p.sb("cosc", [32, 2, 512], BF16)
    sinc = p.sb("sinc", [32, 2, 512], BF16)
    lqf = p.sb("lqf", [128, 512], BF16)
    lkf = p.sb("lkf", [128, 512], BF16)
    lvf = p.sb("lvf", [128, 512], BF16)
    cf = p.sb("cf", [32, 512], BF16)
    ef = p.sb("ef", [32, 512], BF16)
    erf = p.sb("erf", [32, 512], BF16)
    ropef = p.sb("ropef", [32, 512], BF16)
    msk_sb = p.sb("msks", [128, 4, 1024], BF16)
    attT = p.sb("attT", [128, 2, 2048], BF16)
    avf = p.sb("avf", [128, 512])
    rcp = p.sb("rcp", [1, 2, 512])
    ones_c = p.sb("ones_c", [128, 1], BF16)
    ones64 = p.sb("ones64", [64, 64])

    SCall = p.ps("SCall", [128, 2048])
    LA0 = _ColView(SCall, 0)
    LA1 = _ColView(SCall, 512)
    TC = _ColView(SCall, 1024)
    TE = _ColView(SCall, 1536)
    TR = p.ps("S4", [128, 512])
    AS = p.ps("S5", [128, 512])
    PV0 = p.ps("S6", [128, 512])
    PV1 = p.ps("S7", [128, 512])

    with nc.Block() as blk:
        @blk.gpsimd
        def _(g):
            for dst, srci in [(wdq_sb, wdq_i), (wdk_sb, wdk_i), (wdv_sb, wdv_i)]:
                g.dma_start(out=dst[:, :, :], in_=srci[:, :, :]).then_inc(p.sems["wd"], 16)
                p.cnt["wd"] += 16
            g.dma_start(out=wup_sb[:, :], in_=wup_i[:, :]).then_inc(p.sems["wd"], 16)
            p.cnt["wd"] += 16
            g.memset(ones_c[:, :], 1.0)
            ins = g.memset(ones64[:, :], 1.0)
            p.inc(ins, "wd", 1)

        wd0 = p.val("wd")

        @blk.sync
        def _(sync):
            sync.dma_start(out=msk_sb[:, :, :], in_=msk_i[:, :, :]).then_inc(p.sems["ld"], 16)
            sync.dma_start(out=rt_sb[:, :], in_=rt_i[:, :]).then_inc(p.sems["ld"], 16)
            ins = sync.dma_start(out=selp_sb[:, :, :], in_=selp_i[:, :, :])
            p.inc(ins, "ld", 16)
            p.cnt["ld"] += 32

        ld0 = p.val("ld")

        ld_tc = {}
        pe_lat_last = {}
        ve_rope_last = {}
        ve_done = {}

        for tc in range(16):
            @blk.sync
            def _(sync, tc=tc):
                if tc >= 2:
                    sync.wait_ge(p.sems["pe"], pe_lat_last[(tc - 2, 1)])
                    sync.wait_ge(p.sems["ve"], ve_rope_last[(tc - 2, 1)])
                sync.dma_start(out=xnc[:, tc % 2, :, :],
                               in_=xnT_i[:, :, tc * 512:(tc + 1) * 512]
                               ).then_inc(p.sems["ld"], 16)
                sync.dma_start(out=cosc[:, tc % 2, :],
                               in_=cos_i[:, tc * 512:(tc + 1) * 512]
                               ).then_inc(p.sems["ld"], 16)
                ins = sync.dma_start(out=sinc[:, tc % 2, :],
                                     in_=sin_i[:, tc * 512:(tc + 1) * 512])
                p.inc(ins, "ld", 16)
                p.cnt["ld"] += 32
                ld_tc[tc] = p.val("ld")

            for pr in range(2):
                ve_lat = {}
                for li, (wsb, dst) in enumerate([(wdq_sb, lqf), (wdk_sb, lkf),
                                                 (wdv_sb, lvf)]):
                    @blk.tensor
                    def _(t, tc=tc, pr=pr, li=li, wsb=wsb):
                        t.wait_ge(p.sems["ld"], ld_tc[tc])
                        t.wait_ge(p.sems["wd"], wd0)
                        if li >= 1:
                            t.wait_ge(p.sems["ve"], ve_lat[li - 1])
                        elif (tc, pr) != (0, 0):
                            prev = (tc, pr - 1) if pr == 1 else (tc - 1, 1)
                            t.wait_ge(p.sems["ve"], ve_done[prev])
                        last = None
                        for pos in range(2):
                            hl = 2 * pr + pos
                            PA = [LA0, LA1][pos]
                            for ci in range(NCH):
                                last = t.matmul(PA[pos * 64:pos * 64 + 64, :],
                                                wsb[:, ci, hl * 64:(hl + 1) * 64],
                                                xnc[:, tc % 2, ci, :],
                                                start=(ci == 0), stop=(ci == NCH - 1),
                                                tile_position=(0, pos * 64))
                        p.inc(last, "pe")
                        if li == 2:
                            pe_lat_last[(tc, pr)] = p.val("pe")

                    pe_lat = p.val("pe")

                    @blk.vector
                    def _(v, li=li, dst=dst, pe_lat=pe_lat):
                        v.wait_ge(p.sems["pe"], pe_lat)
                        ins = v.tensor_copy(out=dst[0:64, :], in_=LA0[0:64, :])
                        v.wait_ge(p.sems["ve"], p.inc(ins, "ve"))
                        ins = v.tensor_copy(out=dst[64:128, :], in_=LA1[64:128, :])
                        ve_lat[li] = p.inc(ins, "ve")

                for proj, (lf, co, ce, dst) in enumerate(
                        [(lqf, 0, 32, qsb), (lkf, 64, 96, ksb)]):
                    for e in range(2):
                        @blk.tensor
                        def _(t, tc=tc, pr=pr, proj=proj, e=e, lf=lf, co=co, ce=ce):
                            t.wait_ge(p.sems["ve"], ve_lat[2] if proj == 0 and e == 0
                                      else p.val("ve"))
                            b = e * 64
                            tp = (b, 0)
                            t.matmul(TC[0:32, :], wup_sb[b:b + 64, co:co + 32],
                                     lf[b:b + 64, :], start=True, stop=True,
                                     tile_position=tp)
                            ins = t.matmul(TE[0:32, :], wup_sb[b:b + 64, ce:ce + 32],
                                           lf[b:b + 64, :], start=True, stop=True,
                                           tile_position=tp)
                            p.inc(ins, "pe")

                        pe_ce = p.val("pe")

                        @blk.vector
                        def _(v, pe_ce=pe_ce):
                            v.wait_ge(p.sems["pe"], pe_ce)
                            p.inc(v.tensor_copy(out=cf[:, :], in_=TC[0:32, :]), "ve")
                            ins = v.tensor_copy(out=ef[:, :], in_=TE[0:32, :])
                            v.wait_ge(p.sems["ve"], p.inc(ins, "ve"))

                        ve_ce = p.val("ve")

                        @blk.tensor
                        def _(t, ve_ce=ve_ce):
                            t.wait_ge(p.sems["ld"], ld0)
                            t.wait_ge(p.sems["ve"], ve_ce)
                            ins = t.matmul(TR[0:32, :], rt_sb[0:32, :], ef[0:32, :],
                                           start=True, stop=True, tile_position=(0, 0))
                            p.inc(ins, "pe")

                        pe_rot = p.val("pe")

                        @blk.vector
                        def _(v, tc=tc, pe_rot=pe_rot):
                            v.wait_ge(p.sems["pe"], pe_rot)
                            ins = v.tensor_copy(out=erf[:, :], in_=TR[0:32, :])
                            v.wait_ge(p.sems["ve"], p.inc(ins, "ve"))
                            ins = v.tensor_tensor(out=ef[:, :], in0=ef[:, :],
                                                  in1=cosc[:, tc % 2, :], op=ALU.mult)
                            v.wait_ge(p.sems["ve"], p.inc(ins, "ve"))
                            ins = v.tensor_tensor(out=erf[:, :], in0=erf[:, :],
                                                  in1=sinc[:, tc % 2, :], op=ALU.mult)
                            v.wait_ge(p.sems["ve"], p.inc(ins, "ve"))
                            ins = v.tensor_tensor(out=ropef[:, :], in0=ef[:, :],
                                                  in1=erf[:, :], op=ALU.add)
                            v.wait_ge(p.sems["ve"], p.inc(ins, "ve"))

                        ve_rope = p.val("ve")
                        if proj == 1 and e == 1:
                            ve_rope_last[(tc, pr)] = ve_rope

                        @blk.tensor
                        def _(t, e=e, proj=proj, ve_rope=ve_rope):
                            t.wait_ge(p.sems["ve"], ve_rope)
                            t.matmul(AS[:, :], selp_sb[0:32, 2 * e, :], cf[0:32, :],
                                     start=(e == 0), stop=False, tile_position=(0, 0))
                            ins = t.matmul(AS[:, :], selp_sb[0:32, 2 * e + 1, :],
                                           ropef[0:32, :],
                                           start=False, stop=(e == 1),
                                           tile_position=(0, 0))
                            p.inc(ins, "pe")

                    pe_as = p.val("pe")

                    @blk.vector
                    def _(v, tc=tc, pr=pr, dst=dst, pe_as=pe_as):
                        v.wait_ge(p.sems["pe"], pe_as)
                        ins = v.tensor_copy(out=dst[:, pr, tc * 512:(tc + 1) * 512],
                                            in_=AS[:, :])
                        v.wait_ge(p.sems["ve"], p.inc(ins, "ve"))

                ve_v = {}
                for kt in range(4):
                    @blk.tensor
                    def _(t, tc=tc, pr=pr, kt=kt):
                        if kt == 0:
                            t.wait_ge(p.sems["ve"], ve_lat[2])
                        else:
                            t.wait_ge(p.sems["ve"], ve_v[kt - 1])
                        t.matmul(PV0[:, 0:64], lvf[0:64, kt * 128:(kt + 1) * 128],
                                 wup_sb[0:64, 128:192], start=True, stop=True,
                                 tile_position=(0, 0))
                        ins = t.matmul(PV1[:, 0:64], lvf[64:128, kt * 128:(kt + 1) * 128],
                                       wup_sb[64:128, 128:192], start=True, stop=True,
                                       tile_position=(64, 0))
                        p.inc(ins, "pe")

                    pe_vu = p.val("pe")

                    @blk.vector
                    def _(v, tc=tc, pr=pr, kt=kt, pe_vu=pe_vu):
                        v.wait_ge(p.sems["pe"], pe_vu)
                        gkt = tc * 4 + kt
                        p.inc(v.tensor_copy(out=vsb[:, gkt, 2 * pr * 64:2 * pr * 64 + 64],
                                            in_=PV0[:, 0:64]), "ve")
                        ins = v.tensor_copy(out=vsb[:, gkt, (2 * pr + 1) * 64:
                                                (2 * pr + 2) * 64],
                                            in_=PV1[:, 0:64])
                        ve_v[kt] = p.inc(ins, "ve")

                ve_done[(tc, pr)] = p.val("ve")

        phaseA_done = p.val("ve")

        emit_phaseB(p, blk, True, M, qsb, ksb, vsb, msk_sb, attT, avf, rcp,
                    ysb, ones_c, ones64, SCall, TR, AS, PV0, PV1,
                    "ve", phaseA_done, ld0)

        yB_done = p.val("e2")

        @blk.sync
        def _(sync):
            sync.wait_ge(p.sems["e2"], yB_done)
            ins = sync.dma_start(out=y_o[:, :, :], in_=ysb[:, :, :])
            p.inc(ins, "st", 16)
            sync.wait_ge(p.sems["st"], p.val("st"))
    return p


def prep_MLA(I, core, xnT_full):
    m = core % 4
    BFc = BF
    wdq = np.asarray(I["L0_attn_b_Wq_down"], np.float32)
    wdk = np.asarray(I["L0_attn_b_Wk_down"], np.float32)
    wdv = np.asarray(I["L0_attn_b_Wv_down"], np.float32)
    cols = slice(m * 256, (m + 1) * 256)
    wup = np.zeros((128, 192), np.float32)
    for r in (0, 64):
        wup[r:r + 64, 0:32] = np.asarray(I["L0_attn_b_Wq_up_c"], np.float32)
        wup[r:r + 64, 32:64] = np.asarray(I["L0_attn_b_Wq_up_e"], np.float32)
        wup[r:r + 64, 64:96] = np.asarray(I["L0_attn_b_Wk_up_c"], np.float32)
        wup[r:r + 64, 96:128] = np.asarray(I["L0_attn_b_Wk_up_e"], np.float32)
        wup[r:r + 64, 128:192] = np.asarray(I["L0_attn_b_Wv_up"], np.float32)
    R = np.zeros((32, 32), np.float32)
    for j in range(0, 32, 2):
        R[j, j + 1] = -1.0
        R[j + 1, j] = 1.0
    selp = np.zeros((32, 4, 128), np.float32)
    for s in range(4):
        for k in range(32):
            selp[k, s, 32 * s + k] = 1.0
    inv_freq = 1.0 / (10000.0 ** (np.arange(0, 32, 2, dtype=np.float32) / np.float32(32.0)))
    t_idx = np.arange(SEQ, dtype=np.float32)
    ang = t_idx[None, :] * inv_freq[:, None]
    ang = np.concatenate([ang, ang], 0)
    return dict(
        xnT=xnT_full,
        wdq=_chunked(np.ascontiguousarray(wdq[:, cols])),
        wdk=_chunked(np.ascontiguousarray(wdk[:, cols])),
        wdv=_chunked(np.ascontiguousarray(wdv[:, cols])),
        wup=wup,
        rt=np.ascontiguousarray(R.T).astype(BFc),
        selp=selp.astype(BFc),
        cosT=np.cos(ang).astype(BFc),
        sinT=np.sin(ang).astype(BFc),
        msk=_masks_bf(),
    )


_BUILD_CACHE = {}


def _get(name, builder):
    if name not in _BUILD_CACHE:
        _BUILD_CACHE[name] = builder()
    return _BUILD_CACHE[name]


def _bf(a):
    a = np.asarray(a)
    if a.dtype == BF:
        return a
    return a.view(np.uint16).view(BF)


def kernel(**inputs):
    I = inputs

    def run(p, in_maps):
        return run_bass_kernel_spmd(p.nc, in_maps, core_ids=list(range(8)))

    def parts_for(ps):
        out = []
        for b in range(2):
            pb = ps[4 * b:4 * b + 4]
            out.append(np.ascontiguousarray(
                pb.reshape(4, NCH, 128).transpose(2, 0, 1).reshape(128, 32)))
        return out

    def xnT_full(xns):
        return [np.ascontiguousarray(np.concatenate(
            [xns[4 * b + j] for j in range(4)], axis=2)) for b in range(2)]

    def ysls(r):
        yf = [_yfeat([_bf(r.results[4 * b + j]["y"]) for j in range(4)])
              for b in range(2)]
        return [_ysl_from_yfeat(yf[c // 4], c % 4) for c in range(8)]

    p1 = _get("L1", build_L1)
    r1 = run(p1, [prep_L1(I, c) for c in range(8)])
    pst = np.stack([np.asarray(r1.results[c]["pstates"], np.float32)
                    for c in range(8)])

    p2 = _get("L2", build_L2)
    maps = []
    for c in range(8):
        b = c // 4
        pstb = np.ascontiguousarray(pst[4 * b:4 * b + 4].transpose(1, 0, 2, 3))
        maps.append(prep_L2(I, c, np.ascontiguousarray(_bf(r1.results[c]["qn"])),
                            pstb))
    r2 = run(p2, maps)
    x1 = [np.asarray(r2.results[c]["x1"], np.float32) for c in range(8)]
    ps1 = np.stack([np.asarray(r2.results[c]["psums"], np.float32)[0]
                    for c in range(8)])

    p3 = _get("FFN0", build_FFN0)
    pw = parts_for(ps1)
    r3 = run(p3, [prep_FFN0(I, x1[c], pw[c // 4]) for c in range(8)])
    x2 = [np.asarray(r3.results[c]["x_out"], np.float32) for c in range(8)]
    xnTs = [_bf(r3.results[c]["xnT_out"]) for c in range(8)]

    p4 = _get("MLA", build_MLA)
    xf = xnT_full(xnTs)
    r4 = run(p4, [prep_MLA(I, c, xf[c // 4]) for c in range(8)])

    p5 = _get("PROJ", build_PROJ)
    ys = ysls(r4)
    wc_ch = _chunked(np.asarray(I["L0_attn_b_Wc"], np.float32))
    r5 = run(p5, [dict(xs=x2[c], ysl=ys[c], wp=wc_ch) for c in range(8)])
    x3 = [np.asarray(r5.results[c]["x_out"], np.float32) for c in range(8)]
    ps3 = np.stack([np.asarray(r5.results[c]["psums"], np.float32)[0]
                    for c in range(8)])

    pw = parts_for(ps3)
    r6 = run(p3, [prep_FFN0(I, x3[c], pw[c // 4]) for c in range(8)])
    x4 = [np.asarray(r6.results[c]["x_out"], np.float32) for c in range(8)]
    xnTs = [_bf(r6.results[c]["xnT_out"]) for c in range(8)]

    p7 = _get("ATTnc", lambda: build_ATT(False, 12.0))
    xf = xnT_full(xnTs)
    r7 = run(p7, [prep_ATT_L1aa(I, c, xf[c // 4]) for c in range(8)])

    p8 = _get("PF1", lambda: build_PF(True))
    ys = ysls(r7)
    wo = np.asarray(I["L1_attn_a_w_out"], np.float32)
    r8 = run(p8, [prep_PF(I, x4[c], ys[c], wo) for c in range(8)])
    x6 = [np.asarray(r8.results[c]["x_out"], np.float32) for c in range(8)]
    xnTs = [_bf(r8.results[c]["xnT2_out"]) for c in range(8)]

    p9 = _get("ATTc", lambda: build_ATT(True, 16.0))
    xf = xnT_full(xnTs)
    r9 = run(p9, [prep_ATT_L1ab(I, c, xf[c // 4]) for c in range(8)])

    p10 = _get("PF0", lambda: build_PF(False))
    ys = ysls(r9)
    wpt = np.ascontiguousarray(np.asarray(I["L1_attn_b_w_proj"], np.float32).T)
    r10 = run(p10, [prep_PF(I, x6[c], ys[c], wpt) for c in range(8)])

    out = np.empty((2, 8192, 1024), np.float32)
    for c in range(8):
        b, m = c // 4, c % 4
        out[b, m * TOK:(m + 1) * TOK, :] = _untokmajor(
            np.asarray(r10.results[c]["x_out"], np.float32))
    return out
